# version 02
# baseline (speedup 1.0000x reference)
import numpy as np

import concourse.bass as bass
import concourse.mybir as mybir
from concourse.bass_utils import run_bass_kernel_spmd

F32 = mybir.dt.float32
BF16 = mybir.dt.bfloat16
U8 = mybir.dt.uint8
I32 = mybir.dt.int32
AF = mybir.ActivationFunctionType
OP = mybir.AluOpType

D = 1024
INNER = 1536
S = 16
KCONV = 3
DCD = 4096
NEXP = 4096
NH = 4
SEQ = 8192
BATCH = 4
NCORES = 8

W_WARM = 512
T_CORE = 4096
R_SC = 32
NDC = 8
NIC = 12
NPROJ = 24
NQC = 4
NEC = 32
NKC = 32
ST_SIZES = [W_WARM, 1024, 1024, 1024, 1024]
LN_EPS = 1e-5
NEG_BIG = -1.0e30



def _stage_core_inputs(inp, core):
    b, half = core // 2, core % 2
    x = np.ascontiguousarray(np.asarray(inp["L0_attn_x"], np.float32)[b])
    t0 = half * T_CORE
    if half == 0:
        xw = np.zeros((W_WARM, D), np.float32)
    else:
        xw = x[t0 - W_WARM:t0]
    d = {}
    d["x_in"] = np.ascontiguousarray(np.concatenate([xw, x[t0:t0 + T_CORE]], 0))
    d["warm_mask"] = np.broadcast_to(
        np.float32(0.0 if half == 0 else 1.0), (128, W_WARM)).copy()
    d["lnw_rep"] = np.broadcast_to(
        np.asarray(inp["L0_attn_ln_w"], np.float32)[None, :], (128, D)).copy()
    d["lnb_rep"] = np.broadcast_to(
        np.asarray(inp["L0_attn_ln_b"], np.float32)[None, :], (128, D)).copy()
    W_in = np.asarray(inp["L0_attn_W_in"], np.float32)
    d["W_in_r"] = np.ascontiguousarray(
        W_in.reshape(NDC, 128, NPROJ, 128).transpose(2, 1, 0, 3))
    d["conv_sb"] = np.ascontiguousarray(
        np.asarray(inp["L0_attn_conv_w"], np.float32).reshape(NIC, 128, KCONV)
        .transpose(1, 0, 2))
    d["AT"] = np.ascontiguousarray(np.asarray(inp["L0_attn_A"], np.float32).T)
    d["BmT_r"] = np.ascontiguousarray(
        np.asarray(inp["L0_attn_Bm"], np.float32).T.reshape(NIC, 128, S))
    d["CmT"] = np.ascontiguousarray(np.asarray(inp["L0_attn_Cm"], np.float32).T)
    W_out = np.asarray(inp["L0_attn_W_out"], np.float32)
    d["W_out_r"] = np.ascontiguousarray(
        W_out.reshape(NIC, 128, NDC, 128).transpose(2, 0, 1, 3))
    Wq = np.asarray(inp["L0_ffn_Wq"], np.float32)
    perm = np.array([p * 256 + h * 64 + n
                     for h in range(NH) for p in range(2) for n in range(64)])
    d["Wq_r"] = np.ascontiguousarray(
        Wq[:, perm].reshape(4, 2, 128, 512).transpose(0, 2, 1, 3)
        .reshape(4, 128, 1024))
    keys = np.asarray(inp["L0_ffn_keys"], np.float32)
    kbd = np.zeros((NH, 128, 128), np.float32)
    for h in range(NH):
        for p in range(2):
            kbd[h, p * 64:(p + 1) * 64, p * 64:(p + 1) * 64] = keys[h, :, p, :].T
    d["keysbd"] = kbd
    Wg = np.asarray(inp["L0_ffn_Wg"], np.float32)
    Wu = np.asarray(inp["L0_ffn_Wu"], np.float32)
    d["Wg_r"] = np.ascontiguousarray(
        Wg.reshape(NDC, 128, NKC, 128).transpose(2, 1, 0, 3))
    d["Wu_r"] = np.ascontiguousarray(
        Wu.reshape(NDC, 128, NKC, 128).transpose(2, 1, 0, 3))
    d["bg_sb"] = np.ascontiguousarray(
        np.asarray(inp["L0_ffn_bg"], np.float32).reshape(NKC, 128).T)
    d["bu_sb"] = np.ascontiguousarray(
        np.asarray(inp["L0_ffn_bu"], np.float32).reshape(NKC, 128).T)
    d["bd_sb"] = np.ascontiguousarray(
        np.asarray(inp["L0_ffn_bd"], np.float32).reshape(NDC, 128).T)
    d["Wd_r"] = np.ascontiguousarray(
        np.asarray(inp["L0_ffn_Wd"], np.float32).reshape(NKC, 128, NDC, 128))
    down = np.asarray(inp["L0_ffn_down_embed"], np.float32)
    d["DOWNT_r"] = np.ascontiguousarray(
        down.T.reshape(NDC, 128, 4, 1024).transpose(2, 0, 1, 3))
    d["UE_r"] = np.ascontiguousarray(
        np.asarray(inp["L0_ffn_up_embed"], np.float32).reshape(NEC, 128, NDC, 128))
    return d



class Chan:
    __slots__ = ("sem", "total", "name")

    def __init__(self, sem, name):
        self.sem = sem
        self.total = 0
        self.name = name

    def tok(self):
        return (self, self.total)


class Sched:
    ENGINES = ("tensor", "vector", "scalar", "gpsimd", "sync")

    def __init__(self):
        self.steps = {e: [] for e in self.ENGINES}

    def st(self, eng, fn, waits=(), inc=None, inc_amt=1):
        ws = [(c, v) for (c, v) in waits if v > 0]
        self.steps[eng].append((fn, ws, inc, inc_amt))
        if inc is not None:
            inc.total += inc_amt
            return inc.tok()
        return None

    def replay(self, eng_name, e):
        seen = {}
        for fn, waits, inc, inc_amt in self.steps[eng_name]:
            for chan, val in waits:
                if seen.get(chan.name, -1) >= val:
                    continue
                e.wait_ge(chan.sem, val)
                seen[chan.name] = val
            inst = fn(e) if fn is not None else None
            if inc is not None:
                if inst is not None:
                    inst.then_inc(inc.sem, inc_amt)
                else:
                    e.sem_inc(inc.sem, inc_amt)


class Ring:

    def __init__(self, sched, slots_aps, slot_chans, cons_chan):
        self.sched = sched
        self.slots = slots_aps
        self.nslots = len(slots_aps)
        self.slot_chans = slot_chans
        self.cons_chan = cons_chan
        self.load_toks = []
        self.cons_toks = []

    def load(self, make_dma, extra_waits=()):
        p = len(self.load_toks)
        slot = self.slots[p % self.nslots]
        waits = list(extra_waits)
        if p >= self.nslots:
            waits.append(self.cons_toks[p - self.nslots])
        fn = make_dma(slot)
        tok = self.sched.st("gpsimd", fn, waits=waits,
                            inc=self.slot_chans[p % self.nslots], inc_amt=16)
        self.load_toks.append(tok)
        return p

    def slot(self, p):
        return self.slots[p % self.nslots]

    def wait_tok(self, p):
        return self.load_toks[p]

    def consumed(self, p, tok):
        assert p == len(self.cons_toks), (p, len(self.cons_toks))
        self.cons_toks.append(tok)



def build_nc(dbg_names=(), n_st=None, with_ffn=True, stop_phase=None):
    import contextlib
    nc = bass.Bass("TRN2", target_bir_lowering=False)
    st_sizes = ST_SIZES if n_st is None else ST_SIZES[:n_st]
    dbg_names = set(dbg_names)

    def din(name, shape, dt=F32):
        return nc.dram_tensor(name, list(shape), dt, kind="ExternalInput").ap()

    x_in = din("x_in", (W_WARM + T_CORE, D))
    warm_mask = din("warm_mask", (128, W_WARM))
    lnw_rep = din("lnw_rep", (128, D))
    lnb_rep = din("lnb_rep", (128, D))
    W_in_r = din("W_in_r", (NPROJ, 128, NDC, 128))
    conv_sb_d = din("conv_sb", (128, NIC, KCONV))
    AT_d = din("AT", (S, S))
    BmT_d = din("BmT_r", (NIC, 128, S))
    CmT_d = din("CmT", (S, INNER))
    W_out_r = din("W_out_r", (NDC, NIC, 128, 128))
    Wq_r = din("Wq_r", (4, 128, 1024))
    keysbd_d = din("keysbd", (NH, 128, 128))
    Wg_r = din("Wg_r", (NKC, 128, NDC, 128))
    Wu_r = din("Wu_r", (NKC, 128, NDC, 128))
    bg_d = din("bg_sb", (128, NKC))
    bu_d = din("bu_sb", (128, NKC))
    bd_d = din("bd_sb", (128, NDC))
    Wd_r = din("Wd_r", (NKC, 128, NDC, 128))
    DOWNT_r = din("DOWNT_r", (4, NDC, 128, 1024))
    UE_r = din("UE_r", (NEC, 128, NDC, 128))

    y_out = nc.dram_tensor("y_out", [T_CORE, D], F32, kind="ExternalOutput").ap()
    dbg_written = set()

    def sb(name, shape, dt):
        return nc.alloc_sbuf_tensor(name, list(shape), dt).ap()

    xres = sb("xres", (128, 8, D), F32)
    AR1 = sb("AR1", (128, 23040), U8)
    AR2 = sb("AR2", (128, 40960), U8)
    AR3 = sb("AR3", (128, 25088), U8)
    VT = sb("VT", (128, NEC, 512), BF16)
    wring = sb("wring", (128, 12288), U8)
    trans_t = sb("trans_t", (128, 2, 1024), BF16)
    Emat = sb("Emat", (128, NEXP), BF16)
    conv_tmp = sb("conv_tmp", (128, 1024), F32)
    sig_tmp = sb("sig_tmp", (128, 1024), BF16)
    dump16 = sb("dump16", (128, 1024), BF16)
    BuE = sb("BuE", (128, 1056), BF16)
    Bup = conv_tmp
    states = sb("states", (128, 1056), F32)
    states_bf = conv_tmp.bitcast(BF16)[:, 0:1056]
    halo_s = sb("halo_s", (128, NIC, 2), BF16)

    ident_bf = sb("ident_bf", (128, 128), BF16)
    ident_f32 = sb("ident_f32", (128, 128), F32)
    keysbd_s = sb("keysbd_s", (128, NH, 128), BF16)
    lnw_s = sb("lnw_s", (128, D), BF16)
    lnb_s = sb("lnb_s", (128, D), BF16)
    conv_s = sb("conv_s", (128, NIC, KCONV), F32)
    BmT_s = sb("BmT_s", (128, NIC, S), BF16)
    CmT_s = sb("CmT_s", (128, INNER), BF16)
    Pall = sb("Pall", (128, (R_SC + 1) * S), F32)
    Pall_bf = sb("Pall_bf", (128, (R_SC + 1) * S), BF16)
    Mtmp = sb("Mtmp", (128, 3 * S), F32)
    bgbu_s = sb("bgbu_s", (128, 2 * NKC), F32)
    bd_s = sb("bd_s", (128, NDC), F32)
    ones_s = sb("ones_s", (128, 64), F32)
    selh_s = sb("selh_s", (128, 512), F32)
    statb = sb("statb", (128, 256), F32)

    def carve(arena, byte_off, shape, dt):
        nbytes = int(np.prod(shape[1:])) * mybir.dt.size(dt)
        assert byte_off % 32 == 0
        v = arena[:, byte_off:byte_off + nbytes].bitcast(dt)
        if len(shape) == 2:
            return v
        if len(shape) == 3:
            return v.rearrange("p (a b) -> p a b", a=shape[1])
        if len(shape) == 4:
            return v.rearrange("p (a b c) -> p a b c", a=shape[1], b=shape[2])
        if len(shape) == 5:
            return v.rearrange("p (a b c d) -> p a b c d", a=shape[1],
                               b=shape[2], c=shape[3])
        raise ValueError(shape)

    h_fm = carve(AR1, 0, (128, NDC, 1024), BF16)
    mask8 = carve(AR1, 0, (128, 512), F32)
    silu_buf = carve(AR1, 0, (128, 4, 1024), BF16)
    simkm_bf = carve(AR1, 8192, (128, 2, NH, 512), BF16)
    simtm_t = carve(AR1, 16384, (128, 512), F32)
    thrRows = simtm_t
    stage = carve(AR1, 18432, (128, 64), F32)
    thrTM = carve(AR1, 18688, (128, 128), F32)
    nvz = carve(AR1, 19200, (128, 2, 4, 4, 4), F32)
    rall = carve(AR1, 19712, (128, 64), F32)
    sxbf = carve(AR1, 19968, (128, 64), BF16)
    candg4 = carve(AR1, 20096, (128, NH, 64), F32)
    exgrid4 = carve(AR1, 21120, (128, NH, 64), F32)
    v8tmp = carve(AR1, 22144, (128, 32), F32)
    lnzb = carve(AR1, 22272, (128, 32), F32)
    thrKM = carve(AR1, 2048, (128, 512), F32)
    stt_out = carve(AR1, 4096, (128, 2048), BF16)

    xattn = carve(AR2, 0, (128, 8, D), BF16)
    q_fm = carve(AR2, 0, (128, NQC, 1024), BF16)

    sgate = carve(AR2, 16384, (128, NIC, 1024), BF16)
    wmask_s = carve(AR2, 16384, (128, W_WARM), F32)
    acc_j = [carve(AR2, j * 8192, (128, NEXP), BF16) for j in range(4)]
    m_fm = carve(AR2, 0, (128, NKC, 512), BF16)
    tmp_stt = carve(AR2, 32768, (128, 2048), BF16)
    tmp2_stt = carve(AR2, 36864, (128, 2048), BF16)
    ffnout = carve(AR2, 32768, (128, NDC, 512), BF16)

    data_u = carve(AR3, 0, (128, NIC, 1040), BF16)
    blockout = carve(AR3, 0, (128, NDC, 1024), BF16)
    h1_fm = carve(AR3, 0, (128, NDC, 1024), BF16)
    out_stage = carve(AR3, 16384, (128, 2, D), F32)

    rem = nc.sbuf_bytes_remaining
    assert rem >= 0, f"SBUF over budget by {-rem} bytes"

    PS = nc.alloc_psum_tensor("PS", [128, 4096], F32).ap()
    PSb = PS.bitcast(BF16)

    def tps(q):
        return PSb[:, 6 * 1024 + q * 128:6 * 1024 + (q + 1) * 128]

    def tps_grp():
        return PSb[:, 6 * 1024:6 * 1024 + 512]

    def warm_pe(waits):
        sch.st("tensor", lambda e: e.matmul(
            PS[:, 5 * 512:5 * 512 + 128], ident_bf, ident_bf,
            start=True, stop=True), waits=waits)

    _stack = contextlib.ExitStack()
    chans = {}
    for name in ("c_x", "c_w", "c_te", "c_ve", "c_se", "c_ge", "c_out",
                 "c_wc", "c_dbg"):
        sem = _stack.enter_context(nc.semaphore(name))
        chans[name] = Chan(sem, name)
    c_x, c_w, c_te, c_ve, c_se, c_ge, c_out, c_wc, c_dbg = (
        chans[k] for k in
        ("c_x", "c_w", "c_te", "c_ve", "c_se", "c_ge", "c_out", "c_wc",
         "c_dbg"))

    sch = Sched()
    wslots = [carve(wring, i * 2048, (128, 1024), BF16) for i in range(6)]
    slot_chans = []
    for i in range(6):
        sem = _stack.enter_context(nc.semaphore(f"c_ws{i}"))
        slot_chans.append(Chan(sem, f"c_ws{i}"))
    ring = Ring(sch, wslots, slot_chans, c_wc)

    dbg = {}

    def dbg_store(name, src_ap, waits=()):
        if name not in dbg_names or name in dbg_written:
            return
        dbg_written.add(name)
        shp = list(src_ap.shape)
        t = nc.dram_tensor("dbg_" + name, shp, src_ap.dtype,
                           kind="ExternalOutput").ap()
        dbg[name] = True
        sch.st("sync", lambda e, t=t, s=src_ap: e.dma_start(out=t, in_=s),
               waits=waits, inc=c_dbg, inc_amt=16)

    def ld(dst, src, slow=False):
        def f(e, d=dst, s=src, slow=slow):
            if slow:
                return e.dma_start(out=d, in_=s, allow_slow_non_contiguous=True)
            return e.dma_start(out=d, in_=s)
        sch.st("sync", f, inc=c_x, inc_amt=16)

    def ldc(dst, src, slow=False):
        def f(e, d=dst, s=src, slow=slow):
            if slow:
                return e.dma_start(out=d, in_=s, allow_slow_non_contiguous=True)
            return e.dma_start(out=d, in_=s)
        sch.st("gpsimd", f, inc=c_w, inc_amt=16)

    ld(conv_s, conv_sb_d)
    ld(wmask_s, warm_mask)
    ld(bgbu_s[:, 0:NKC], bg_d)
    ld(bgbu_s[:, NKC:2 * NKC], bu_d)
    ld(bd_s, bd_d)
    ld(Mtmp[0:S, 0:S], AT_d)
    t_loads = c_x.tok()

    ldc(lnw_s, lnw_rep)
    ldc(lnb_s, lnb_rep)
    ldc(keysbd_s, keysbd_d.rearrange("a p b -> p a b"), slow=True)
    ldc(BmT_s, BmT_d.rearrange("a p b -> p a b"), slow=True)
    ldc(CmT_s[0:S, :], CmT_d)
    t_cast_pre = c_w.tok()

    def g_ident(e):
        e.memset(ident_bf, 0.0)
        e.affine_select(out=ident_bf, in_=ident_bf, compare_op=OP.not_equal,
                        fill=1.0, base=0, pattern=[[-1, 128]],
                        channel_multiplier=1)
        e.memset(ident_f32, 0.0)
        e.affine_select(out=ident_f32, in_=ident_f32, compare_op=OP.not_equal,
                        fill=1.0, base=0, pattern=[[-1, 128]],
                        channel_multiplier=1)
        e.memset(Emat, 0.0)
        ev = Emat.rearrange("p (a b) -> p a b", a=64)
        e.affine_select(out=ev[0:64], in_=ev[0:64], compare_op=OP.not_equal,
                        fill=1.0, base=0, pattern=[[1, 64], [0, 64]],
                        channel_multiplier=-1)
        e.affine_select(out=ev[64:128], in_=ev[64:128], compare_op=OP.not_equal,
                        fill=1.0, base=0, pattern=[[0, 64], [1, 64]],
                        channel_multiplier=-1)
        e.memset(ones_s, 1.0)
        e.memset(selh_s, 0.0)
        sv = selh_s.rearrange("p (a b c) -> p a b c", a=NH, b=2)
        e.affine_select(out=sv[0:8], in_=sv[0:8], compare_op=OP.not_equal,
                        fill=1.0, base=0, pattern=[[-2, NH], [-1, 2], [0, 64]],
                        channel_multiplier=1)
        e.memset(BuE[0:S, :], 0.0)
        e.memset(states[0:S, 0:R_SC], 0.0)
        return e.memset(halo_s.rearrange("p a b -> p (a b)"), 0.0)

    sch.st("gpsimd", g_ident, inc=c_ge)
    t_ident = c_ge.tok()

    sch.st("vector", lambda e: e.tensor_add(
        Mtmp[0:S, S:2 * S], Mtmp[0:S, 0:S], ident_f32[0:S, 0:S]),
        waits=[t_loads, t_ident], inc=c_ve)
    t_m = c_ve.tok()
    sch.st("tensor", lambda e: e.transpose(
        PS[0:S, 0:S], Mtmp[0:S, S:2 * S], ident_f32[0:S, 0:S]),
        waits=[t_m], inc=c_te)
    t_mt = c_te.tok()
    sch.st("vector", lambda e: e.tensor_copy(Mtmp[0:S, 2 * S:3 * S],
                                             PS[0:S, 0:S]),
           waits=[t_mt], inc=c_ve)
    t_mtc = c_ve.tok()
    sch.st("vector", lambda e: e.tensor_copy(Pall[0:S, 0:S],
                                             ident_f32[0:S, 0:S]),
           waits=[t_ident], inc=c_ve)
    sch.st("vector", lambda e: e.tensor_copy(Pall[0:S, S:2 * S],
                                             Mtmp[0:S, S:2 * S]),
           waits=[t_m], inc=c_ve)
    tv = c_ve.tok()
    for k in range(2, R_SC + 1):
        sch.st("tensor", lambda e, k=k: e.matmul(
            PS[0:S, 512:512 + S], Mtmp[0:S, 2 * S:3 * S],
            Pall[0:S, (k - 1) * S:k * S], start=True, stop=True),
            waits=[t_mtc, tv], inc=c_te)
        tk = c_te.tok()
        sch.st("vector", lambda e, k=k: e.tensor_copy(
            Pall[0:S, k * S:(k + 1) * S], PS[0:S, 512:512 + S]),
            waits=[tk], inc=c_ve)
        tv = c_ve.tok()
    sch.st("vector", lambda e: e.tensor_copy(Pall_bf[0:S, :], Pall[0:S, :]),
           waits=[tv], inc=c_ve)
    t_pre = c_ve.tok()
    dbg_store("lnw", lnw_s, waits=[t_cast_pre])
    dbg_store("keysbd", keysbd_s.rearrange("p a b -> p (a b)"),
              waits=[t_cast_pre])
    dbg_store("BmTs", BmT_s, waits=[t_cast_pre])
    dbg_store("Pall", Pall[0:S, :], waits=[t_pre])

    def ln_stats_finish(c0, nt, waits):
        sums = statb[:, c0:c0 + nt]
        sumsq = statb[:, c0 + 8:c0 + 8 + nt]
        mu = statb[:, c0 + 16:c0 + 16 + nt]
        inv = statb[:, c0 + 24:c0 + 24 + nt]
        vs = statb[:, c0 + 32:c0 + 32 + nt]
        yb = statb[:, c0 + 40:c0 + 40 + nt]
        tb = statb[:, c0 + 48:c0 + 48 + nt]

        spc_a = statb[:, 240:248]
        spc_b = statb[:, 248:256]

        def f1(e):
            def spc():
                e.tensor_copy(spc_b, spc_a)
            e.tensor_scalar(mu, sums, 1.0 / D, None, OP.mult)
            e.tensor_scalar(vs, sumsq, 1.0 / D, None, OP.mult)
            e.tensor_tensor(tb, mu, mu, OP.mult)
            spc()
            e.tensor_sub(vs, vs, tb)
            spc()
            return e.tensor_scalar(vs, vs, float(LN_EPS), None, OP.add)

        sch.st("vector", f1, waits=waits, inc=c_ve)
        tv1 = c_ve.tok()

        def fs(e):
            e.activation(yb, vs, AF.Ln)
            e.activation(dump16[:, 0:64], dump16[:, 64:128], AF.Square)
            return e.activation(tb, yb, AF.Exp, scale=-0.5)

        sch.st("scalar", fs, waits=[tv1], inc=c_se)
        ts1 = c_se.tok()

        def f2(e):
            def spc():
                e.tensor_copy(spc_b, spc_a)
            e.tensor_tensor(yb, tb, tb, OP.mult)
            spc()
            e.tensor_tensor(yb, yb, vs, OP.mult)
            spc()
            e.tensor_scalar(yb, yb, -0.5, 1.5, OP.mult, OP.add)
            spc()
            e.tensor_tensor(tb, tb, yb, OP.mult)
            spc()
            e.tensor_copy(inv, tb)
            return e.tensor_copy(spc_b, spc_a)

        return sch.st("vector", f2, waits=[ts1], inc=c_ve)

    def mu_col(c0, j):
        return statb[:, c0 + 16 + j:c0 + 17 + j]

    def inv_col(c0, j):
        return statb[:, c0 + 24 + j:c0 + 25 + j]

    st_base = 0
    xres_waits = [c_out.tok()]

    for st_i, T in enumerate(st_sizes):
        NT = T // 128
        NTH = T // 512
        warm_only = (st_i == 0)
        sfx = f"_st{st_i}"

        st_toks = []
        xs = x_in[st_base:st_base + T].rearrange("(a p) d -> p a d", p=128)
        tok_x = sch.st("sync", lambda e, xs=xs, NT=NT: e.dma_start(
            out=xres[:, 0:NT, :], in_=xs), waits=xres_waits,
            inc=c_x, inc_amt=16)

        c0a = 0
        for j in range(NT):
            sch.st("vector", lambda e, j=j: e.tensor_scalar(
                dump16, xres[:, j, :], 1.0, 0.0, OP.mult, OP.add,
                accum_out=statb[:, c0a + j:c0a + j + 1]),
                waits=[tok_x, t_pre], inc=c_ve)
            sch.st("scalar", lambda e, j=j: e.activation(
                dump16, xres[:, j, :], AF.Square,
                accum_out=statb[:, c0a + 8 + j:c0a + 9 + j]),
                waits=[tok_x], inc=c_se)
        t_stats1 = ln_stats_finish(c0a, NT, [c_ve.tok(), c_se.tok()])
        warm_pe([c_ve.tok()])

        c0b = 64
        t_hfm_tiles = []
        for j in range(NT):
            sch.st("vector", lambda e, j=j: e.tensor_scalar(
                xattn[:, j, :], xres[:, j, :], mu_col(c0a, j), inv_col(c0a, j),
                OP.subtract, OP.mult,
                accum_out=statb[:, c0b + j:c0b + j + 1]),
                waits=[t_stats1], inc=c_ve)
            tv_ = c_ve.tok()
            sch.st("scalar", lambda e, j=j: e.activation(
                dump16, xattn[:, j, :], AF.Square,
                accum_out=statb[:, c0b + 8 + j:c0b + 9 + j]),
                waits=[tv_], inc=c_se)
        t_stats2 = ln_stats_finish(c0b, NT, [c_ve.tok(), c_se.tok()])
        warm_pe([c_ve.tok()])
        dbg_store("xattn" + sfx, xattn, waits=[c_ve.tok()])
        dbg_store("statb1" + sfx, statb, waits=[t_stats1])
        dbg_store("statb" + sfx, statb, waits=[t_stats2])

        t_hfm_done = None
        for j in range(NT):
            tslot = trans_t[:, j % 2, :]
            prev_evac = c_se.tok()
            sch.st("vector", lambda e, j=j, tslot=tslot: (
                e.tensor_scalar(tslot, xattn[:, j, :], mu_col(c0b, j),
                                inv_col(c0b, j), OP.subtract, OP.mult),
                e.tensor_tensor(tslot, tslot, lnw_s, OP.mult),
                e.tensor_tensor(tslot, tslot, lnb_s, OP.add))[-1],
                waits=[t_stats2, prev_evac, t_cast_pre], inc=c_ve)
            t_h = c_ve.tok()
            for g in range(2):
                def t_tr(e, g=g, tslot=tslot):
                    r = None
                    for q in range(4):
                        dc = g * 4 + q
                        r = e.transpose(
                            tps(q), tslot[:, dc * 128:(dc + 1) * 128], ident_bf)
                    return r
                sch.st("tensor", t_tr, waits=[t_h, c_se.tok()], inc=c_te)
                t_tp = c_te.tok()
                sch.st("scalar", lambda e, j=j, g=g: e.activation(
                    h_fm[:, g * 4:(g + 1) * 4, j * 128:(j + 1) * 128],
                    tps_grp().rearrange("p (a b) -> p a b", a=4), AF.Identity),
                    waits=[t_tp], inc=c_se)
            if j == 0:
                dbg_store("htm" + sfx, trans_t[:, 0, :], waits=[t_h])
            t_hfm_done = c_se.tok()
            t_hfm_tiles.append(t_hfm_done)
            warm_pe([c_ve.tok()])
        dbg_store("h_fm" + sfx, h_fm, waits=[t_hfm_done])

        b_done = []
        t_gate_sig = None
        n_ic = NIC if warm_only else NPROJ
        for ic in range(n_ic):
            pid = ring.load(lambda slot, ic=ic: (
                lambda e: e.dma_start(
                    out=slot, in_=W_in_r[ic].rearrange("p a b -> p (a b)"))))
            wslot = ring.slot(pid)
            ltok = ring.wait_tok(pid)
            pb = (ic % 2) * 2
            wprev = b_done[-2] if len(b_done) >= 2 else (c_ve, 0)

            for th in range(NTH):
                def t_mm(e, wslot=wslot, pb=pb, th=th):
                    r = None
                    for dc in range(NDC):
                        r = e.matmul(
                            PS[:, (pb + th) * 512:(pb + th + 1) * 512],
                            wslot[:, dc * 128:(dc + 1) * 128],
                            h_fm[:, dc, th * 512:(th + 1) * 512],
                            start=(dc == 0), stop=(dc == NDC - 1))
                    return r
                hw = t_hfm_tiles[min(th * 4 + 3, len(t_hfm_tiles) - 1)]
                sch.st("tensor", t_mm, waits=[ltok, hw, wprev], inc=c_wc)
            t_mmd = c_wc.tok()
            ring.consumed(pid, t_mmd)
            if ic < NIC:
                sch.st("vector", lambda e, ic=ic, pb=pb, T=T: e.tensor_copy(
                    data_u[:, ic, 2:2 + T], PS[:, pb * 512:pb * 512 + T]),
                    waits=[t_mmd], inc=c_ve)
                b_done.append(c_ve.tok())
            else:
                sch.st("scalar", lambda e, ic=ic, pb=pb, T=T: e.activation(
                    sgate[:, ic - NIC, 0:T], PS[:, pb * 512:pb * 512 + T],
                    AF.Sigmoid), waits=[t_mmd], inc=c_se)
                b_done.append(c_se.tok())
                t_gate_sig = c_se.tok()

        data_toks = list(b_done[:NIC])
        if warm_only:
            for ic in range(NIC):
                sch.st("vector", lambda e, ic=ic: e.tensor_tensor(
                    data_u[:, ic, 2:2 + W_WARM], data_u[:, ic, 2:2 + W_WARM],
                    wmask_s, OP.mult), waits=[data_toks[ic]], inc=c_ve)
                data_toks[ic] = c_ve.tok()

        dbg_store("data" + sfx, data_u, waits=[b_done[NIC - 1]])
        if stop_phase == "Bdata":
            st_base += T
            xres_waits = [c_out.tok(), c_ve.tok(), c_se.tok()]
            continue

        t_u_last = None
        for ic in range(NIC):
            w0 = conv_s[:, ic, 0:1]
            w1 = conv_s[:, ic, 1:2]
            w2 = conv_s[:, ic, 2:3]

            def v_conv(e, ic=ic, w0=w0, w1=w1, w2=w2, T=T):
                dd = data_u[:, ic, :]
                e.tensor_copy(dd[:, 0:2], halo_s[:, ic, :])
                e.tensor_copy(statb[:, 248:256], statb[:, 240:248])
                e.tensor_scalar(conv_tmp[:, 0:T], dd[:, 0:T], w0, None, OP.mult)
                e.scalar_tensor_tensor(conv_tmp[:, 0:T], dd[:, 1:1 + T], w1,
                                       conv_tmp[:, 0:T], OP.mult, OP.add)
                e.scalar_tensor_tensor(conv_tmp[:, 0:T], dd[:, 2:2 + T], w2,
                                       conv_tmp[:, 0:T], OP.mult, OP.add)
                return e.tensor_copy(halo_s[:, ic, :], dd[:, T:T + 2])
            sch.st("vector", v_conv, waits=[data_toks[ic]], inc=c_ve)
            t_c = c_ve.tok()
            sch.st("scalar", lambda e, T=T: e.activation(
                sig_tmp[:, 0:T], conv_tmp[:, 0:T], AF.Sigmoid),
                waits=[t_c], inc=c_se)
            t_s = c_se.tok()
            sch.st("vector", lambda e, ic=ic, T=T: e.tensor_tensor(
                data_u[:, ic, 2:2 + T], conv_tmp[:, 0:T], sig_tmp[:, 0:T],
                OP.mult), waits=[t_s], inc=c_ve)
            t_u_last = c_ve.tok()
        dbg_store("u_fm" + sfx, data_u, waits=[t_u_last])
        if stop_phase == "B":
            st_base += T
            xres_waits = [c_out.tok(), c_ve.tok(), c_se.tok()]
            continue

        def t_bu(e, T=T, NTH2=(T + 511) // 512):
            r = None
            for th in range(NTH2):
                n = min(512, T - th * 512)
                for ic in range(NIC):
                    r = e.matmul(PS[0:S, (4 + th) * 512:(4 + th) * 512 + n],
                                 BmT_s[:, ic, :],
                                 data_u[:, ic, 2 + th * 512:2 + th * 512 + n],
                                 start=(ic == 0), stop=(ic == NIC - 1))
            return r
        sch.st("tensor", t_bu, waits=[t_u_last], inc=c_te)
        t_bu_t = c_te.tok()
        sch.st("vector", lambda e, T=T: e.tensor_copy(
            BuE[0:S, 31:31 + T], PS[0:S, 4 * 512:4 * 512 + T]),
            waits=[t_bu_t], inc=c_ve)
        t_bue_t = c_ve.tok()

        def t_taps(e, T=T):
            r = None
            for th in range((T + 511) // 512):
                n = min(512, T - th * 512)
                for k in range(R_SC):
                    r = e.matmul(PS[0:S, (4 + th) * 512:(4 + th) * 512 + n],
                                 Pall_bf[0:S, k * S:(k + 1) * S],
                                 BuE[0:S, 31 - k + th * 512:31 - k + th * 512 + n],
                                 start=(k == 0), stop=(k == R_SC - 1))
            return r
        dbg_store("BuE" + sfx, BuE[0:S, :], waits=[t_bue_t])
        sch.st("tensor", t_taps, waits=[t_bue_t], inc=c_te)
        t_taps_t = c_te.tok()
        sch.st("vector", lambda e, T=T: (
            e.tensor_copy(Bup[0:S, 0:T], PS[0:S, 4 * 512:4 * 512 + T]),
            e.tensor_copy(statb[:, 248:256], statb[:, 240:248]))[-1],
            waits=[t_taps_t], inc=c_ve)
        tv_prev = c_ve.tok()

        dbg_store("Bup" + sfx, Bup[0:S, :], waits=[tv_prev])
        for jj in range(T // R_SC):
            sch.st("tensor", lambda e, jj=jj: e.matmul(
                PS[0:S, 7 * 512:7 * 512 + R_SC],
                Pall[0:S, R_SC * S:(R_SC + 1) * S],
                states[0:S, jj * R_SC:(jj + 1) * R_SC],
                start=True, stop=True), waits=[tv_prev], inc=c_te)
            tt = c_te.tok()
            sch.st("vector", lambda e, jj=jj: e.tensor_add(
                states[0:S, R_SC + jj * R_SC:R_SC + (jj + 1) * R_SC],
                PS[0:S, 7 * 512:7 * 512 + R_SC],
                Bup[0:S, jj * R_SC:(jj + 1) * R_SC]),
                waits=[tt], inc=c_ve)
            tv_prev = c_ve.tok()

        sch.st("vector", lambda e, T=T: (
            e.tensor_copy(states_bf[0:S, R_SC:R_SC + T],
                          states[0:S, R_SC:R_SC + T]),
            e.tensor_copy(states[0:S, 0:R_SC], states[0:S, T:T + R_SC]))[-1],
            waits=[tv_prev], inc=c_ve)
        t_scan = c_ve.tok()
        dbg_store("states" + sfx, states[0:S, :], waits=[t_scan])
        if stop_phase == "scan":
            st_base += T
            xres_waits = [c_out.tok(), c_ve.tok(), c_se.tok()]
            continue

        if not warm_only:
            t_in2 = []
            for ic in range(NIC):
                pb = (ic % 2) * 2
                wprev = t_in2[-2] if len(t_in2) >= 2 else (c_ve, 0)

                def t_y(e, ic=ic, pb=pb, NTH=NTH):
                    r = None
                    for th in range(NTH):
                        r = e.matmul(
                            PS[:, (pb + th) * 512:(pb + th + 1) * 512],
                            CmT_s[0:S, ic * 128:(ic + 1) * 128],
                            states_bf[0:S, R_SC + th * 512:R_SC + (th + 1) * 512],
                            start=True, stop=True)
                    return r
                sch.st("tensor", t_y, waits=[t_scan, t_gate_sig, wprev],
                       inc=c_te)
                t_yt = c_te.tok()
                sch.st("vector", lambda e, ic=ic, pb=pb, T=T: e.tensor_tensor(
                    sgate[:, ic, 0:T], PS[:, pb * 512:pb * 512 + T],
                    sgate[:, ic, 0:T], OP.mult), waits=[t_yt], inc=c_ve)
                t_in2.append(c_ve.tok())

            t_ao = []
            for dc in range(NDC):
                pids = []
                for hp in range(2):
                    pid = ring.load(lambda slot, dc=dc, hp=hp: (
                        lambda e: e.dma_start(
                            out=slot[:, 0:768].rearrange(
                                "p (a b) -> p a b", a=6),
                            in_=W_out_r[dc, hp * 6:(hp + 1) * 6].rearrange(
                                "a p b -> p a b"), allow_slow_non_contiguous=True)))
                    pids.append(pid)
                ltoks = [ring.wait_tok(p) for p in pids]
                pb = (dc % 2) * 2
                wprev = t_ao[-2] if len(t_ao) >= 2 else (c_ve, 0)

                def t_wo(e, pids=pids, pb=pb, NTH=NTH):
                    r = None
                    for th in range(NTH):
                        for k2 in range(NIC):
                            sl = ring.slot(pids[k2 // 6])
                            r = e.matmul(
                                PS[:, (pb + th) * 512:(pb + th + 1) * 512],
                                sl[:, (k2 % 6) * 128:(k2 % 6 + 1) * 128],
                                sgate[:, k2, th * 512:(th + 1) * 512],
                                start=(k2 == 0), stop=(k2 == NIC - 1))
                    return r
                sch.st("tensor", t_wo, waits=ltoks + [t_in2[-1], wprev],
                       inc=c_wc)
                t_wot = c_wc.tok()
                for pid in pids:
                    ring.consumed(pid, t_wot)
                sch.st("scalar", lambda e, dc=dc, pb=pb, T=T: e.activation(
                    blockout[:, dc, 0:T], PS[:, pb * 512:pb * 512 + T],
                    AF.Identity), waits=[t_wot], inc=c_se)
                t_ao.append(c_se.tok())
            dbg_store("attn_out" + sfx, blockout, waits=[t_ao[-1]])
            if stop_phase == "C":
                st_base += T
                xres_waits = [c_out.tok(), c_ve.tok(), c_se.tok()]
                continue

            c0c = 128
            for j in range(NT):
                for g in range(2):
                    def t_tr(e, j=j, g=g):
                        r = None
                        for q in range(4):
                            dc = g * 4 + q
                            r = e.transpose(
                                tps(q), blockout[:, dc, j * 128:(j + 1) * 128],
                                ident_bf)
                        return r
                    sch.st("tensor", t_tr, waits=[t_ao[g * 4 + 3], c_ve.tok()],
                           inc=c_te)
                    t_tp = c_te.tok()
                    sch.st("vector", lambda e, j=j, g=g: e.tensor_tensor(
                        xattn[:, j, g * 512:(g + 1) * 512], tps_grp(),
                        xattn[:, j, g * 512:(g + 1) * 512], OP.add),
                        waits=[t_tp], inc=c_ve)
                t_in = c_ve.tok()
                sch.st("vector", lambda e, j=j: e.tensor_scalar(
                    dump16, xattn[:, j, :], 1.0, 0.0, OP.mult, OP.add,
                    accum_out=statb[:, c0c + j:c0c + j + 1]),
                    waits=[t_in], inc=c_ve)
                tv_ = c_ve.tok()
                sch.st("scalar", lambda e, j=j: e.activation(
                    dump16, xattn[:, j, :], AF.Square,
                    accum_out=statb[:, c0c + 8 + j:c0c + 9 + j]),
                    waits=[tv_], inc=c_se)
            t_stats3 = ln_stats_finish(c0c, NT, [c_ve.tok(), c_se.tok()])
            warm_pe([c_ve.tok()])

            t_x1 = []
            for j in range(NT):
                sch.st("vector", lambda e, j=j: (
                    e.tensor_scalar(sig_tmp, xattn[:, j, :], mu_col(c0c, j),
                                    inv_col(c0c, j), OP.subtract, OP.mult),
                    e.tensor_tensor(xres[:, j, :], xres[:, j, :], sig_tmp,
                                    OP.add))[-1],
                    waits=[t_stats3], inc=c_ve)
                t_x1.append(c_ve.tok())
            dbg_store("x1" + sfx, xres, waits=[t_x1[-1]])

            c0d = 192
            for j in range(NT):
                sch.st("vector", lambda e, j=j: e.tensor_scalar(
                    dump16, xres[:, j, :], 1.0, 0.0, OP.mult, OP.add,
                    accum_out=statb[:, c0d + j:c0d + j + 1]),
                    waits=[t_x1[j]], inc=c_ve)
                tv_ = c_ve.tok()
                sch.st("scalar", lambda e, j=j: e.activation(
                    dump16, xres[:, j, :], AF.Square,
                    accum_out=statb[:, c0d + 8 + j:c0d + 9 + j]),
                    waits=[tv_], inc=c_se)
            t_stats4 = ln_stats_finish(c0d, NT, [c_ve.tok(), c_se.tok()])
            warm_pe([c_ve.tok()])

            t_h1_tiles = []
            t_h1_done = None
            for j in range(NT):
                tslot = trans_t[:, j % 2, :]
                prev_evac = c_se.tok()
                sch.st("vector", lambda e, j=j, tslot=tslot: e.tensor_scalar(
                    tslot, xres[:, j, :], mu_col(c0d, j), inv_col(c0d, j),
                    OP.subtract, OP.mult), waits=[t_stats4, prev_evac],
                    inc=c_ve)
                t_h = c_ve.tok()
                for g in range(2):
                    def t_tr(e, g=g, tslot=tslot):
                        r = None
                        for q in range(4):
                            dc = g * 4 + q
                            r = e.transpose(
                                tps(q), tslot[:, dc * 128:(dc + 1) * 128],
                                ident_bf)
                        return r
                    sch.st("tensor", t_tr, waits=[t_h, c_se.tok()], inc=c_te)
                    t_tp = c_te.tok()
                    sch.st("scalar", lambda e, j=j, g=g: e.activation(
                        h1_fm[:, g * 4:(g + 1) * 4, j * 128:(j + 1) * 128],
                        tps_grp().rearrange("p (a b) -> p a b", a=4),
                        AF.Identity), waits=[t_tp], inc=c_se)
                t_h1_done = c_se.tok()
                t_h1_tiles.append(t_h1_done)
                warm_pe([c_ve.tok()])
            dbg_store("h1_fm" + sfx, h1_fm, waits=[t_h1_done])

            wq_pids = [ring.load(lambda slot, dp=dp: (
                lambda e: e.dma_start(out=slot, in_=Wq_r[dp])))
                for dp in range(NDC // 2)]
            t_q = []
            for qc in range(NQC):
                pb = (qc % 2) * 2
                wprev = t_q[-2] if len(t_q) >= 2 else (c_ve, 0)

                def t_wq(e, qc=qc, pb=pb, NTH=NTH):
                    r = None
                    for th in range(NTH):
                        for dc in range(NDC):
                            sl = ring.slot(wq_pids[dc // 2])
                            r = e.matmul(
                                PS[:, (pb + th) * 512:(pb + th + 1) * 512],
                                sl[:, (dc % 2) * 512 + qc * 128:
                                   (dc % 2) * 512 + (qc + 1) * 128],
                                h1_fm[:, dc, th * 512:(th + 1) * 512],
                                start=(dc == 0), stop=(dc == NDC - 1))
                    return r
                sch.st("tensor", t_wq,
                       waits=[ring.wait_tok(p) for p in wq_pids] +
                             [t_h1_done, wprev], inc=c_wc)
                t_qt = c_wc.tok()
                sch.st("vector", lambda e, qc=qc, pb=pb, T=T: e.tensor_copy(
                    q_fm[:, qc, 0:T], PS[:, pb * 512:pb * 512 + T]),
                    waits=[t_qt], inc=c_ve)
                t_q.append(c_ve.tok())
            for pid in wq_pids:
                ring.consumed(pid, t_qt)
            dbg_store("q_fm" + sfx, q_fm, waits=[t_q[-1]])

            if with_ffn:
                t_q_done = t_q[-1]
                t_sim_evac = (c_ve, 0)
                for half in range(2):
                    for jj in range(4):
                        if stop_phase == "F1t0" and (half, jj) != (0, 0):
                            continue
                        j = half * 4 + jj

                        def t_sim(e, j=j):
                            r = None
                            for h in range(NH):
                                r = e.matmul(
                                    PS[:, 4 * 512 + h * 128:4 * 512 + (h + 1) * 128],
                                    q_fm[:, h, j * 128:(j + 1) * 128],
                                    keysbd_s[:, h, :],
                                    start=(h == 0), stop=(h == NH - 1))
                            return r
                        sch.st("tensor", t_sim, waits=[t_q_done, t_sim_evac],
                               inc=c_te)
                        t_simt = c_te.tok()

                        def v_stage(e, half=half, jj=jj):
                            e.tensor_copy(simtm_t, PS[:, 4 * 512:5 * 512])
                            for h in range(NH):
                                e.max(out=stage[:, h * 16:h * 16 + 8],
                                      in_=simtm_t[:, h * 128:h * 128 + 64])
                                e.max(out=stage[:, h * 16 + 8:h * 16 + 16],
                                      in_=simtm_t[:, h * 128 + 64:h * 128 + 128])
                            base = half * 32 + jj * 8
                            e.tensor_copy(statb[:, 248:256], statb[:, 240:248])
                            e.tensor_copy(statb[:, 248:256], statb[:, 240:248])
                            e.tensor_copy(sxbf, stage)
                            thv = thrTM[:, base:base + 8].rearrange(
                                "p (a b) -> p a b", a=4)
                            stv = stage.rearrange("p (a b) -> p a b", a=4)
                            e.tensor_copy(thv[:, :, 0:1], stv[:, :, 7:8])
                            e.tensor_copy(thv[:, :, 1:2], stv[:, :, 15:16])
                            return e.tensor_copy(rall, sxbf)
                        sch.st("vector", v_stage, waits=[t_simt], inc=c_ve)
                        t_sim_evac = c_ve.tok()

                        def v_cand(e, half=half, jj=jj):
                            for h in range(NH):
                                for a in range(8):
                                    e.tensor_scalar(
                                        candg4[:, h, a * 8:(a + 1) * 8],
                                        rall[:, h * 16 + 8:h * 16 + 16],
                                        rall[:, h * 16 + a:h * 16 + a + 1],
                                        None, OP.add)
                                e.max(out=v8tmp[:, h * 8:h * 8 + 8],
                                      in_=candg4[:, h, :])
                            e.tensor_copy(statb[:, 248:256], statb[:, 240:248])
                            e.tensor_copy(statb[:, 248:256], statb[:, 240:248])
                            for h in range(NH):
                                e.tensor_copy(nvz[:, half, jj, h, 0:1],
                                              v8tmp[:, h * 8 + 7:h * 8 + 8])
                                e.tensor_scalar(
                                    nvz[:, half, jj, h, 1:2],
                                    v8tmp[:, h * 8:h * 8 + 1],
                                    -1.0, None, OP.mult)
                            return e.tensor_copy(statb[:, 248:256],
                                                 statb[:, 240:248])
                        sch.st("vector", v_cand, waits=[t_sim_evac], inc=c_ve)
                        tvc = c_ve.tok()

                        def s_exp(e, half=half, jj=jj):
                            r = None
                            for h in range(NH):
                                r = e.activation(exgrid4[:, h, :],
                                                 candg4[:, h, :], AF.Exp,
                                                 bias=nvz[:, half, jj, h, 1:2])
                            return r
                        sch.st("scalar", s_exp, waits=[tvc], inc=c_se)
                        tse = c_se.tok()

                        def v_z(e, half=half, jj=jj):
                            for h in range(NH):
                                e.scalar_tensor_tensor(
                                    exgrid4[:, h, :], candg4[:, h, :],
                                    nvz[:, half, jj, h, 0:1],
                                    exgrid4[:, h, :], OP.is_ge, OP.mult)
                            for h in range(NH):
                                e.reduce_sum(nvz[:, half, jj, h, 2:3],
                                             exgrid4[:, h, :],
                                             axis=mybir.AxisListType.X)
                            return e.tensor_copy(statb[:, 248:256],
                                                 statb[:, 240:248])
                        sch.st("vector", v_z, waits=[tse], inc=c_ve)
                        if stop_phase == "F1t0":
                            dbg_store("f1_simtm" + sfx, simtm_t,
                                      waits=[c_ve.tok()])
                            dbg_store("f1_stage" + sfx, stage,
                                      waits=[c_ve.tok()])
                            dbg_store("f1_rall" + sfx, rall,
                                      waits=[c_ve.tok()])
                            dbg_store("f1_candg" + sfx,
                                      candg4.rearrange("p a b -> p (a b)"),
                                      waits=[c_ve.tok()])
                            dbg_store("f1_exg" + sfx,
                                      exgrid4.rearrange("p a b -> p (a b)"),
                                      waits=[c_ve.tok()])
                            dbg_store("f1_nvz" + sfx,
                                      nvz.rearrange("p a b c d -> p (a b c d)"),
                                      waits=[c_ve.tok()])
                            dbg_store("f1_v8t" + sfx, v8tmp,
                                      waits=[c_ve.tok()])

                tvz_all = c_ve.tok()
                sch.st("scalar", lambda e: e.activation(
                    lnzb, nvz[:, :, :, :, 2].rearrange("p a b c -> p (a b c)"),
                    AF.Ln), waits=[tvz_all], inc=c_se)
                tlnz = c_se.tok()
                sch.st("vector", lambda e: e.tensor_sub(
                    nvz[:, :, :, :, 3].rearrange("p a b c -> p (a b c)"),
                    nvz[:, :, :, :, 1].rearrange("p a b c -> p (a b c)"),
                    lnzb), waits=[tlnz], inc=c_ve)
                t_f1_done = c_ve.tok()
                dbg_store("nvz" + sfx, nvz.rearrange("p a b c d -> p (a b c d)"),
                          waits=[t_f1_done])

                t_f0_done = []
                for half in range(2):
                    def t_thr(e, half=half):
                        r = None
                        for jj in range(4):
                            r = e.transpose(
                                PS[0:8, 5 * 512 + jj * 128:5 * 512 + (jj + 1) * 128],
                                thrTM[:, half * 32 + jj * 8:half * 32 + (jj + 1) * 8],
                                ident_f32)
                        return r
                    sch.st("tensor", t_thr, waits=[t_f1_done], inc=c_te)
                    t_thrt = c_te.tok()
                    sch.st("vector", lambda e: e.tensor_copy(
                        thrRows[0:8, :], PS[0:8, 5 * 512:6 * 512]),
                        waits=[t_thrt], inc=c_ve)
                    t_thre = c_ve.tok()

                    for h in range(NH):
                        b0 = (h % 2) * 2
                        b1 = b0 + 1

                        def t_km(e, half=half, h=h, b0=b0, b1=b1):
                            e.matmul(PS[:, b0 * 512:(b0 + 1) * 512],
                                     keysbd_s[:, h, :],
                                     q_fm[:, h, half * 512:(half + 1) * 512],
                                     start=True, stop=True)
                            return e.matmul(
                                PS[:, b1 * 512:(b1 + 1) * 512],
                                selh_s[0:8, h * 128:(h + 1) * 128],
                                thrRows[0:8, :], start=True, stop=True)
                        wprev = t_f0_done[-2] if len(t_f0_done) >= 2 else (c_ve, 0)
                        sch.st("tensor", t_km, waits=[t_thre, wprev], inc=c_te)
                        t_kmt = c_te.tok()

                        def v_km(e, half=half, h=h, b0=b0, b1=b1):
                            e.tensor_copy(thrKM, PS[:, b1 * 512:(b1 + 1) * 512])
                            e.tensor_tensor(mask8, PS[:, b0 * 512:(b0 + 1) * 512],
                                            thrKM, OP.is_ge)
                            e.tensor_scalar(mask8, mask8, 1.0e30, -1.0e30,
                                            OP.mult, OP.add)
                            return e.tensor_tensor(
                                simkm_bf[:, half, h, :], mask8,
                                PS[:, b0 * 512:(b0 + 1) * 512], OP.add)
                        sch.st("vector", v_km, waits=[t_kmt], inc=c_ve)
                        t_f0_done.append(c_ve.tok())
                t_f0_tok = t_f0_done[-1]

                dbg_store("simkm" + sfx,
                          simkm_bf.rearrange("p a b c -> p (a b c)"),
                          waits=[t_f0_tok])

                t_J_evac_prev = (c_se, 0)
                for half in range(2):
                    if stop_phase in ("F1", "F0", "F1t0"):
                        break
                    unit = 0
                    t_stt_done = {}
                    t_mm_done = {}
                    last_add = (c_ve, 0)
                    for jj in range(4):
                        for h in range(NH):
                            for eh2 in range(2):
                                base = (unit % 2) * 2048
                                expd = tmp_stt if unit % 2 == 0 else tmp2_stt
                                wprev = t_stt_done.get(unit - 2, (c_ve, 0))

                                def t_cf(e, half=half, h=h, jj=jj, eh2=eh2,
                                         base=base):
                                    r = None
                                    for es in range(4):
                                        r = e.matmul(
                                            PS[:, base + es * 512:base + (es + 1) * 512],
                                            simkm_bf[:, half, h,
                                                     jj * 128:(jj + 1) * 128],
                                            Emat[:, eh2 * 2048 + es * 512:
                                                 eh2 * 2048 + (es + 1) * 512],
                                            start=True, stop=True)
                                    return r
                                w_cf = [t_f0_tok, wprev]
                                if unit == 0:
                                    w_cf += [c_ve.tok(), c_se.tok(),
                                             c_te.tok(), t_J_evac_prev]
                                sch.st("tensor", t_cf, waits=w_cf, inc=c_te)
                                t_mm_done[unit] = c_te.tok()

                                sch.st("scalar", lambda e, half=half, jj=jj,
                                       h=h, base=base, expd=expd:
                                       e.activation(
                                           expd, PS[:, base:base + 2048],
                                           AF.Exp,
                                           bias=nvz[:, half, jj, h, 3:4]),
                                       waits=[t_mm_done[unit]], inc=c_se)
                                t_exp = c_se.tok()

                                def v_stt(e, half=half, jj=jj, h=h, base=base,
                                          eh2=eh2, expd=expd):
                                    dst = (acc_j[jj][:, eh2 * 2048:(eh2 + 1) * 2048]
                                           if h == 0 else stt_out)
                                    e.scalar_tensor_tensor(
                                        dst, PS[:, base:base + 2048],
                                        nvz[:, half, jj, h, 0:1], expd,
                                        OP.is_ge, OP.mult)
                                    if h > 0:
                                        return e.tensor_add(
                                            acc_j[jj][:, eh2 * 2048:(eh2 + 1) * 2048],
                                            acc_j[jj][:, eh2 * 2048:(eh2 + 1) * 2048],
                                            stt_out)
                                    return e.tensor_copy(statb[:, 248:256],
                                                         statb[:, 240:248])
                                sch.st("vector", v_stt, waits=[t_exp], inc=c_ve)
                                t_stt_done[unit] = c_ve.tok()
                                unit += 1
                    t_f3_done = c_ve.tok()
                    if half == 0:
                        dbg_store("SWacc" + sfx, acc_j[0], waits=[t_f3_done])
                    if stop_phase == "F3":
                        continue

                    t_tr_done = (c_te, 0)
                    t_evV_prev = (c_se, 0)
                    t_vt_prev = (c_ve, 0)
                    for eq in range(4):
                        t_wallt = None
                        for part in range(2):
                            dcs = range(part * 4, part * 4 + 4)
                            pids = [ring.load(
                                lambda slot, eq=eq, dc=dc: (
                                    lambda e: e.dma_start(out=slot,
                                                          in_=DOWNT_r[eq, dc])))
                                for dc in dcs]

                            t_h1_half = t_h1_tiles[half * 4 + 3]

                            def t_wall(e, half=half, pids=pids, part=part):
                                r = None
                                for i, dc in enumerate(
                                        range(part * 4, part * 4 + 4)):
                                    sl = ring.slot(pids[i])
                                    for jj in range(4):
                                        lhs = h1_fm[:, dc,
                                                    half * 512 + jj * 128:
                                                    half * 512 + (jj + 1) * 128]
                                        for eh in range(2):
                                            r = e.matmul(
                                                PS[:, jj * 1024 + eh * 512:
                                                   jj * 1024 + (eh + 1) * 512],
                                                lhs,
                                                sl[:, eh * 512:(eh + 1) * 512],
                                                start=(dc == 0),
                                                stop=(dc == NDC - 1))
                                return r
                            waits = [ring.wait_tok(p) for p in pids]
                            if part == 0:
                                waits += [t_f3_done, t_evV_prev, t_vt_prev,
                                          t_h1_half]
                            sch.st("tensor", t_wall, waits=waits, inc=c_wc)
                            t_wallt = c_wc.tok()
                            for p in pids:
                                ring.consumed(p, t_wallt)

                        t_ev = []
                        for jj in range(4):
                            sch.st("scalar", lambda e, jj=jj: e.activation(
                                silu_buf[:, jj, :],
                                PS[:, jj * 1024:(jj + 1) * 1024], AF.Silu),
                                waits=[t_wallt], inc=c_se)
                            t_ev.append(c_se.tok())
                        t_evV_prev = t_ev[-1]

                        t_vm = []
                        for jj in range(4):
                            sch.st("vector", lambda e, jj=jj, eq=eq:
                                   e.tensor_tensor(
                                       acc_j[jj][:, eq * 1024:(eq + 1) * 1024],
                                       acc_j[jj][:, eq * 1024:(eq + 1) * 1024],
                                       silu_buf[:, jj, :], OP.mult),
                                   waits=[t_ev[jj]], inc=c_ve)
                            t_vm.append(c_ve.tok())

                        for jj in range(4):
                            def t_vt(e, jj=jj, eq=eq):
                                r = None
                                for ec in range(8):
                                    r = e.transpose(
                                        PSb[:, 2 * jj * 1024 + ec * 128:
                                            2 * jj * 1024 + (ec + 1) * 128],
                                        acc_j[jj][:, (eq * 8 + ec) * 128:
                                                  (eq * 8 + ec + 1) * 128],
                                        ident_bf)
                                return r
                            sch.st("tensor", t_vt,
                                   waits=[t_vm[jj], t_ev[jj]], inc=c_te)
                            t_trt = c_te.tok()
                            sch.st("vector", lambda e, jj=jj, eq=eq:
                                   e.tensor_copy(
                                       VT[:, eq * 8:eq * 8 + 8,
                                          jj * 128:(jj + 1) * 128],
                                       PSb[:, 2 * jj * 1024:2 * jj * 1024 + 1024]
                                       .rearrange("p (a b) -> p a b", a=8)),
                                   waits=[t_trt], inc=c_ve)
                        t_vt_prev = c_ve.tok()
                    t_f2_done = c_ve.tok()
                    if half == 0:
                        dbg_store("VT" + sfx,
                                  VT.rearrange("p a b -> p (a b)"),
                                  waits=[t_f2_done])
                    if stop_phase == "F2":
                        continue

                    t_m_toks = []
                    for dcd in range(NKC):
                        pidg = ring.load(lambda slot, dcd=dcd: (
                            lambda e: e.dma_start(
                                out=slot, in_=Wg_r[dcd].rearrange(
                                    "p a b -> p (a b)"))))
                        pidu = ring.load(lambda slot, dcd=dcd: (
                            lambda e: e.dma_start(
                                out=slot, in_=Wu_r[dcd].rearrange(
                                    "p a b -> p (a b)"))))
                        bkA = (dcd % 2) * 2
                        bkB = bkA + 1
                        wpv = (t_m_toks[-2][0] if len(t_m_toks) >= 2
                               else (c_ve, 0))
                        wpv2 = (t_m_toks[-2][1] if len(t_m_toks) >= 2
                                else (c_se, 0))

                        def t_gu(e, half=half, pidg=pidg, pidu=pidu,
                                 bkA=bkA, bkB=bkB):
                            r = None
                            for pid, bk in ((pidg, bkA), (pidu, bkB)):
                                sl = ring.slot(pid)
                                for dc in range(NDC):
                                    r = e.matmul(
                                        PS[:, bk * 512:(bk + 1) * 512],
                                        sl[:, dc * 128:(dc + 1) * 128],
                                        h1_fm[:, dc,
                                              half * 512:(half + 1) * 512],
                                        start=(dc == 0), stop=(dc == NDC - 1))
                            return r
                        sch.st("tensor", t_gu,
                               waits=[ring.wait_tok(pidg), ring.wait_tok(pidu),
                                      t_f2_done, wpv, wpv2,
                                      t_h1_tiles[half * 4 + 3]], inc=c_wc)
                        t_gut = c_wc.tok()
                        ring.consumed(pidg, t_gut)
                        ring.consumed(pidu, t_gut)

                        sh = (dcd % 2) * 512
                        sch.st("scalar", lambda e, dcd=dcd, bkA=bkA, sh=sh:
                               e.activation(sig_tmp[:, sh:sh + 512],
                                            PS[:, bkA * 512:(bkA + 1) * 512],
                                            AF.Silu,
                                            bias=bgbu_s[:, dcd:dcd + 1]),
                               waits=[t_gut], inc=c_se)
                        t_sg = c_se.tok()

                        def v_m(e, dcd=dcd, bkB=bkB, sh=sh):
                            e.tensor_scalar(
                                dump16[:, sh:sh + 512],
                                PS[:, bkB * 512:(bkB + 1) * 512],
                                bgbu_s[:, NKC + dcd:NKC + dcd + 1], None,
                                OP.add)
                            return e.tensor_tensor(
                                m_fm[:, dcd, :], sig_tmp[:, sh:sh + 512],
                                dump16[:, sh:sh + 512], OP.mult)
                        sch.st("vector", v_m, waits=[t_sg], inc=c_ve)
                        t_m_toks.append((c_ve.tok(), t_sg))
                    t_h_done = c_ve.tok()
                    if half == 0:
                        dbg_store("m" + sfx,
                                  m_fm.rearrange("p a b -> p (a b)"),
                                  waits=[t_h_done])
                    if stop_phase == "H":
                        continue

                    t_i_mm = None
                    for k in range(64):
                        if k < NKC:
                            srcd = Wd_r[k]
                        else:
                            srcd = UE_r[k - NKC]
                        pid = ring.load(lambda slot, srcd=srcd: (
                            lambda e: e.dma_start(
                                out=slot,
                                in_=srcd.rearrange("p a b -> p (a b)"))))

                        def t_i(e, k=k, pid=pid, half=half):
                            sl = ring.slot(pid)
                            rhs = (m_fm[:, k, :] if k < NKC
                                   else VT[:, k - NKC, :])
                            r = None
                            for ds in range(NDC):
                                r = e.matmul(
                                    PS[:, ds * 512:(ds + 1) * 512],
                                    sl[:, ds * 128:(ds + 1) * 128], rhs,
                                    start=(k == 0), stop=(k == 63))
                            return r
                        waits = [ring.wait_tok(pid)]
                        if k == 0:
                            waits += [t_h_done, t_f2_done, t_J_evac_prev]
                        sch.st("tensor", t_i, waits=waits, inc=c_wc)
                        t_i_mm = c_wc.tok()
                        ring.consumed(pid, t_i_mm)

                    t_ffn_ev = []
                    for ds in range(NDC):
                        sch.st("scalar", lambda e, ds=ds: e.activation(
                            ffnout[:, ds, :], PS[:, ds * 512:(ds + 1) * 512],
                            AF.Identity, bias=bd_s[:, ds:ds + 1]),
                            waits=[t_i_mm], inc=c_se)
                        t_ffn_ev.append(c_se.tok())
                    t_J_evac_prev = t_ffn_ev[-1]
                    if half == 0:
                        dbg_store("ffnout" + sfx,
                                  ffnout.rearrange("p a b -> p (a b)"),
                                  waits=[t_J_evac_prev])
                    if stop_phase == "I":
                        continue

                    c0J = 0
                    for pair in range(2):
                        tvJ = None
                        tsJ = None
                        for pj in range(2):
                            jj = pair * 2 + pj
                            for g in range(2):
                                def t_trJ(e, jj=jj, g=g):
                                    r = None
                                    for q in range(4):
                                        dc = g * 4 + q
                                        r = e.transpose(
                                            tps(q),
                                            ffnout[:, dc, jj * 128:(jj + 1) * 128],
                                            ident_bf)
                                    return r
                                sch.st("tensor", t_trJ,
                                       waits=[t_ffn_ev[-1], c_ve.tok(),
                                              c_se.tok()], inc=c_te)
                                t_tpJ = c_te.tok()

                                def v_evJ(e, pj=pj, g=g):
                                    return e.tensor_scalar(
                                        trans_t[:, pj, g * 512:(g + 1) * 512],
                                        tps_grp(), 1.0, 0.0, OP.mult, OP.add,
                                        accum_out=statb[:, c0J + 48 + g * 8 + pj:
                                                        c0J + 49 + g * 8 + pj])
                                sch.st("vector", v_evJ, waits=[t_tpJ], inc=c_ve)
                            tvJ_t = c_ve.tok()
                            sch.st("vector", lambda e, pj=pj: e.tensor_add(
                                statb[:, c0J + pj:c0J + pj + 1],
                                statb[:, c0J + 48 + pj:c0J + 49 + pj],
                                statb[:, c0J + 56 + pj:c0J + 57 + pj]),
                                waits=[tvJ_t], inc=c_ve)
                            tvJ = c_ve.tok()
                            sch.st("scalar", lambda e, pj=pj: e.activation(
                                dump16, trans_t[:, pj, :], AF.Square,
                                accum_out=statb[:, c0J + 8 + pj:c0J + 9 + pj]),
                                waits=[tvJ], inc=c_se)
                            tsJ = c_se.tok()
                        t_stJ = ln_stats_finish(c0J, 2, [tvJ, tsJ])

                        for pj in range(2):
                            jj = pair * 2 + pj
                            j = half * 4 + jj
                            wst = (st_toks[-2] if len(st_toks) >= 2
                                   else (c_out, 0))

                            def v_x2(e, pj=pj, j=j):
                                e.tensor_scalar(
                                    trans_t[:, pj, :], trans_t[:, pj, :],
                                    mu_col(c0J, pj), inv_col(c0J, pj),
                                    OP.subtract, OP.mult)
                                return e.tensor_tensor(
                                    out_stage[:, pj, :], xres[:, j, :],
                                    trans_t[:, pj, :], OP.add)
                            sch.st("vector", v_x2, waits=[t_stJ, wst], inc=c_ve)
                            t_x2 = c_ve.tok()
                            row0 = (st_base - W_WARM) + j * 128
                            tok = sch.st("sync", lambda e, pj=pj, row0=row0:
                                         e.dma_start(
                                             out=y_out[row0:row0 + 128, :],
                                             in_=out_stage[:, pj, :]),
                                         waits=[t_x2], inc=c_out, inc_amt=16)
                            st_toks.append(tok)
            else:
                for j in range(NT):
                    sch.st("sync", lambda e, j=j: e.dma_start(
                        out=y_out[(st_base - W_WARM) + j * 128:
                                  (st_base - W_WARM) + (j + 1) * 128, :],
                        in_=xres[:, j, :]), waits=[t_x1[-1]],
                        inc=c_out, inc_amt=16)

        st_base += T
        xres_waits = [c_out.tok(), c_ve.tok(), c_se.tok()]

    sch.st("sync", None, waits=[c_out.tok(), c_dbg.tok()])

    with nc.Block() as block:
        @block.tensor
        def _(e):
            sch.replay("tensor", e)

        @block.vector
        def _(e):
            sch.replay("vector", e)

        @block.scalar
        def _(e):
            sch.replay("scalar", e)

        @block.gpsimd
        def _(e):
            sch.replay("gpsimd", e)

        @block.sync
        def _(e):
            sch.replay("sync", e)

    return nc, sorted(dbg_written)


def emit_ffn_st(ctx):
    raise NotImplementedError


def kernel(**inputs):
    nc, _ = build_nc()
    in_maps = [_stage_core_inputs(inputs, c) for c in range(NCORES)]
    res = run_bass_kernel_spmd(nc, in_maps, core_ids=list(range(NCORES)))
    out = np.empty((BATCH, SEQ, D), np.float32)
    for c in range(NCORES):
        b, half = c // 2, c % 2
        out[b, half * T_CORE:(half + 1) * T_CORE] = res.results[c]["y_out"]
    return out


# version 03
# speedup vs baseline: 1.0198x; 1.0198x over previous
import numpy as np

import concourse.bass as bass
import concourse.mybir as mybir
from concourse.bass_utils import run_bass_kernel_spmd

F32 = mybir.dt.float32
BF16 = mybir.dt.bfloat16
U8 = mybir.dt.uint8
I32 = mybir.dt.int32
AF = mybir.ActivationFunctionType
OP = mybir.AluOpType

D = 1024
INNER = 1536
S = 16
KCONV = 3
DCD = 4096
NEXP = 4096
NH = 4
SEQ = 8192
BATCH = 4
NCORES = 8

W_WARM = 512
T_CORE = 4096
R_SC = 32
NDC = 8
NIC = 12
NPROJ = 24
NQC = 4
NEC = 32
NKC = 32
ST_SIZES = [W_WARM, 1024, 1024, 1024, 1024]
LN_EPS = 1e-5
NEG_BIG = -1.0e30



def _stage_core_inputs(inp, core):
    b, half = core // 2, core % 2
    x = np.ascontiguousarray(np.asarray(inp["L0_attn_x"], np.float32)[b])
    t0 = half * T_CORE
    if half == 0:
        xw = np.zeros((W_WARM, D), np.float32)
    else:
        xw = x[t0 - W_WARM:t0]
    d = {}
    d["x_in"] = np.ascontiguousarray(np.concatenate([xw, x[t0:t0 + T_CORE]], 0))
    d["warm_mask"] = np.broadcast_to(
        np.float32(0.0 if half == 0 else 1.0), (128, W_WARM)).copy()
    d["lnw_rep"] = np.broadcast_to(
        np.asarray(inp["L0_attn_ln_w"], np.float32)[None, :], (128, D)).copy()
    d["lnb_rep"] = np.broadcast_to(
        np.asarray(inp["L0_attn_ln_b"], np.float32)[None, :], (128, D)).copy()
    W_in = np.asarray(inp["L0_attn_W_in"], np.float32)
    d["W_in_r"] = np.ascontiguousarray(
        W_in.reshape(NDC, 128, NPROJ, 128).transpose(2, 1, 0, 3))
    d["conv_sb"] = np.ascontiguousarray(
        np.asarray(inp["L0_attn_conv_w"], np.float32).reshape(NIC, 128, KCONV)
        .transpose(1, 0, 2))
    d["AT"] = np.ascontiguousarray(np.asarray(inp["L0_attn_A"], np.float32).T)
    d["BmT_r"] = np.ascontiguousarray(
        np.asarray(inp["L0_attn_Bm"], np.float32).T.reshape(NIC, 128, S))
    d["CmT"] = np.ascontiguousarray(np.asarray(inp["L0_attn_Cm"], np.float32).T)
    W_out = np.asarray(inp["L0_attn_W_out"], np.float32)
    d["W_out_r"] = np.ascontiguousarray(
        W_out.reshape(NIC, 128, NDC, 128).transpose(2, 0, 1, 3))
    Wq = np.asarray(inp["L0_ffn_Wq"], np.float32)
    perm = np.array([p * 256 + h * 64 + n
                     for h in range(NH) for p in range(2) for n in range(64)])
    d["Wq_r"] = np.ascontiguousarray(
        Wq[:, perm].reshape(4, 2, 128, 512).transpose(0, 2, 1, 3)
        .reshape(4, 128, 1024))
    keys = np.asarray(inp["L0_ffn_keys"], np.float32)
    kbd = np.zeros((NH, 128, 128), np.float32)
    for h in range(NH):
        for p in range(2):
            kbd[h, p * 64:(p + 1) * 64, p * 64:(p + 1) * 64] = keys[h, :, p, :].T
    d["keysbd"] = kbd
    Wg = np.asarray(inp["L0_ffn_Wg"], np.float32)
    Wu = np.asarray(inp["L0_ffn_Wu"], np.float32)
    d["Wg_r"] = np.ascontiguousarray(
        Wg.reshape(NDC, 128, NKC, 128).transpose(2, 1, 0, 3))
    d["Wu_r"] = np.ascontiguousarray(
        Wu.reshape(NDC, 128, NKC, 128).transpose(2, 1, 0, 3))
    d["bg_sb"] = np.ascontiguousarray(
        np.asarray(inp["L0_ffn_bg"], np.float32).reshape(NKC, 128).T)
    d["bu_sb"] = np.ascontiguousarray(
        np.asarray(inp["L0_ffn_bu"], np.float32).reshape(NKC, 128).T)
    d["bd_sb"] = np.ascontiguousarray(
        np.asarray(inp["L0_ffn_bd"], np.float32).reshape(NDC, 128).T)
    d["Wd_r"] = np.ascontiguousarray(
        np.asarray(inp["L0_ffn_Wd"], np.float32).reshape(NKC, 128, NDC, 128))
    down = np.asarray(inp["L0_ffn_down_embed"], np.float32)
    d["DOWNT_r"] = np.ascontiguousarray(
        down.T.reshape(NDC, 128, 4, 1024).transpose(2, 0, 1, 3))
    d["UE_r"] = np.ascontiguousarray(
        np.asarray(inp["L0_ffn_up_embed"], np.float32).reshape(NEC, 128, NDC, 128))
    return d



class Chan:
    __slots__ = ("sem", "total", "name")

    def __init__(self, sem, name):
        self.sem = sem
        self.total = 0
        self.name = name

    def tok(self):
        return (self, self.total)


class Sched:
    ENGINES = ("tensor", "vector", "scalar", "gpsimd", "sync")

    def __init__(self):
        self.steps = {e: [] for e in self.ENGINES}

    def st(self, eng, fn, waits=(), inc=None, inc_amt=1):
        ws = [(c, v) for (c, v) in waits if v > 0]
        self.steps[eng].append((fn, ws, inc, inc_amt))
        if inc is not None:
            inc.total += inc_amt
            return inc.tok()
        return None

    def replay(self, eng_name, e):
        seen = {}
        for fn, waits, inc, inc_amt in self.steps[eng_name]:
            for chan, val in waits:
                if seen.get(chan.name, -1) >= val:
                    continue
                e.wait_ge(chan.sem, val)
                seen[chan.name] = val
            inst = fn(e) if fn is not None else None
            if inc is not None:
                if inst is not None:
                    inst.then_inc(inc.sem, inc_amt)
                else:
                    e.sem_inc(inc.sem, inc_amt)


class Ring:

    def __init__(self, sched, slots_aps, slot_chans, cons_chan):
        self.sched = sched
        self.slots = slots_aps
        self.nslots = len(slots_aps)
        self.slot_chans = slot_chans
        self.cons_chan = cons_chan
        self.load_toks = []
        self.cons_toks = []

    def load(self, make_dma, extra_waits=()):
        p = len(self.load_toks)
        slot = self.slots[p % self.nslots]
        waits = list(extra_waits)
        if p >= self.nslots:
            waits.append(self.cons_toks[p - self.nslots])
        fn = make_dma(slot)
        tok = self.sched.st("gpsimd", fn, waits=waits,
                            inc=self.slot_chans[p % self.nslots], inc_amt=16)
        self.load_toks.append(tok)
        return p

    def slot(self, p):
        return self.slots[p % self.nslots]

    def wait_tok(self, p):
        return self.load_toks[p]

    def consumed(self, p, tok):
        assert p == len(self.cons_toks), (p, len(self.cons_toks))
        self.cons_toks.append(tok)



def build_nc(dbg_names=(), n_st=None, with_ffn=True, stop_phase=None):
    import contextlib
    nc = bass.Bass("TRN2", target_bir_lowering=False)
    st_sizes = ST_SIZES if n_st is None else ST_SIZES[:n_st]
    dbg_names = set(dbg_names)

    def din(name, shape, dt=F32):
        return nc.dram_tensor(name, list(shape), dt, kind="ExternalInput").ap()

    x_in = din("x_in", (W_WARM + T_CORE, D))
    warm_mask = din("warm_mask", (128, W_WARM))
    lnw_rep = din("lnw_rep", (128, D))
    lnb_rep = din("lnb_rep", (128, D))
    W_in_r = din("W_in_r", (NPROJ, 128, NDC, 128))
    conv_sb_d = din("conv_sb", (128, NIC, KCONV))
    AT_d = din("AT", (S, S))
    BmT_d = din("BmT_r", (NIC, 128, S))
    CmT_d = din("CmT", (S, INNER))
    W_out_r = din("W_out_r", (NDC, NIC, 128, 128))
    Wq_r = din("Wq_r", (4, 128, 1024))
    keysbd_d = din("keysbd", (NH, 128, 128))
    Wg_r = din("Wg_r", (NKC, 128, NDC, 128))
    Wu_r = din("Wu_r", (NKC, 128, NDC, 128))
    bg_d = din("bg_sb", (128, NKC))
    bu_d = din("bu_sb", (128, NKC))
    bd_d = din("bd_sb", (128, NDC))
    Wd_r = din("Wd_r", (NKC, 128, NDC, 128))
    DOWNT_r = din("DOWNT_r", (4, NDC, 128, 1024))
    UE_r = din("UE_r", (NEC, 128, NDC, 128))

    y_out = nc.dram_tensor("y_out", [T_CORE, D], F32, kind="ExternalOutput").ap()
    dbg_written = set()

    def sb(name, shape, dt):
        return nc.alloc_sbuf_tensor(name, list(shape), dt).ap()

    xres = sb("xres", (128, 8, D), F32)
    AR1 = sb("AR1", (128, 23040), U8)
    AR2 = sb("AR2", (128, 40960), U8)
    AR3 = sb("AR3", (128, 25088), U8)
    VT = sb("VT", (128, NEC, 512), BF16)
    wring = sb("wring", (128, 12288), U8)
    trans_t = sb("trans_t", (128, 2, 1024), BF16)
    Emat = sb("Emat", (128, NEXP), BF16)
    conv_tmp = sb("conv_tmp", (128, 1024), F32)
    sig_tmp = sb("sig_tmp", (128, 1024), BF16)
    dump16 = sb("dump16", (128, 1024), BF16)
    BuE = sb("BuE", (128, 1056), BF16)
    Bup = conv_tmp
    states = sb("states", (128, 1056), F32)
    states_bf = conv_tmp.bitcast(BF16)[:, 0:1056]
    halo_s = sb("halo_s", (128, NIC, 2), BF16)

    ident_bf = sb("ident_bf", (128, 128), BF16)
    ident_f32 = sb("ident_f32", (128, 128), F32)
    keysbd_s = sb("keysbd_s", (128, NH, 128), BF16)
    lnw_s = sb("lnw_s", (128, D), BF16)
    lnb_s = sb("lnb_s", (128, D), BF16)
    conv_s = sb("conv_s", (128, NIC, KCONV), F32)
    BmT_s = sb("BmT_s", (128, NIC, S), BF16)
    CmT_s = sb("CmT_s", (128, INNER), BF16)
    Pall = sb("Pall", (128, (R_SC + 1) * S), F32)
    Pall_bf = sb("Pall_bf", (128, (R_SC + 1) * S), BF16)
    Mtmp = sb("Mtmp", (128, 3 * S), F32)
    bgbu_s = sb("bgbu_s", (128, 2 * NKC), F32)
    bd_s = sb("bd_s", (128, NDC), F32)
    ones_s = sb("ones_s", (128, 64), F32)
    selh_s = sb("selh_s", (128, 512), F32)
    statb = sb("statb", (128, 256), F32)

    def carve(arena, byte_off, shape, dt):
        nbytes = int(np.prod(shape[1:])) * mybir.dt.size(dt)
        assert byte_off % 32 == 0
        v = arena[:, byte_off:byte_off + nbytes].bitcast(dt)
        if len(shape) == 2:
            return v
        if len(shape) == 3:
            return v.rearrange("p (a b) -> p a b", a=shape[1])
        if len(shape) == 4:
            return v.rearrange("p (a b c) -> p a b c", a=shape[1], b=shape[2])
        if len(shape) == 5:
            return v.rearrange("p (a b c d) -> p a b c d", a=shape[1],
                               b=shape[2], c=shape[3])
        raise ValueError(shape)

    h_fm = carve(AR1, 0, (128, NDC, 1024), BF16)
    mask8 = carve(AR1, 0, (128, 512), F32)
    silu_buf = carve(AR1, 0, (128, 4, 1024), BF16)
    simkm_bf = carve(AR1, 8192, (128, 2, NH, 512), BF16)
    simtm_t = carve(AR1, 16384, (128, 512), F32)
    thrRows = simtm_t
    stage = carve(AR1, 18432, (128, 64), F32)
    thrTM = carve(AR1, 18688, (128, 128), F32)
    nvz = carve(AR1, 19200, (128, 2, 4, 4, 4), F32)
    rall = carve(AR1, 19712, (128, 64), F32)
    sxbf = carve(AR1, 19968, (128, 64), BF16)
    candg4 = carve(AR1, 20096, (128, NH, 64), F32)
    exgrid4 = carve(AR1, 21120, (128, NH, 64), F32)
    v8tmp = carve(AR1, 22144, (128, 32), F32)
    lnzb = carve(AR1, 22272, (128, 32), F32)
    thrKM = carve(AR1, 2048, (128, 512), F32)
    stt_out = carve(AR1, 4096, (128, 2048), BF16)

    xattn = carve(AR2, 0, (128, 8, D), BF16)
    q_fm = carve(AR2, 0, (128, NQC, 1024), BF16)

    sgate = carve(AR2, 16384, (128, NIC, 1024), BF16)
    wmask_s = carve(AR2, 16384, (128, W_WARM), F32)
    acc_j = [carve(AR2, j * 8192, (128, NEXP), BF16) for j in range(4)]
    m_fm = carve(AR2, 0, (128, NKC, 512), BF16)
    tmp_stt = carve(AR2, 32768, (128, 2048), BF16)
    tmp2_stt = carve(AR2, 36864, (128, 2048), BF16)
    ffnout = carve(AR2, 32768, (128, NDC, 512), BF16)

    data_u = carve(AR3, 0, (128, NIC, 1040), BF16)
    blockout = carve(AR3, 0, (128, NDC, 1024), BF16)
    h1_fm = carve(AR3, 0, (128, NDC, 1024), BF16)
    out_stage = carve(AR3, 16384, (128, 2, D), F32)

    rem = nc.sbuf_bytes_remaining
    assert rem >= 0, f"SBUF over budget by {-rem} bytes"

    PS = nc.alloc_psum_tensor("PS", [128, 4096], F32).ap()
    PSb = PS.bitcast(BF16)

    def tps(q):
        return PSb[:, 6 * 1024 + q * 128:6 * 1024 + (q + 1) * 128]

    def tps_grp():
        return PSb[:, 6 * 1024:6 * 1024 + 512]

    def warm_pe(waits):
        sch.st("tensor", lambda e: e.matmul(
            PS[:, 5 * 512:5 * 512 + 128], ident_bf, ident_bf,
            start=True, stop=True), waits=waits)

    _stack = contextlib.ExitStack()
    chans = {}
    for name in ("c_x", "c_w", "c_te", "c_ve", "c_se", "c_ge", "c_out",
                 "c_wc", "c_dbg"):
        sem = _stack.enter_context(nc.semaphore(name))
        chans[name] = Chan(sem, name)
    c_x, c_w, c_te, c_ve, c_se, c_ge, c_out, c_wc, c_dbg = (
        chans[k] for k in
        ("c_x", "c_w", "c_te", "c_ve", "c_se", "c_ge", "c_out", "c_wc",
         "c_dbg"))

    sch = Sched()
    wslots = [carve(wring, i * 2048, (128, 1024), BF16) for i in range(6)]
    slot_chans = []
    for i in range(6):
        sem = _stack.enter_context(nc.semaphore(f"c_ws{i}"))
        slot_chans.append(Chan(sem, f"c_ws{i}"))
    ring = Ring(sch, wslots, slot_chans, c_wc)

    dbg = {}

    def dbg_store(name, src_ap, waits=()):
        if name not in dbg_names or name in dbg_written:
            return
        dbg_written.add(name)
        shp = list(src_ap.shape)
        t = nc.dram_tensor("dbg_" + name, shp, src_ap.dtype,
                           kind="ExternalOutput").ap()
        dbg[name] = True
        sch.st("sync", lambda e, t=t, s=src_ap: e.dma_start(out=t, in_=s),
               waits=waits, inc=c_dbg, inc_amt=16)

    def ld(dst, src, slow=False):
        def f(e, d=dst, s=src, slow=slow):
            if slow:
                return e.dma_start(out=d, in_=s, allow_slow_non_contiguous=True)
            return e.dma_start(out=d, in_=s)
        sch.st("sync", f, inc=c_x, inc_amt=16)

    def ldc(dst, src, slow=False):
        def f(e, d=dst, s=src, slow=slow):
            if slow:
                return e.dma_start(out=d, in_=s, allow_slow_non_contiguous=True)
            return e.dma_start(out=d, in_=s)
        sch.st("gpsimd", f, inc=c_w, inc_amt=16)

    ld(conv_s, conv_sb_d)
    ld(wmask_s, warm_mask)
    ld(bgbu_s[:, 0:NKC], bg_d)
    ld(bgbu_s[:, NKC:2 * NKC], bu_d)
    ld(bd_s, bd_d)
    ld(Mtmp[0:S, 0:S], AT_d)
    t_loads = c_x.tok()

    ldc(lnw_s, lnw_rep)
    ldc(lnb_s, lnb_rep)
    ldc(keysbd_s, keysbd_d.rearrange("a p b -> p a b"), slow=True)
    ldc(BmT_s, BmT_d.rearrange("a p b -> p a b"), slow=True)
    ldc(CmT_s[0:S, :], CmT_d)
    t_cast_pre = c_w.tok()

    def g_ident(e):
        e.memset(ident_bf, 0.0)
        e.affine_select(out=ident_bf, in_=ident_bf, compare_op=OP.not_equal,
                        fill=1.0, base=0, pattern=[[-1, 128]],
                        channel_multiplier=1)
        e.memset(ident_f32, 0.0)
        e.affine_select(out=ident_f32, in_=ident_f32, compare_op=OP.not_equal,
                        fill=1.0, base=0, pattern=[[-1, 128]],
                        channel_multiplier=1)
        e.memset(Emat, 0.0)
        ev = Emat.rearrange("p (a b) -> p a b", a=64)
        e.affine_select(out=ev[0:64], in_=ev[0:64], compare_op=OP.not_equal,
                        fill=1.0, base=0, pattern=[[1, 64], [0, 64]],
                        channel_multiplier=-1)
        e.affine_select(out=ev[64:128], in_=ev[64:128], compare_op=OP.not_equal,
                        fill=1.0, base=0, pattern=[[0, 64], [1, 64]],
                        channel_multiplier=-1)
        e.memset(ones_s, 1.0)
        e.memset(selh_s, 0.0)
        sv = selh_s.rearrange("p (a b c) -> p a b c", a=NH, b=2)
        e.affine_select(out=sv[0:8], in_=sv[0:8], compare_op=OP.not_equal,
                        fill=1.0, base=0, pattern=[[-2, NH], [-1, 2], [0, 64]],
                        channel_multiplier=1)
        e.memset(BuE[0:S, :], 0.0)
        e.memset(states[0:S, 0:R_SC], 0.0)
        return e.memset(halo_s.rearrange("p a b -> p (a b)"), 0.0)

    sch.st("gpsimd", g_ident, inc=c_ge)
    t_ident = c_ge.tok()

    sch.st("vector", lambda e: e.tensor_add(
        Mtmp[0:S, S:2 * S], Mtmp[0:S, 0:S], ident_f32[0:S, 0:S]),
        waits=[t_loads, t_ident], inc=c_ve)
    t_m = c_ve.tok()
    sch.st("tensor", lambda e: e.transpose(
        PS[0:S, 0:S], Mtmp[0:S, S:2 * S], ident_f32[0:S, 0:S]),
        waits=[t_m], inc=c_te)
    t_mt = c_te.tok()
    sch.st("vector", lambda e: e.tensor_copy(Mtmp[0:S, 2 * S:3 * S],
                                             PS[0:S, 0:S]),
           waits=[t_mt], inc=c_ve)
    t_mtc = c_ve.tok()
    sch.st("vector", lambda e: e.tensor_copy(Pall[0:S, 0:S],
                                             ident_f32[0:S, 0:S]),
           waits=[t_ident], inc=c_ve)
    sch.st("vector", lambda e: e.tensor_copy(Pall[0:S, S:2 * S],
                                             Mtmp[0:S, S:2 * S]),
           waits=[t_m], inc=c_ve)
    tv = c_ve.tok()
    for k in range(2, R_SC + 1):
        sch.st("tensor", lambda e, k=k: e.matmul(
            PS[0:S, 512:512 + S], Mtmp[0:S, 2 * S:3 * S],
            Pall[0:S, (k - 1) * S:k * S], start=True, stop=True),
            waits=[t_mtc, tv], inc=c_te)
        tk = c_te.tok()
        sch.st("vector", lambda e, k=k: e.tensor_copy(
            Pall[0:S, k * S:(k + 1) * S], PS[0:S, 512:512 + S]),
            waits=[tk], inc=c_ve)
        tv = c_ve.tok()
    sch.st("vector", lambda e: e.tensor_copy(Pall_bf[0:S, :], Pall[0:S, :]),
           waits=[tv], inc=c_ve)
    t_pre = c_ve.tok()
    dbg_store("lnw", lnw_s, waits=[t_cast_pre])
    dbg_store("keysbd", keysbd_s.rearrange("p a b -> p (a b)"),
              waits=[t_cast_pre])
    dbg_store("BmTs", BmT_s, waits=[t_cast_pre])
    dbg_store("Pall", Pall[0:S, :], waits=[t_pre])

    def ln_stats_finish(c0, nt, waits):
        sums = statb[:, c0:c0 + nt]
        sumsq = statb[:, c0 + 8:c0 + 8 + nt]
        mu = statb[:, c0 + 16:c0 + 16 + nt]
        inv = statb[:, c0 + 24:c0 + 24 + nt]
        vs = statb[:, c0 + 32:c0 + 32 + nt]
        yb = statb[:, c0 + 40:c0 + 40 + nt]
        tb = statb[:, c0 + 48:c0 + 48 + nt]

        spc_a = statb[:, 240:248]
        spc_b = statb[:, 248:256]

        def f1(e):
            def spc():
                e.tensor_copy(spc_b, spc_a)
            e.tensor_scalar(mu, sums, 1.0 / D, None, OP.mult)
            e.tensor_scalar(vs, sumsq, 1.0 / D, None, OP.mult)
            e.tensor_tensor(tb, mu, mu, OP.mult)
            spc()
            e.tensor_sub(vs, vs, tb)
            spc()
            return e.tensor_scalar(vs, vs, float(LN_EPS), None, OP.add)

        sch.st("vector", f1, waits=waits, inc=c_ve)
        tv1 = c_ve.tok()

        def fs(e):
            e.activation(yb, vs, AF.Ln)
            e.activation(dump16[:, 0:64], dump16[:, 64:128], AF.Square)
            return e.activation(tb, yb, AF.Exp, scale=-0.5)

        sch.st("scalar", fs, waits=[tv1], inc=c_se)
        ts1 = c_se.tok()

        def f2(e):
            def spc():
                e.tensor_copy(spc_b, spc_a)
            e.tensor_tensor(yb, tb, tb, OP.mult)
            spc()
            e.tensor_tensor(yb, yb, vs, OP.mult)
            spc()
            e.tensor_scalar(yb, yb, -0.5, 1.5, OP.mult, OP.add)
            spc()
            e.tensor_tensor(tb, tb, yb, OP.mult)
            spc()
            e.tensor_copy(inv, tb)
            return e.tensor_copy(spc_b, spc_a)

        return sch.st("vector", f2, waits=[ts1], inc=c_ve)

    def mu_col(c0, j):
        return statb[:, c0 + 16 + j:c0 + 17 + j]

    def inv_col(c0, j):
        return statb[:, c0 + 24 + j:c0 + 25 + j]

    st_base = 0
    xres_waits = [c_out.tok()]

    for st_i, T in enumerate(st_sizes):
        NT = T // 128
        NTH = T // 512
        warm_only = (st_i == 0)
        sfx = f"_st{st_i}"

        st_toks = []
        xs = x_in[st_base:st_base + T].rearrange("(a p) d -> p a d", p=128)
        tok_x = sch.st("sync", lambda e, xs=xs, NT=NT: e.dma_start(
            out=xres[:, 0:NT, :], in_=xs), waits=xres_waits,
            inc=c_x, inc_amt=16)

        c0a = 0
        for j in range(NT):
            sch.st("vector", lambda e, j=j: e.tensor_scalar(
                dump16, xres[:, j, :], 1.0, 0.0, OP.mult, OP.add,
                accum_out=statb[:, c0a + j:c0a + j + 1]),
                waits=[tok_x, t_pre], inc=c_ve)
            sch.st("scalar", lambda e, j=j: e.activation(
                dump16, xres[:, j, :], AF.Square,
                accum_out=statb[:, c0a + 8 + j:c0a + 9 + j]),
                waits=[tok_x], inc=c_se)
        t_stats1 = ln_stats_finish(c0a, NT, [c_ve.tok(), c_se.tok()])
        warm_pe([c_ve.tok()])

        c0b = 64
        t_hfm_tiles = []
        for j in range(NT):
            sch.st("vector", lambda e, j=j: e.tensor_scalar(
                xattn[:, j, :], xres[:, j, :], mu_col(c0a, j), inv_col(c0a, j),
                OP.subtract, OP.mult,
                accum_out=statb[:, c0b + j:c0b + j + 1]),
                waits=[t_stats1], inc=c_ve)
            tv_ = c_ve.tok()
            sch.st("scalar", lambda e, j=j: e.activation(
                dump16, xattn[:, j, :], AF.Square,
                accum_out=statb[:, c0b + 8 + j:c0b + 9 + j]),
                waits=[tv_], inc=c_se)
        t_stats2 = ln_stats_finish(c0b, NT, [c_ve.tok(), c_se.tok()])
        warm_pe([c_ve.tok()])
        dbg_store("xattn" + sfx, xattn, waits=[c_ve.tok()])
        dbg_store("statb1" + sfx, statb, waits=[t_stats1])
        dbg_store("statb" + sfx, statb, waits=[t_stats2])

        t_hfm_done = None
        for j in range(NT):
            tslot = trans_t[:, j % 2, :]
            prev_evac = c_se.tok()
            sch.st("vector", lambda e, j=j, tslot=tslot: (
                e.tensor_scalar(tslot, xattn[:, j, :], mu_col(c0b, j),
                                inv_col(c0b, j), OP.subtract, OP.mult),
                e.tensor_tensor(tslot, tslot, lnw_s, OP.mult),
                e.tensor_tensor(tslot, tslot, lnb_s, OP.add))[-1],
                waits=[t_stats2, prev_evac, t_cast_pre], inc=c_ve)
            t_h = c_ve.tok()
            for g in range(2):
                def t_tr(e, g=g, tslot=tslot):
                    r = None
                    for q in range(4):
                        dc = g * 4 + q
                        r = e.transpose(
                            tps(q), tslot[:, dc * 128:(dc + 1) * 128], ident_bf)
                    return r
                sch.st("tensor", t_tr, waits=[t_h, c_se.tok()], inc=c_te)
                t_tp = c_te.tok()
                sch.st("scalar", lambda e, j=j, g=g: e.activation(
                    h_fm[:, g * 4:(g + 1) * 4, j * 128:(j + 1) * 128],
                    tps_grp().rearrange("p (a b) -> p a b", a=4), AF.Identity),
                    waits=[t_tp], inc=c_se)
            if j == 0:
                dbg_store("htm" + sfx, trans_t[:, 0, :], waits=[t_h])
            t_hfm_done = c_se.tok()
            t_hfm_tiles.append(t_hfm_done)
            warm_pe([c_ve.tok()])
        dbg_store("h_fm" + sfx, h_fm, waits=[t_hfm_done])

        b_done = []
        t_gate_sig = None
        n_ic = NIC if warm_only else NPROJ
        for ic in range(n_ic):
            pid = ring.load(lambda slot, ic=ic: (
                lambda e: e.dma_start(
                    out=slot, in_=W_in_r[ic].rearrange("p a b -> p (a b)"))))
            wslot = ring.slot(pid)
            ltok = ring.wait_tok(pid)
            pb = (ic % 2) * 2
            wprev = b_done[-2] if len(b_done) >= 2 else (c_ve, 0)

            for th in range(NTH):
                def t_mm(e, wslot=wslot, pb=pb, th=th):
                    r = None
                    for dc in range(NDC):
                        r = e.matmul(
                            PS[:, (pb + th) * 512:(pb + th + 1) * 512],
                            wslot[:, dc * 128:(dc + 1) * 128],
                            h_fm[:, dc, th * 512:(th + 1) * 512],
                            start=(dc == 0), stop=(dc == NDC - 1))
                    return r
                hw = t_hfm_tiles[min(th * 4 + 3, len(t_hfm_tiles) - 1)]
                sch.st("tensor", t_mm, waits=[ltok, hw, wprev], inc=c_wc)
            t_mmd = c_wc.tok()
            ring.consumed(pid, t_mmd)
            if ic < NIC:
                sch.st("vector", lambda e, ic=ic, pb=pb, T=T: e.tensor_copy(
                    data_u[:, ic, 2:2 + T], PS[:, pb * 512:pb * 512 + T]),
                    waits=[t_mmd], inc=c_ve)
                b_done.append(c_ve.tok())
                dtok = c_ve.tok()
                if warm_only:
                    sch.st("vector", lambda e, ic=ic: e.tensor_tensor(
                        data_u[:, ic, 2:2 + W_WARM],
                        data_u[:, ic, 2:2 + W_WARM],
                        wmask_s, OP.mult), waits=[dtok], inc=c_ve)
                    dtok = c_ve.tok()
                w0 = conv_s[:, ic, 0:1]
                w1 = conv_s[:, ic, 1:2]
                w2 = conv_s[:, ic, 2:3]

                def v_conv(e, ic=ic, w0=w0, w1=w1, w2=w2, T=T):
                    dd = data_u[:, ic, :]
                    e.tensor_copy(dd[:, 0:2], halo_s[:, ic, :])
                    e.tensor_copy(statb[:, 248:256], statb[:, 240:248])
                    e.tensor_scalar(conv_tmp[:, 0:T], dd[:, 0:T], w0, None,
                                    OP.mult)
                    e.scalar_tensor_tensor(conv_tmp[:, 0:T], dd[:, 1:1 + T],
                                           w1, conv_tmp[:, 0:T],
                                           OP.mult, OP.add)
                    e.scalar_tensor_tensor(conv_tmp[:, 0:T], dd[:, 2:2 + T],
                                           w2, conv_tmp[:, 0:T],
                                           OP.mult, OP.add)
                    return e.tensor_copy(halo_s[:, ic, :], dd[:, T:T + 2])
                sch.st("vector", v_conv, waits=[dtok], inc=c_ve)
                t_c = c_ve.tok()
                sch.st("scalar", lambda e, T=T: e.activation(
                    sig_tmp[:, 0:T], conv_tmp[:, 0:T], AF.Sigmoid),
                    waits=[t_c], inc=c_se)
                t_s = c_se.tok()
                sch.st("vector", lambda e, ic=ic, T=T: e.tensor_tensor(
                    data_u[:, ic, 2:2 + T], conv_tmp[:, 0:T], sig_tmp[:, 0:T],
                    OP.mult), waits=[t_s], inc=c_ve)
                t_u_last = c_ve.tok()
            else:
                sch.st("scalar", lambda e, ic=ic, pb=pb, T=T: e.activation(
                    sgate[:, ic - NIC, 0:T], PS[:, pb * 512:pb * 512 + T],
                    AF.Sigmoid), waits=[t_mmd], inc=c_se)
                b_done.append(c_se.tok())
                t_gate_sig = c_se.tok()
        dbg_store("u_fm" + sfx, data_u, waits=[t_u_last])
        if stop_phase == "B":
            st_base += T
            xres_waits = [c_out.tok(), c_ve.tok(), c_se.tok()]
            continue

        def t_bu(e, T=T, NTH2=(T + 511) // 512):
            r = None
            for th in range(NTH2):
                n = min(512, T - th * 512)
                for ic in range(NIC):
                    r = e.matmul(PS[0:S, (4 + th) * 512:(4 + th) * 512 + n],
                                 BmT_s[:, ic, :],
                                 data_u[:, ic, 2 + th * 512:2 + th * 512 + n],
                                 start=(ic == 0), stop=(ic == NIC - 1))
            return r
        sch.st("tensor", t_bu, waits=[t_u_last], inc=c_te)
        t_bu_t = c_te.tok()
        sch.st("vector", lambda e, T=T: e.tensor_copy(
            BuE[0:S, 31:31 + T], PS[0:S, 4 * 512:4 * 512 + T]),
            waits=[t_bu_t], inc=c_ve)
        t_bue_t = c_ve.tok()

        def t_taps(e, T=T):
            r = None
            for th in range((T + 511) // 512):
                n = min(512, T - th * 512)
                for k in range(R_SC):
                    r = e.matmul(PS[0:S, (4 + th) * 512:(4 + th) * 512 + n],
                                 Pall_bf[0:S, k * S:(k + 1) * S],
                                 BuE[0:S, 31 - k + th * 512:31 - k + th * 512 + n],
                                 start=(k == 0), stop=(k == R_SC - 1))
            return r
        dbg_store("BuE" + sfx, BuE[0:S, :], waits=[t_bue_t])
        sch.st("tensor", t_taps, waits=[t_bue_t], inc=c_te)
        t_taps_t = c_te.tok()
        sch.st("vector", lambda e, T=T: (
            e.tensor_copy(Bup[0:S, 0:T], PS[0:S, 4 * 512:4 * 512 + T]),
            e.tensor_copy(statb[:, 248:256], statb[:, 240:248]))[-1],
            waits=[t_taps_t], inc=c_ve)
        tv_prev = c_ve.tok()

        dbg_store("Bup" + sfx, Bup[0:S, :], waits=[tv_prev])
        for jj in range(T // R_SC):
            sch.st("tensor", lambda e, jj=jj: e.matmul(
                PS[0:S, 7 * 512:7 * 512 + R_SC],
                Pall[0:S, R_SC * S:(R_SC + 1) * S],
                states[0:S, jj * R_SC:(jj + 1) * R_SC],
                start=True, stop=True), waits=[tv_prev], inc=c_te)
            tt = c_te.tok()
            sch.st("vector", lambda e, jj=jj: e.tensor_add(
                states[0:S, R_SC + jj * R_SC:R_SC + (jj + 1) * R_SC],
                PS[0:S, 7 * 512:7 * 512 + R_SC],
                Bup[0:S, jj * R_SC:(jj + 1) * R_SC]),
                waits=[tt], inc=c_ve)
            tv_prev = c_ve.tok()

        sch.st("vector", lambda e, T=T: (
            e.tensor_copy(states_bf[0:S, R_SC:R_SC + T],
                          states[0:S, R_SC:R_SC + T]),
            e.tensor_copy(states[0:S, 0:R_SC], states[0:S, T:T + R_SC]))[-1],
            waits=[tv_prev], inc=c_ve)
        t_scan = c_ve.tok()
        dbg_store("states" + sfx, states[0:S, :], waits=[t_scan])
        if stop_phase == "scan":
            st_base += T
            xres_waits = [c_out.tok(), c_ve.tok(), c_se.tok()]
            continue

        if not warm_only:
            t_in2 = []
            for ic in range(NIC):
                pb = (ic % 2) * 2
                wprev = t_in2[-2] if len(t_in2) >= 2 else (c_ve, 0)

                def t_y(e, ic=ic, pb=pb, NTH=NTH):
                    r = None
                    for th in range(NTH):
                        r = e.matmul(
                            PS[:, (pb + th) * 512:(pb + th + 1) * 512],
                            CmT_s[0:S, ic * 128:(ic + 1) * 128],
                            states_bf[0:S, R_SC + th * 512:R_SC + (th + 1) * 512],
                            start=True, stop=True)
                    return r
                sch.st("tensor", t_y, waits=[t_scan, t_gate_sig, wprev],
                       inc=c_te)
                t_yt = c_te.tok()
                sch.st("vector", lambda e, ic=ic, pb=pb, T=T: e.tensor_tensor(
                    sgate[:, ic, 0:T], PS[:, pb * 512:pb * 512 + T],
                    sgate[:, ic, 0:T], OP.mult), waits=[t_yt], inc=c_ve)
                t_in2.append(c_ve.tok())

            t_ao = []
            for dc in range(NDC):
                pids = []
                for hp in range(2):
                    pid = ring.load(lambda slot, dc=dc, hp=hp: (
                        lambda e: e.dma_start(
                            out=slot[:, 0:768].rearrange(
                                "p (a b) -> p a b", a=6),
                            in_=W_out_r[dc, hp * 6:(hp + 1) * 6].rearrange(
                                "a p b -> p a b"), allow_slow_non_contiguous=True)))
                    pids.append(pid)
                ltoks = [ring.wait_tok(p) for p in pids]
                pb = (dc % 2) * 2
                wprev = t_ao[-2] if len(t_ao) >= 2 else (c_ve, 0)

                def t_wo(e, pids=pids, pb=pb, NTH=NTH):
                    r = None
                    for th in range(NTH):
                        for k2 in range(NIC):
                            sl = ring.slot(pids[k2 // 6])
                            r = e.matmul(
                                PS[:, (pb + th) * 512:(pb + th + 1) * 512],
                                sl[:, (k2 % 6) * 128:(k2 % 6 + 1) * 128],
                                sgate[:, k2, th * 512:(th + 1) * 512],
                                start=(k2 == 0), stop=(k2 == NIC - 1))
                    return r
                sch.st("tensor", t_wo, waits=ltoks + [t_in2[-1], wprev],
                       inc=c_wc)
                t_wot = c_wc.tok()
                for pid in pids:
                    ring.consumed(pid, t_wot)
                sch.st("scalar", lambda e, dc=dc, pb=pb, T=T: e.activation(
                    blockout[:, dc, 0:T], PS[:, pb * 512:pb * 512 + T],
                    AF.Identity), waits=[t_wot], inc=c_se)
                t_ao.append(c_se.tok())
            dbg_store("attn_out" + sfx, blockout, waits=[t_ao[-1]])
            if stop_phase == "C":
                st_base += T
                xres_waits = [c_out.tok(), c_ve.tok(), c_se.tok()]
                continue

            c0c = 128
            for j in range(NT):
                for g in range(2):
                    def t_tr(e, j=j, g=g):
                        r = None
                        for q in range(4):
                            dc = g * 4 + q
                            r = e.transpose(
                                tps(q), blockout[:, dc, j * 128:(j + 1) * 128],
                                ident_bf)
                        return r
                    sch.st("tensor", t_tr, waits=[t_ao[g * 4 + 3], c_ve.tok()],
                           inc=c_te)
                    t_tp = c_te.tok()
                    sch.st("vector", lambda e, j=j, g=g: e.tensor_tensor(
                        xattn[:, j, g * 512:(g + 1) * 512], tps_grp(),
                        xattn[:, j, g * 512:(g + 1) * 512], OP.add),
                        waits=[t_tp], inc=c_ve)
                t_in = c_ve.tok()
                sch.st("vector", lambda e, j=j: e.tensor_scalar(
                    dump16, xattn[:, j, :], 1.0, 0.0, OP.mult, OP.add,
                    accum_out=statb[:, c0c + j:c0c + j + 1]),
                    waits=[t_in], inc=c_ve)
                tv_ = c_ve.tok()
                sch.st("scalar", lambda e, j=j: e.activation(
                    dump16, xattn[:, j, :], AF.Square,
                    accum_out=statb[:, c0c + 8 + j:c0c + 9 + j]),
                    waits=[tv_], inc=c_se)
            t_stats3 = ln_stats_finish(c0c, NT, [c_ve.tok(), c_se.tok()])
            warm_pe([c_ve.tok()])

            t_x1 = []
            for j in range(NT):
                sch.st("vector", lambda e, j=j: (
                    e.tensor_scalar(sig_tmp, xattn[:, j, :], mu_col(c0c, j),
                                    inv_col(c0c, j), OP.subtract, OP.mult),
                    e.tensor_tensor(xres[:, j, :], xres[:, j, :], sig_tmp,
                                    OP.add))[-1],
                    waits=[t_stats3], inc=c_ve)
                t_x1.append(c_ve.tok())
            dbg_store("x1" + sfx, xres, waits=[t_x1[-1]])

            c0d = 192
            for j in range(NT):
                sch.st("vector", lambda e, j=j: e.tensor_scalar(
                    dump16, xres[:, j, :], 1.0, 0.0, OP.mult, OP.add,
                    accum_out=statb[:, c0d + j:c0d + j + 1]),
                    waits=[t_x1[j]], inc=c_ve)
                tv_ = c_ve.tok()
                sch.st("scalar", lambda e, j=j: e.activation(
                    dump16, xres[:, j, :], AF.Square,
                    accum_out=statb[:, c0d + 8 + j:c0d + 9 + j]),
                    waits=[tv_], inc=c_se)
            t_stats4 = ln_stats_finish(c0d, NT, [c_ve.tok(), c_se.tok()])
            warm_pe([c_ve.tok()])

            t_h1_tiles = []
            t_h1_done = None
            for j in range(NT):
                tslot = trans_t[:, j % 2, :]
                prev_evac = c_se.tok()
                sch.st("vector", lambda e, j=j, tslot=tslot: e.tensor_scalar(
                    tslot, xres[:, j, :], mu_col(c0d, j), inv_col(c0d, j),
                    OP.subtract, OP.mult), waits=[t_stats4, prev_evac],
                    inc=c_ve)
                t_h = c_ve.tok()
                for g in range(2):
                    def t_tr(e, g=g, tslot=tslot):
                        r = None
                        for q in range(4):
                            dc = g * 4 + q
                            r = e.transpose(
                                tps(q), tslot[:, dc * 128:(dc + 1) * 128],
                                ident_bf)
                        return r
                    sch.st("tensor", t_tr, waits=[t_h, c_se.tok()], inc=c_te)
                    t_tp = c_te.tok()
                    sch.st("scalar", lambda e, j=j, g=g: e.activation(
                        h1_fm[:, g * 4:(g + 1) * 4, j * 128:(j + 1) * 128],
                        tps_grp().rearrange("p (a b) -> p a b", a=4),
                        AF.Identity), waits=[t_tp], inc=c_se)
                t_h1_done = c_se.tok()
                t_h1_tiles.append(t_h1_done)
                warm_pe([c_ve.tok()])
            dbg_store("h1_fm" + sfx, h1_fm, waits=[t_h1_done])

            wq_pids = [ring.load(lambda slot, dp=dp: (
                lambda e: e.dma_start(out=slot, in_=Wq_r[dp])))
                for dp in range(NDC // 2)]
            t_q = []
            for qc in range(NQC):
                pb = (qc % 2) * 2
                wprev = t_q[-2] if len(t_q) >= 2 else (c_ve, 0)

                def t_wq(e, qc=qc, pb=pb, NTH=NTH):
                    r = None
                    for th in range(NTH):
                        for dc in range(NDC):
                            sl = ring.slot(wq_pids[dc // 2])
                            r = e.matmul(
                                PS[:, (pb + th) * 512:(pb + th + 1) * 512],
                                sl[:, (dc % 2) * 512 + qc * 128:
                                   (dc % 2) * 512 + (qc + 1) * 128],
                                h1_fm[:, dc, th * 512:(th + 1) * 512],
                                start=(dc == 0), stop=(dc == NDC - 1))
                    return r
                sch.st("tensor", t_wq,
                       waits=[ring.wait_tok(p) for p in wq_pids] +
                             [t_h1_done, wprev], inc=c_wc)
                t_qt = c_wc.tok()
                sch.st("vector", lambda e, qc=qc, pb=pb, T=T: e.tensor_copy(
                    q_fm[:, qc, 0:T], PS[:, pb * 512:pb * 512 + T]),
                    waits=[t_qt], inc=c_ve)
                t_q.append(c_ve.tok())
            for pid in wq_pids:
                ring.consumed(pid, t_qt)
            dbg_store("q_fm" + sfx, q_fm, waits=[t_q[-1]])

            if with_ffn:
                t_q_done = t_q[-1]
                t_sim_evac = (c_ve, 0)
                for half in range(2):
                    for jj in range(4):
                        if stop_phase == "F1t0" and (half, jj) != (0, 0):
                            continue
                        j = half * 4 + jj

                        def t_sim(e, j=j):
                            r = None
                            for h in range(NH):
                                r = e.matmul(
                                    PS[:, 4 * 512 + h * 128:4 * 512 + (h + 1) * 128],
                                    q_fm[:, h, j * 128:(j + 1) * 128],
                                    keysbd_s[:, h, :],
                                    start=(h == 0), stop=(h == NH - 1))
                            return r
                        sch.st("tensor", t_sim, waits=[t_q_done, t_sim_evac],
                               inc=c_te)
                        t_simt = c_te.tok()

                        def v_stage(e, half=half, jj=jj):
                            e.tensor_copy(simtm_t, PS[:, 4 * 512:5 * 512])
                            for h in range(NH):
                                e.max(out=stage[:, h * 16:h * 16 + 8],
                                      in_=simtm_t[:, h * 128:h * 128 + 64])
                                e.max(out=stage[:, h * 16 + 8:h * 16 + 16],
                                      in_=simtm_t[:, h * 128 + 64:h * 128 + 128])
                            base = half * 32 + jj * 8
                            e.tensor_copy(statb[:, 248:256], statb[:, 240:248])
                            e.tensor_copy(statb[:, 248:256], statb[:, 240:248])
                            e.tensor_copy(sxbf, stage)
                            thv = thrTM[:, base:base + 8].rearrange(
                                "p (a b) -> p a b", a=4)
                            stv = stage.rearrange("p (a b) -> p a b", a=4)
                            e.tensor_copy(thv[:, :, 0:1], stv[:, :, 7:8])
                            e.tensor_copy(thv[:, :, 1:2], stv[:, :, 15:16])
                            return e.tensor_copy(rall, sxbf)
                        sch.st("vector", v_stage, waits=[t_simt], inc=c_ve)
                        t_sim_evac = c_ve.tok()

                        def v_cand(e, half=half, jj=jj):
                            for h in range(NH):
                                for a in range(8):
                                    e.tensor_scalar(
                                        candg4[:, h, a * 8:(a + 1) * 8],
                                        rall[:, h * 16 + 8:h * 16 + 16],
                                        rall[:, h * 16 + a:h * 16 + a + 1],
                                        None, OP.add)
                                e.max(out=v8tmp[:, h * 8:h * 8 + 8],
                                      in_=candg4[:, h, :])
                            e.tensor_copy(statb[:, 248:256], statb[:, 240:248])
                            e.tensor_copy(statb[:, 248:256], statb[:, 240:248])
                            for h in range(NH):
                                e.tensor_copy(nvz[:, half, jj, h, 0:1],
                                              v8tmp[:, h * 8 + 7:h * 8 + 8])
                                e.tensor_scalar(
                                    nvz[:, half, jj, h, 1:2],
                                    v8tmp[:, h * 8:h * 8 + 1],
                                    -1.0, None, OP.mult)
                            return e.tensor_copy(statb[:, 248:256],
                                                 statb[:, 240:248])
                        sch.st("vector", v_cand, waits=[t_sim_evac], inc=c_ve)
                        tvc = c_ve.tok()

                        def s_exp(e, half=half, jj=jj):
                            r = None
                            for h in range(NH):
                                r = e.activation(exgrid4[:, h, :],
                                                 candg4[:, h, :], AF.Exp,
                                                 bias=nvz[:, half, jj, h, 1:2])
                            return r
                        sch.st("scalar", s_exp, waits=[tvc], inc=c_se)
                        tse = c_se.tok()

                        def v_z(e, half=half, jj=jj):
                            for h in range(NH):
                                e.scalar_tensor_tensor(
                                    exgrid4[:, h, :], candg4[:, h, :],
                                    nvz[:, half, jj, h, 0:1],
                                    exgrid4[:, h, :], OP.is_ge, OP.mult)
                            for h in range(NH):
                                e.reduce_sum(nvz[:, half, jj, h, 2:3],
                                             exgrid4[:, h, :],
                                             axis=mybir.AxisListType.X)
                            return e.tensor_copy(statb[:, 248:256],
                                                 statb[:, 240:248])
                        sch.st("vector", v_z, waits=[tse], inc=c_ve)
                        if stop_phase == "F1t0":
                            dbg_store("f1_simtm" + sfx, simtm_t,
                                      waits=[c_ve.tok()])
                            dbg_store("f1_stage" + sfx, stage,
                                      waits=[c_ve.tok()])
                            dbg_store("f1_rall" + sfx, rall,
                                      waits=[c_ve.tok()])
                            dbg_store("f1_candg" + sfx,
                                      candg4.rearrange("p a b -> p (a b)"),
                                      waits=[c_ve.tok()])
                            dbg_store("f1_exg" + sfx,
                                      exgrid4.rearrange("p a b -> p (a b)"),
                                      waits=[c_ve.tok()])
                            dbg_store("f1_nvz" + sfx,
                                      nvz.rearrange("p a b c d -> p (a b c d)"),
                                      waits=[c_ve.tok()])
                            dbg_store("f1_v8t" + sfx, v8tmp,
                                      waits=[c_ve.tok()])

                tvz_all = c_ve.tok()
                sch.st("scalar", lambda e: e.activation(
                    lnzb, nvz[:, :, :, :, 2].rearrange("p a b c -> p (a b c)"),
                    AF.Ln), waits=[tvz_all], inc=c_se)
                tlnz = c_se.tok()
                sch.st("vector", lambda e: e.tensor_sub(
                    nvz[:, :, :, :, 3].rearrange("p a b c -> p (a b c)"),
                    nvz[:, :, :, :, 1].rearrange("p a b c -> p (a b c)"),
                    lnzb), waits=[tlnz], inc=c_ve)
                t_f1_done = c_ve.tok()
                dbg_store("nvz" + sfx, nvz.rearrange("p a b c d -> p (a b c d)"),
                          waits=[t_f1_done])

                t_f0_done = []
                for half in range(2):
                    def t_thr(e, half=half):
                        r = None
                        for jj in range(4):
                            r = e.transpose(
                                PS[0:8, 5 * 512 + jj * 128:5 * 512 + (jj + 1) * 128],
                                thrTM[:, half * 32 + jj * 8:half * 32 + (jj + 1) * 8],
                                ident_f32)
                        return r
                    sch.st("tensor", t_thr, waits=[t_f1_done], inc=c_te)
                    t_thrt = c_te.tok()
                    sch.st("vector", lambda e: e.tensor_copy(
                        thrRows[0:8, :], PS[0:8, 5 * 512:6 * 512]),
                        waits=[t_thrt], inc=c_ve)
                    t_thre = c_ve.tok()

                    for h in range(NH):
                        b0 = (h % 2) * 2
                        b1 = b0 + 1

                        def t_km(e, half=half, h=h, b0=b0, b1=b1):
                            e.matmul(PS[:, b0 * 512:(b0 + 1) * 512],
                                     keysbd_s[:, h, :],
                                     q_fm[:, h, half * 512:(half + 1) * 512],
                                     start=True, stop=True)
                            return e.matmul(
                                PS[:, b1 * 512:(b1 + 1) * 512],
                                selh_s[0:8, h * 128:(h + 1) * 128],
                                thrRows[0:8, :], start=True, stop=True)
                        wprev = t_f0_done[-2] if len(t_f0_done) >= 2 else (c_ve, 0)
                        sch.st("tensor", t_km, waits=[t_thre, wprev], inc=c_te)
                        t_kmt = c_te.tok()

                        def v_km(e, half=half, h=h, b0=b0, b1=b1):
                            e.tensor_copy(thrKM, PS[:, b1 * 512:(b1 + 1) * 512])
                            e.tensor_tensor(mask8, PS[:, b0 * 512:(b0 + 1) * 512],
                                            thrKM, OP.is_ge)
                            e.tensor_scalar(mask8, mask8, 1.0e30, -1.0e30,
                                            OP.mult, OP.add)
                            return e.tensor_tensor(
                                simkm_bf[:, half, h, :], mask8,
                                PS[:, b0 * 512:(b0 + 1) * 512], OP.add)
                        sch.st("vector", v_km, waits=[t_kmt], inc=c_ve)
                        t_f0_done.append(c_ve.tok())
                t_f0_tok = t_f0_done[-1]

                dbg_store("simkm" + sfx,
                          simkm_bf.rearrange("p a b c -> p (a b c)"),
                          waits=[t_f0_tok])

                t_J_evac_prev = (c_se, 0)
                for half in range(2):
                    if stop_phase in ("F1", "F0", "F1t0"):
                        break
                    unit = 0
                    t_stt_done = {}
                    t_mm_done = {}
                    last_add = (c_ve, 0)
                    for jj in range(4):
                        for h in range(NH):
                            for eh2 in range(2):
                                base = (unit % 2) * 2048
                                expd = tmp_stt if unit % 2 == 0 else tmp2_stt
                                wprev = t_stt_done.get(unit - 2, (c_ve, 0))

                                def t_cf(e, half=half, h=h, jj=jj, eh2=eh2,
                                         base=base):
                                    r = None
                                    for es in range(4):
                                        r = e.matmul(
                                            PS[:, base + es * 512:base + (es + 1) * 512],
                                            simkm_bf[:, half, h,
                                                     jj * 128:(jj + 1) * 128],
                                            Emat[:, eh2 * 2048 + es * 512:
                                                 eh2 * 2048 + (es + 1) * 512],
                                            start=True, stop=True)
                                    return r
                                w_cf = [t_f0_tok, wprev]
                                if unit == 0:
                                    w_cf += [c_ve.tok(), c_se.tok(),
                                             c_te.tok(), t_J_evac_prev]
                                sch.st("tensor", t_cf, waits=w_cf, inc=c_te)
                                t_mm_done[unit] = c_te.tok()

                                sch.st("scalar", lambda e, half=half, jj=jj,
                                       h=h, base=base, expd=expd:
                                       e.activation(
                                           expd, PS[:, base:base + 2048],
                                           AF.Exp,
                                           bias=nvz[:, half, jj, h, 3:4]),
                                       waits=[t_mm_done[unit]], inc=c_se)
                                t_exp = c_se.tok()

                                def v_stt(e, half=half, jj=jj, h=h, base=base,
                                          eh2=eh2, expd=expd):
                                    dst = (acc_j[jj][:, eh2 * 2048:(eh2 + 1) * 2048]
                                           if h == 0 else stt_out)
                                    e.scalar_tensor_tensor(
                                        dst, PS[:, base:base + 2048],
                                        nvz[:, half, jj, h, 0:1], expd,
                                        OP.is_ge, OP.mult)
                                    if h > 0:
                                        return e.tensor_add(
                                            acc_j[jj][:, eh2 * 2048:(eh2 + 1) * 2048],
                                            acc_j[jj][:, eh2 * 2048:(eh2 + 1) * 2048],
                                            stt_out)
                                    return e.tensor_copy(statb[:, 248:256],
                                                         statb[:, 240:248])
                                sch.st("vector", v_stt, waits=[t_exp], inc=c_ve)
                                t_stt_done[unit] = c_ve.tok()
                                unit += 1
                    t_f3_done = c_ve.tok()
                    if half == 0:
                        dbg_store("SWacc" + sfx, acc_j[0], waits=[t_f3_done])
                    if stop_phase == "F3":
                        continue

                    t_tr_done = (c_te, 0)
                    t_evV_prev = (c_se, 0)
                    t_vt_prev = (c_ve, 0)
                    for eq in range(4):
                        t_wallt = None
                        for part in range(2):
                            dcs = range(part * 4, part * 4 + 4)
                            pids = [ring.load(
                                lambda slot, eq=eq, dc=dc: (
                                    lambda e: e.dma_start(out=slot,
                                                          in_=DOWNT_r[eq, dc])))
                                for dc in dcs]

                            t_h1_half = t_h1_tiles[half * 4 + 3]

                            def t_wall(e, half=half, pids=pids, part=part):
                                r = None
                                for i, dc in enumerate(
                                        range(part * 4, part * 4 + 4)):
                                    sl = ring.slot(pids[i])
                                    for jj in range(4):
                                        lhs = h1_fm[:, dc,
                                                    half * 512 + jj * 128:
                                                    half * 512 + (jj + 1) * 128]
                                        for eh in range(2):
                                            r = e.matmul(
                                                PS[:, jj * 1024 + eh * 512:
                                                   jj * 1024 + (eh + 1) * 512],
                                                lhs,
                                                sl[:, eh * 512:(eh + 1) * 512],
                                                start=(dc == 0),
                                                stop=(dc == NDC - 1))
                                return r
                            waits = [ring.wait_tok(p) for p in pids]
                            if part == 0:
                                waits += [t_f3_done, t_evV_prev, t_vt_prev,
                                          t_h1_half]
                            sch.st("tensor", t_wall, waits=waits, inc=c_wc)
                            t_wallt = c_wc.tok()
                            for p in pids:
                                ring.consumed(p, t_wallt)

                        t_ev = []
                        for jj in range(4):
                            sch.st("scalar", lambda e, jj=jj: e.activation(
                                silu_buf[:, jj, :],
                                PS[:, jj * 1024:(jj + 1) * 1024], AF.Silu),
                                waits=[t_wallt], inc=c_se)
                            t_ev.append(c_se.tok())
                        t_evV_prev = t_ev[-1]

                        t_vm = []
                        for jj in range(4):
                            sch.st("vector", lambda e, jj=jj, eq=eq:
                                   e.tensor_tensor(
                                       acc_j[jj][:, eq * 1024:(eq + 1) * 1024],
                                       acc_j[jj][:, eq * 1024:(eq + 1) * 1024],
                                       silu_buf[:, jj, :], OP.mult),
                                   waits=[t_ev[jj]], inc=c_ve)
                            t_vm.append(c_ve.tok())

                        for jj in range(4):
                            def t_vt(e, jj=jj, eq=eq):
                                r = None
                                for ec in range(8):
                                    r = e.transpose(
                                        PSb[:, 2 * jj * 1024 + ec * 128:
                                            2 * jj * 1024 + (ec + 1) * 128],
                                        acc_j[jj][:, (eq * 8 + ec) * 128:
                                                  (eq * 8 + ec + 1) * 128],
                                        ident_bf)
                                return r
                            sch.st("tensor", t_vt,
                                   waits=[t_vm[jj], t_ev[jj]], inc=c_te)
                            t_trt = c_te.tok()
                            sch.st("vector", lambda e, jj=jj, eq=eq:
                                   e.tensor_copy(
                                       VT[:, eq * 8:eq * 8 + 8,
                                          jj * 128:(jj + 1) * 128],
                                       PSb[:, 2 * jj * 1024:2 * jj * 1024 + 1024]
                                       .rearrange("p (a b) -> p a b", a=8)),
                                   waits=[t_trt], inc=c_ve)
                        t_vt_prev = c_ve.tok()
                    t_f2_done = c_ve.tok()
                    if half == 0:
                        dbg_store("VT" + sfx,
                                  VT.rearrange("p a b -> p (a b)"),
                                  waits=[t_f2_done])
                    if stop_phase == "F2":
                        continue

                    t_m_toks = []
                    for dcd in range(NKC):
                        pidg = ring.load(lambda slot, dcd=dcd: (
                            lambda e: e.dma_start(
                                out=slot, in_=Wg_r[dcd].rearrange(
                                    "p a b -> p (a b)"))))
                        pidu = ring.load(lambda slot, dcd=dcd: (
                            lambda e: e.dma_start(
                                out=slot, in_=Wu_r[dcd].rearrange(
                                    "p a b -> p (a b)"))))
                        bkA = (dcd % 2) * 2
                        bkB = bkA + 1
                        wpv = (t_m_toks[-2][0] if len(t_m_toks) >= 2
                               else (c_ve, 0))
                        wpv2 = (t_m_toks[-2][1] if len(t_m_toks) >= 2
                                else (c_se, 0))

                        def t_gu(e, half=half, pidg=pidg, pidu=pidu,
                                 bkA=bkA, bkB=bkB):
                            r = None
                            for pid, bk in ((pidg, bkA), (pidu, bkB)):
                                sl = ring.slot(pid)
                                for dc in range(NDC):
                                    r = e.matmul(
                                        PS[:, bk * 512:(bk + 1) * 512],
                                        sl[:, dc * 128:(dc + 1) * 128],
                                        h1_fm[:, dc,
                                              half * 512:(half + 1) * 512],
                                        start=(dc == 0), stop=(dc == NDC - 1))
                            return r
                        sch.st("tensor", t_gu,
                               waits=[ring.wait_tok(pidg), ring.wait_tok(pidu),
                                      t_f2_done, wpv, wpv2,
                                      t_h1_tiles[half * 4 + 3]], inc=c_wc)
                        t_gut = c_wc.tok()
                        ring.consumed(pidg, t_gut)
                        ring.consumed(pidu, t_gut)

                        sh = (dcd % 2) * 512
                        sch.st("scalar", lambda e, dcd=dcd, bkA=bkA, sh=sh:
                               e.activation(sig_tmp[:, sh:sh + 512],
                                            PS[:, bkA * 512:(bkA + 1) * 512],
                                            AF.Silu,
                                            bias=bgbu_s[:, dcd:dcd + 1]),
                               waits=[t_gut], inc=c_se)
                        t_sg = c_se.tok()

                        def v_m(e, dcd=dcd, bkB=bkB, sh=sh):
                            e.tensor_scalar(
                                dump16[:, sh:sh + 512],
                                PS[:, bkB * 512:(bkB + 1) * 512],
                                bgbu_s[:, NKC + dcd:NKC + dcd + 1], None,
                                OP.add)
                            return e.tensor_tensor(
                                m_fm[:, dcd, :], sig_tmp[:, sh:sh + 512],
                                dump16[:, sh:sh + 512], OP.mult)
                        sch.st("vector", v_m, waits=[t_sg], inc=c_ve)
                        t_m_toks.append((c_ve.tok(), t_sg))
                    t_h_done = c_ve.tok()
                    if half == 0:
                        dbg_store("m" + sfx,
                                  m_fm.rearrange("p a b -> p (a b)"),
                                  waits=[t_h_done])
                    if stop_phase == "H":
                        continue

                    t_i_mm = None
                    for k in range(64):
                        if k < NKC:
                            srcd = Wd_r[k]
                        else:
                            srcd = UE_r[k - NKC]
                        pid = ring.load(lambda slot, srcd=srcd: (
                            lambda e: e.dma_start(
                                out=slot,
                                in_=srcd.rearrange("p a b -> p (a b)"))))

                        def t_i(e, k=k, pid=pid, half=half):
                            sl = ring.slot(pid)
                            rhs = (m_fm[:, k, :] if k < NKC
                                   else VT[:, k - NKC, :])
                            r = None
                            for ds in range(NDC):
                                r = e.matmul(
                                    PS[:, ds * 512:(ds + 1) * 512],
                                    sl[:, ds * 128:(ds + 1) * 128], rhs,
                                    start=(k == 0), stop=(k == 63))
                            return r
                        waits = [ring.wait_tok(pid)]
                        if k == 0:
                            waits += [t_h_done, t_f2_done, t_J_evac_prev]
                        sch.st("tensor", t_i, waits=waits, inc=c_wc)
                        t_i_mm = c_wc.tok()
                        ring.consumed(pid, t_i_mm)

                    t_ffn_ev = []
                    for ds in range(NDC):
                        sch.st("scalar", lambda e, ds=ds: e.activation(
                            ffnout[:, ds, :], PS[:, ds * 512:(ds + 1) * 512],
                            AF.Identity, bias=bd_s[:, ds:ds + 1]),
                            waits=[t_i_mm], inc=c_se)
                        t_ffn_ev.append(c_se.tok())
                    t_J_evac_prev = t_ffn_ev[-1]
                    if half == 0:
                        dbg_store("ffnout" + sfx,
                                  ffnout.rearrange("p a b -> p (a b)"),
                                  waits=[t_J_evac_prev])
                    if stop_phase == "I":
                        continue

                    c0J = 0
                    for pair in range(2):
                        tvJ = None
                        tsJ = None
                        for pj in range(2):
                            jj = pair * 2 + pj
                            for g in range(2):
                                def t_trJ(e, jj=jj, g=g):
                                    r = None
                                    for q in range(4):
                                        dc = g * 4 + q
                                        r = e.transpose(
                                            tps(q),
                                            ffnout[:, dc, jj * 128:(jj + 1) * 128],
                                            ident_bf)
                                    return r
                                sch.st("tensor", t_trJ,
                                       waits=[t_ffn_ev[-1], c_ve.tok(),
                                              c_se.tok()], inc=c_te)
                                t_tpJ = c_te.tok()

                                def v_evJ(e, pj=pj, g=g):
                                    return e.tensor_scalar(
                                        trans_t[:, pj, g * 512:(g + 1) * 512],
                                        tps_grp(), 1.0, 0.0, OP.mult, OP.add,
                                        accum_out=statb[:, c0J + 48 + g * 8 + pj:
                                                        c0J + 49 + g * 8 + pj])
                                sch.st("vector", v_evJ, waits=[t_tpJ], inc=c_ve)
                            tvJ_t = c_ve.tok()
                            sch.st("vector", lambda e, pj=pj: e.tensor_add(
                                statb[:, c0J + pj:c0J + pj + 1],
                                statb[:, c0J + 48 + pj:c0J + 49 + pj],
                                statb[:, c0J + 56 + pj:c0J + 57 + pj]),
                                waits=[tvJ_t], inc=c_ve)
                            tvJ = c_ve.tok()
                            sch.st("scalar", lambda e, pj=pj: e.activation(
                                dump16, trans_t[:, pj, :], AF.Square,
                                accum_out=statb[:, c0J + 8 + pj:c0J + 9 + pj]),
                                waits=[tvJ], inc=c_se)
                            tsJ = c_se.tok()
                        t_stJ = ln_stats_finish(c0J, 2, [tvJ, tsJ])

                        for pj in range(2):
                            jj = pair * 2 + pj
                            j = half * 4 + jj
                            wst = (st_toks[-2] if len(st_toks) >= 2
                                   else (c_out, 0))

                            def v_x2(e, pj=pj, j=j):
                                e.tensor_scalar(
                                    trans_t[:, pj, :], trans_t[:, pj, :],
                                    mu_col(c0J, pj), inv_col(c0J, pj),
                                    OP.subtract, OP.mult)
                                return e.tensor_tensor(
                                    out_stage[:, pj, :], xres[:, j, :],
                                    trans_t[:, pj, :], OP.add)
                            sch.st("vector", v_x2, waits=[t_stJ, wst], inc=c_ve)
                            t_x2 = c_ve.tok()
                            row0 = (st_base - W_WARM) + j * 128
                            tok = sch.st("sync", lambda e, pj=pj, row0=row0:
                                         e.dma_start(
                                             out=y_out[row0:row0 + 128, :],
                                             in_=out_stage[:, pj, :]),
                                         waits=[t_x2], inc=c_out, inc_amt=16)
                            st_toks.append(tok)
            else:
                for j in range(NT):
                    sch.st("sync", lambda e, j=j: e.dma_start(
                        out=y_out[(st_base - W_WARM) + j * 128:
                                  (st_base - W_WARM) + (j + 1) * 128, :],
                        in_=xres[:, j, :]), waits=[t_x1[-1]],
                        inc=c_out, inc_amt=16)

        st_base += T
        xres_waits = [c_out.tok(), c_ve.tok(), c_se.tok()]

    sch.st("sync", None, waits=[c_out.tok(), c_dbg.tok()])

    with nc.Block() as block:
        @block.tensor
        def _(e):
            sch.replay("tensor", e)

        @block.vector
        def _(e):
            sch.replay("vector", e)

        @block.scalar
        def _(e):
            sch.replay("scalar", e)

        @block.gpsimd
        def _(e):
            sch.replay("gpsimd", e)

        @block.sync
        def _(e):
            sch.replay("sync", e)

    return nc, sorted(dbg_written)


def emit_ffn_st(ctx):
    raise NotImplementedError


def kernel(**inputs):
    nc, _ = build_nc()
    in_maps = [_stage_core_inputs(inputs, c) for c in range(NCORES)]
    res = run_bass_kernel_spmd(nc, in_maps, core_ids=list(range(NCORES)))
    out = np.empty((BATCH, SEQ, D), np.float32)
    for c in range(NCORES):
        b, half = c // 2, c % 2
        out[b, half * T_CORE:(half + 1) * T_CORE] = res.results[c]["y_out"]
    return out


# version 04 (final)
# speedup vs baseline: 1.0656x; 1.0449x over previous
import numpy as np

import concourse.bass as bass
import concourse.mybir as mybir
from concourse.bass_utils import run_bass_kernel_spmd

F32 = mybir.dt.float32
BF16 = mybir.dt.bfloat16
U8 = mybir.dt.uint8
I32 = mybir.dt.int32
AF = mybir.ActivationFunctionType
OP = mybir.AluOpType

D = 1024
INNER = 1536
S = 16
KCONV = 3
DCD = 4096
NEXP = 4096
NH = 4
SEQ = 8192
BATCH = 4
NCORES = 8

W_WARM = 512
T_CORE = 4096
R_SC = 32
NDC = 8
NIC = 12
NPROJ = 24
NQC = 4
NEC = 32
NKC = 32
ST_SIZES = [W_WARM, 1024, 1024, 1024, 1024]
LN_EPS = 1e-5
NEG_BIG = -1.0e30



def _stage_core_inputs(inp, core):
    b, half = core // 2, core % 2
    x = np.ascontiguousarray(np.asarray(inp["L0_attn_x"], np.float32)[b])
    t0 = half * T_CORE
    if half == 0:
        xw = np.zeros((W_WARM, D), np.float32)
    else:
        xw = x[t0 - W_WARM:t0]
    d = {}
    d["x_in"] = np.ascontiguousarray(np.concatenate([xw, x[t0:t0 + T_CORE]], 0))
    d["warm_mask"] = np.broadcast_to(
        np.float32(0.0 if half == 0 else 1.0), (128, W_WARM)).copy()
    d["lnw_rep"] = np.broadcast_to(
        np.asarray(inp["L0_attn_ln_w"], np.float32)[None, :], (128, D)).copy()
    d["lnb_rep"] = np.broadcast_to(
        np.asarray(inp["L0_attn_ln_b"], np.float32)[None, :], (128, D)).copy()
    W_in = np.asarray(inp["L0_attn_W_in"], np.float32)
    d["W_in_r"] = np.ascontiguousarray(
        W_in.reshape(NDC, 128, NPROJ, 128).transpose(2, 1, 0, 3))
    d["conv_sb"] = np.ascontiguousarray(
        np.asarray(inp["L0_attn_conv_w"], np.float32).reshape(NIC, 128, KCONV)
        .transpose(1, 0, 2))
    d["AT"] = np.ascontiguousarray(np.asarray(inp["L0_attn_A"], np.float32).T)
    d["BmT_r"] = np.ascontiguousarray(
        np.asarray(inp["L0_attn_Bm"], np.float32).T.reshape(NIC, 128, S))
    d["CmT"] = np.ascontiguousarray(np.asarray(inp["L0_attn_Cm"], np.float32).T)
    W_out = np.asarray(inp["L0_attn_W_out"], np.float32)
    d["W_out_r"] = np.ascontiguousarray(
        W_out.reshape(NIC, 128, NDC, 128).transpose(2, 0, 1, 3))
    Wq = np.asarray(inp["L0_ffn_Wq"], np.float32)
    perm = np.array([p * 256 + h * 64 + n
                     for h in range(NH) for p in range(2) for n in range(64)])
    d["Wq_r"] = np.ascontiguousarray(
        Wq[:, perm].reshape(4, 2, 128, 512).transpose(0, 2, 1, 3)
        .reshape(4, 128, 1024))
    keys = np.asarray(inp["L0_ffn_keys"], np.float32)
    kbd = np.zeros((NH, 128, 128), np.float32)
    for h in range(NH):
        for p in range(2):
            kbd[h, p * 64:(p + 1) * 64, p * 64:(p + 1) * 64] = keys[h, :, p, :].T
    d["keysbd"] = kbd
    Wg = np.asarray(inp["L0_ffn_Wg"], np.float32)
    Wu = np.asarray(inp["L0_ffn_Wu"], np.float32)
    d["Wg_r"] = np.ascontiguousarray(
        Wg.reshape(NDC, 128, NKC, 128).transpose(2, 1, 0, 3))
    d["Wu_r"] = np.ascontiguousarray(
        Wu.reshape(NDC, 128, NKC, 128).transpose(2, 1, 0, 3))
    d["bg_sb"] = np.ascontiguousarray(
        np.asarray(inp["L0_ffn_bg"], np.float32).reshape(NKC, 128).T)
    d["bu_sb"] = np.ascontiguousarray(
        np.asarray(inp["L0_ffn_bu"], np.float32).reshape(NKC, 128).T)
    d["bd_sb"] = np.ascontiguousarray(
        np.asarray(inp["L0_ffn_bd"], np.float32).reshape(NDC, 128).T)
    d["Wd_r"] = np.ascontiguousarray(
        np.asarray(inp["L0_ffn_Wd"], np.float32).reshape(NKC, 128, NDC, 128))
    down = np.asarray(inp["L0_ffn_down_embed"], np.float32)
    d["DOWNT_r"] = np.ascontiguousarray(
        down.T.reshape(NDC, 128, 4, 1024).transpose(2, 0, 1, 3))
    d["UE_r"] = np.ascontiguousarray(
        np.asarray(inp["L0_ffn_up_embed"], np.float32).reshape(NEC, 128, NDC, 128))
    return d



class Chan:
    __slots__ = ("sem", "total", "name")

    def __init__(self, sem, name):
        self.sem = sem
        self.total = 0
        self.name = name

    def tok(self):
        return (self, self.total)


class Sched:
    ENGINES = ("tensor", "vector", "scalar", "gpsimd", "sync")

    def __init__(self):
        self.steps = {e: [] for e in self.ENGINES}

    def st(self, eng, fn, waits=(), inc=None, inc_amt=1):
        ws = [(c, v) for (c, v) in waits if v > 0]
        self.steps[eng].append((fn, ws, inc, inc_amt))
        if inc is not None:
            inc.total += inc_amt
            return inc.tok()
        return None

    def replay(self, eng_name, e):
        seen = {}
        for fn, waits, inc, inc_amt in self.steps[eng_name]:
            for chan, val in waits:
                if seen.get(chan.name, -1) >= val:
                    continue
                e.wait_ge(chan.sem, val)
                seen[chan.name] = val
            inst = fn(e) if fn is not None else None
            if inc is not None:
                if inst is not None:
                    inst.then_inc(inc.sem, inc_amt)
                else:
                    e.sem_inc(inc.sem, inc_amt)


class Ring:

    def __init__(self, sched, slots_aps, slot_chans, cons_chan):
        self.sched = sched
        self.slots = slots_aps
        self.nslots = len(slots_aps)
        self.slot_chans = slot_chans
        self.cons_chan = cons_chan
        self.load_toks = []
        self.cons_toks = []

    def load(self, make_dma, extra_waits=()):
        p = len(self.load_toks)
        slot = self.slots[p % self.nslots]
        waits = list(extra_waits)
        if p >= self.nslots:
            waits.append(self.cons_toks[p - self.nslots])
        fn = make_dma(slot)
        tok = self.sched.st("gpsimd", fn, waits=waits,
                            inc=self.slot_chans[p % self.nslots], inc_amt=16)
        self.load_toks.append(tok)
        return p

    def slot(self, p):
        return self.slots[p % self.nslots]

    def wait_tok(self, p):
        return self.load_toks[p]

    def consumed(self, p, tok):
        assert p == len(self.cons_toks), (p, len(self.cons_toks))
        self.cons_toks.append(tok)



def build_nc(dbg_names=(), n_st=None, with_ffn=True, stop_phase=None):
    import contextlib
    nc = bass.Bass("TRN2", target_bir_lowering=False)
    st_sizes = ST_SIZES if n_st is None else ST_SIZES[:n_st]
    dbg_names = set(dbg_names)

    def din(name, shape, dt=F32):
        return nc.dram_tensor(name, list(shape), dt, kind="ExternalInput").ap()

    x_in = din("x_in", (W_WARM + T_CORE, D))
    warm_mask = din("warm_mask", (128, W_WARM))
    lnw_rep = din("lnw_rep", (128, D))
    lnb_rep = din("lnb_rep", (128, D))
    W_in_r = din("W_in_r", (NPROJ, 128, NDC, 128))
    conv_sb_d = din("conv_sb", (128, NIC, KCONV))
    AT_d = din("AT", (S, S))
    BmT_d = din("BmT_r", (NIC, 128, S))
    CmT_d = din("CmT", (S, INNER))
    W_out_r = din("W_out_r", (NDC, NIC, 128, 128))
    Wq_r = din("Wq_r", (4, 128, 1024))
    keysbd_d = din("keysbd", (NH, 128, 128))
    Wg_r = din("Wg_r", (NKC, 128, NDC, 128))
    Wu_r = din("Wu_r", (NKC, 128, NDC, 128))
    bg_d = din("bg_sb", (128, NKC))
    bu_d = din("bu_sb", (128, NKC))
    bd_d = din("bd_sb", (128, NDC))
    Wd_r = din("Wd_r", (NKC, 128, NDC, 128))
    DOWNT_r = din("DOWNT_r", (4, NDC, 128, 1024))
    UE_r = din("UE_r", (NEC, 128, NDC, 128))

    y_out = nc.dram_tensor("y_out", [T_CORE, D], F32, kind="ExternalOutput").ap()
    dbg_written = set()

    def sb(name, shape, dt):
        return nc.alloc_sbuf_tensor(name, list(shape), dt).ap()

    xres = sb("xres", (128, 8, D), F32)
    AR1 = sb("AR1", (128, 23040), U8)
    AR2 = sb("AR2", (128, 40960), U8)
    AR3 = sb("AR3", (128, 25088), U8)
    VT = sb("VT", (128, NEC, 512), BF16)
    wring = sb("wring", (128, 12288), U8)
    trans_t = sb("trans_t", (128, 2, 1024), BF16)
    Emat = sb("Emat", (128, NEXP), BF16)
    conv_tmp = sb("conv_tmp", (128, 1024), F32)
    sig_tmp = sb("sig_tmp", (128, 1024), BF16)
    dump16 = sb("dump16", (128, 1024), BF16)
    BuE = sb("BuE", (128, 1056), BF16)
    Bup = conv_tmp
    states = sb("states", (128, 1056), F32)
    states_bf = conv_tmp.bitcast(BF16)[:, 0:1056]
    halo_s = sb("halo_s", (128, NIC, 2), BF16)

    ident_bf = sb("ident_bf", (128, 128), BF16)
    ident_f32 = sb("ident_f32", (128, 128), F32)
    keysbd_s = sb("keysbd_s", (128, NH, 128), BF16)
    lnw_s = sb("lnw_s", (128, D), BF16)
    lnb_s = sb("lnb_s", (128, D), BF16)
    conv_s = sb("conv_s", (128, NIC, KCONV), F32)
    BmT_s = sb("BmT_s", (128, NIC, S), BF16)
    CmT_s = sb("CmT_s", (128, INNER), BF16)
    Pall = sb("Pall", (128, (R_SC + 1) * S), F32)
    Pall_bf = sb("Pall_bf", (128, (R_SC + 1) * S), BF16)
    Mtmp = sb("Mtmp", (128, 3 * S), F32)
    bgbu_s = sb("bgbu_s", (128, 2 * NKC), F32)
    bd_s = sb("bd_s", (128, NDC), F32)
    ones_s = sb("ones_s", (128, 64), F32)
    selh_s = sb("selh_s", (128, 512), F32)
    statb = sb("statb", (128, 256), F32)

    def carve(arena, byte_off, shape, dt):
        nbytes = int(np.prod(shape[1:])) * mybir.dt.size(dt)
        assert byte_off % 32 == 0
        v = arena[:, byte_off:byte_off + nbytes].bitcast(dt)
        if len(shape) == 2:
            return v
        if len(shape) == 3:
            return v.rearrange("p (a b) -> p a b", a=shape[1])
        if len(shape) == 4:
            return v.rearrange("p (a b c) -> p a b c", a=shape[1], b=shape[2])
        if len(shape) == 5:
            return v.rearrange("p (a b c d) -> p a b c d", a=shape[1],
                               b=shape[2], c=shape[3])
        raise ValueError(shape)

    h_fm = carve(AR1, 0, (128, NDC, 1024), BF16)
    mask8 = carve(AR1, 0, (128, 512), F32)
    silu_buf = carve(AR1, 0, (128, 4, 1024), BF16)
    simkm_bf = carve(AR1, 8192, (128, 2, NH, 512), BF16)
    simtm_t = carve(AR1, 16384, (128, 512), F32)
    thrRows = simtm_t
    stage = carve(AR1, 18432, (128, 64), F32)
    thrTM = carve(AR1, 18688, (128, 128), F32)
    nvz = carve(AR1, 19200, (128, 2, 4, 4, 4), F32)
    rall = carve(AR1, 19712, (128, 64), F32)
    sxbf = carve(AR1, 19968, (128, 64), BF16)
    candg4 = carve(AR1, 20096, (128, NH, 64), F32)
    exgrid4 = carve(AR1, 21120, (128, NH, 64), F32)
    v8tmp = carve(AR1, 22144, (128, 32), F32)
    lnzb = carve(AR1, 22272, (128, 32), F32)
    thrKM = carve(AR1, 2048, (128, 512), F32)
    stt_out = carve(AR1, 4096, (128, 2048), BF16)

    xattn = carve(AR2, 0, (128, 8, D), BF16)
    q_fm = carve(AR2, 0, (128, NQC, 1024), BF16)

    sgate = carve(AR2, 16384, (128, NIC, 1024), BF16)
    wmask_s = carve(AR2, 16384, (128, W_WARM), F32)
    acc_j = [carve(AR2, j * 8192, (128, NEXP), BF16) for j in range(4)]
    m_fm = carve(AR2, 0, (128, NKC, 512), BF16)
    tmp_stt = carve(AR2, 32768, (128, 2048), BF16)
    tmp2_stt = carve(AR2, 36864, (128, 2048), BF16)
    ffnout = carve(AR2, 32768, (128, NDC, 512), BF16)

    data_u = carve(AR3, 0, (128, NIC, 1040), BF16)
    blockout = carve(AR3, 0, (128, NDC, 1024), BF16)
    h1_fm = carve(AR3, 0, (128, NDC, 1024), BF16)
    out_stage = carve(AR3, 16384, (128, 2, D), F32)

    rem = nc.sbuf_bytes_remaining
    assert rem >= 0, f"SBUF over budget by {-rem} bytes"

    PS = nc.alloc_psum_tensor("PS", [128, 4096], F32).ap()
    PSb = PS.bitcast(BF16)

    def tps(q, alt=0):
        b = 6 - alt
        return PSb[:, b * 1024 + q * 128:b * 1024 + (q + 1) * 128]

    def tps_grp(alt=0):
        b = 6 - alt
        return PSb[:, b * 1024:b * 1024 + 512]

    def warm_pe(waits):
        sch.st("tensor", lambda e: e.matmul(
            PS[:, 4 * 512:4 * 512 + 128], ident_bf, ident_bf,
            start=True, stop=True), waits=waits)

    _stack = contextlib.ExitStack()
    chans = {}
    for name in ("c_x", "c_w", "c_te", "c_ve", "c_se", "c_ge", "c_out",
                 "c_wc", "c_dbg"):
        sem = _stack.enter_context(nc.semaphore(name))
        chans[name] = Chan(sem, name)
    c_x, c_w, c_te, c_ve, c_se, c_ge, c_out, c_wc, c_dbg = (
        chans[k] for k in
        ("c_x", "c_w", "c_te", "c_ve", "c_se", "c_ge", "c_out", "c_wc",
         "c_dbg"))

    sch = Sched()
    wslots = [carve(wring, i * 2048, (128, 1024), BF16) for i in range(6)]
    slot_chans = []
    for i in range(6):
        sem = _stack.enter_context(nc.semaphore(f"c_ws{i}"))
        slot_chans.append(Chan(sem, f"c_ws{i}"))
    ring = Ring(sch, wslots, slot_chans, c_wc)

    dbg = {}

    def dbg_store(name, src_ap, waits=()):
        if name not in dbg_names or name in dbg_written:
            return
        dbg_written.add(name)
        shp = list(src_ap.shape)
        t = nc.dram_tensor("dbg_" + name, shp, src_ap.dtype,
                           kind="ExternalOutput").ap()
        dbg[name] = True
        sch.st("sync", lambda e, t=t, s=src_ap: e.dma_start(out=t, in_=s),
               waits=waits, inc=c_dbg, inc_amt=16)

    def ld(dst, src, slow=False):
        def f(e, d=dst, s=src, slow=slow):
            if slow:
                return e.dma_start(out=d, in_=s, allow_slow_non_contiguous=True)
            return e.dma_start(out=d, in_=s)
        sch.st("sync", f, inc=c_x, inc_amt=16)

    def ldc(dst, src, slow=False):
        def f(e, d=dst, s=src, slow=slow):
            if slow:
                return e.dma_start(out=d, in_=s, allow_slow_non_contiguous=True)
            return e.dma_start(out=d, in_=s)
        sch.st("gpsimd", f, inc=c_w, inc_amt=16)

    ld(conv_s, conv_sb_d)
    ld(wmask_s, warm_mask)
    ld(bgbu_s[:, 0:NKC], bg_d)
    ld(bgbu_s[:, NKC:2 * NKC], bu_d)
    ld(bd_s, bd_d)
    ld(Mtmp[0:S, 0:S], AT_d)
    t_loads = c_x.tok()

    ldc(lnw_s, lnw_rep)
    ldc(lnb_s, lnb_rep)
    ldc(keysbd_s, keysbd_d.rearrange("a p b -> p a b"), slow=True)
    ldc(BmT_s, BmT_d.rearrange("a p b -> p a b"), slow=True)
    ldc(CmT_s[0:S, :], CmT_d)
    t_cast_pre = c_w.tok()

    def g_ident(e):
        e.memset(ident_bf, 0.0)
        e.affine_select(out=ident_bf, in_=ident_bf, compare_op=OP.not_equal,
                        fill=1.0, base=0, pattern=[[-1, 128]],
                        channel_multiplier=1)
        e.memset(ident_f32, 0.0)
        e.affine_select(out=ident_f32, in_=ident_f32, compare_op=OP.not_equal,
                        fill=1.0, base=0, pattern=[[-1, 128]],
                        channel_multiplier=1)
        e.memset(Emat, 0.0)
        ev = Emat.rearrange("p (a b) -> p a b", a=64)
        e.affine_select(out=ev[0:64], in_=ev[0:64], compare_op=OP.not_equal,
                        fill=1.0, base=0, pattern=[[1, 64], [0, 64]],
                        channel_multiplier=-1)
        e.affine_select(out=ev[64:128], in_=ev[64:128], compare_op=OP.not_equal,
                        fill=1.0, base=0, pattern=[[0, 64], [1, 64]],
                        channel_multiplier=-1)
        e.memset(ones_s, 1.0)
        e.memset(selh_s, 0.0)
        sv = selh_s.rearrange("p (a b c) -> p a b c", a=NH, b=2)
        e.affine_select(out=sv[0:8], in_=sv[0:8], compare_op=OP.not_equal,
                        fill=1.0, base=0, pattern=[[-2, NH], [-1, 2], [0, 64]],
                        channel_multiplier=1)
        e.memset(BuE[0:S, :], 0.0)
        e.memset(states[0:S, 0:R_SC], 0.0)
        return e.memset(halo_s.rearrange("p a b -> p (a b)"), 0.0)

    sch.st("gpsimd", g_ident, inc=c_ge)
    t_ident = c_ge.tok()

    sch.st("vector", lambda e: e.tensor_add(
        Mtmp[0:S, S:2 * S], Mtmp[0:S, 0:S], ident_f32[0:S, 0:S]),
        waits=[t_loads, t_ident], inc=c_ve)
    t_m = c_ve.tok()
    sch.st("tensor", lambda e: e.transpose(
        PS[0:S, 0:S], Mtmp[0:S, S:2 * S], ident_f32[0:S, 0:S]),
        waits=[t_m], inc=c_te)
    t_mt = c_te.tok()
    sch.st("vector", lambda e: e.tensor_copy(Mtmp[0:S, 2 * S:3 * S],
                                             PS[0:S, 0:S]),
           waits=[t_mt], inc=c_ve)
    t_mtc = c_ve.tok()
    sch.st("vector", lambda e: e.tensor_copy(Pall[0:S, 0:S],
                                             ident_f32[0:S, 0:S]),
           waits=[t_ident], inc=c_ve)
    sch.st("vector", lambda e: e.tensor_copy(Pall[0:S, S:2 * S],
                                             Mtmp[0:S, S:2 * S]),
           waits=[t_m], inc=c_ve)
    tv = c_ve.tok()
    for k in range(2, R_SC + 1):
        sch.st("tensor", lambda e, k=k: e.matmul(
            PS[0:S, 512:512 + S], Mtmp[0:S, 2 * S:3 * S],
            Pall[0:S, (k - 1) * S:k * S], start=True, stop=True),
            waits=[t_mtc, tv], inc=c_te)
        tk = c_te.tok()
        sch.st("vector", lambda e, k=k: e.tensor_copy(
            Pall[0:S, k * S:(k + 1) * S], PS[0:S, 512:512 + S]),
            waits=[tk], inc=c_ve)
        tv = c_ve.tok()
    sch.st("vector", lambda e: e.tensor_copy(Pall_bf[0:S, :], Pall[0:S, :]),
           waits=[tv], inc=c_ve)
    t_pre = c_ve.tok()
    dbg_store("lnw", lnw_s, waits=[t_cast_pre])
    dbg_store("keysbd", keysbd_s.rearrange("p a b -> p (a b)"),
              waits=[t_cast_pre])
    dbg_store("BmTs", BmT_s, waits=[t_cast_pre])
    dbg_store("Pall", Pall[0:S, :], waits=[t_pre])

    def ln_stats_finish(c0, nt, waits):
        sums = statb[:, c0:c0 + nt]
        sumsq = statb[:, c0 + 8:c0 + 8 + nt]
        mu = statb[:, c0 + 16:c0 + 16 + nt]
        inv = statb[:, c0 + 24:c0 + 24 + nt]
        vs = statb[:, c0 + 32:c0 + 32 + nt]
        yb = statb[:, c0 + 40:c0 + 40 + nt]
        tb = statb[:, c0 + 48:c0 + 48 + nt]

        spc_a = statb[:, 240:248]
        spc_b = statb[:, 248:256]

        def f1(e):
            def spc():
                e.tensor_copy(spc_b, spc_a)
            e.tensor_scalar(mu, sums, 1.0 / D, None, OP.mult)
            e.tensor_scalar(vs, sumsq, 1.0 / D, None, OP.mult)
            e.tensor_tensor(tb, mu, mu, OP.mult)
            spc()
            e.tensor_sub(vs, vs, tb)
            spc()
            return e.tensor_scalar(vs, vs, float(LN_EPS), None, OP.add)

        sch.st("vector", f1, waits=waits, inc=c_ve)
        tv1 = c_ve.tok()

        def fs(e):
            e.activation(yb, vs, AF.Ln)
            e.activation(dump16[:, 0:64], dump16[:, 64:128], AF.Square)
            return e.activation(tb, yb, AF.Exp, scale=-0.5)

        sch.st("scalar", fs, waits=[tv1], inc=c_se)
        ts1 = c_se.tok()

        def f2(e):
            def spc():
                e.tensor_copy(spc_b, spc_a)
            e.tensor_tensor(yb, tb, tb, OP.mult)
            spc()
            e.tensor_tensor(yb, yb, vs, OP.mult)
            spc()
            e.tensor_scalar(yb, yb, -0.5, 1.5, OP.mult, OP.add)
            spc()
            e.tensor_tensor(tb, tb, yb, OP.mult)
            spc()
            e.tensor_copy(inv, tb)
            return e.tensor_copy(spc_b, spc_a)

        return sch.st("vector", f2, waits=[ts1], inc=c_ve)

    def mu_col(c0, j):
        return statb[:, c0 + 16 + j:c0 + 17 + j]

    def inv_col(c0, j):
        return statb[:, c0 + 24 + j:c0 + 25 + j]

    st_base = 0
    xres_waits = [c_out.tok()]

    for st_i, T in enumerate(st_sizes):
        NT = T // 128
        NTH = T // 512
        warm_only = (st_i == 0)
        sfx = f"_st{st_i}"

        st_toks = []
        xs = x_in[st_base:st_base + T].rearrange("(a p) d -> p a d", p=128)
        tok_x = sch.st("sync", lambda e, xs=xs, NT=NT: e.dma_start(
            out=xres[:, 0:NT, :], in_=xs), waits=xres_waits,
            inc=c_x, inc_amt=16)

        c0a = 0
        for j in range(NT):
            sch.st("vector", lambda e, j=j: e.tensor_scalar(
                dump16, xres[:, j, :], 1.0, 0.0, OP.mult, OP.add,
                accum_out=statb[:, c0a + j:c0a + j + 1]),
                waits=[tok_x, t_pre], inc=c_ve)
            sch.st("scalar", lambda e, j=j: e.activation(
                dump16, xres[:, j, :], AF.Square,
                accum_out=statb[:, c0a + 8 + j:c0a + 9 + j]),
                waits=[tok_x], inc=c_se)
        t_stats1 = ln_stats_finish(c0a, NT, [c_ve.tok(), c_se.tok()])
        warm_pe([c_ve.tok()])

        c0b = 64
        t_hfm_tiles = []
        se_tp_hist = []
        for j in range(NT):
            sch.st("vector", lambda e, j=j: e.tensor_scalar(
                xattn[:, j, :], xres[:, j, :], mu_col(c0a, j), inv_col(c0a, j),
                OP.subtract, OP.mult,
                accum_out=statb[:, c0b + j:c0b + j + 1]),
                waits=[t_stats1], inc=c_ve)
            tv_ = c_ve.tok()
            sch.st("scalar", lambda e, j=j: e.activation(
                dump16, xattn[:, j, :], AF.Square,
                accum_out=statb[:, c0b + 8 + j:c0b + 9 + j]),
                waits=[tv_], inc=c_se)
        t_stats2 = ln_stats_finish(c0b, NT, [c_ve.tok(), c_se.tok()])
        warm_pe([c_ve.tok()])
        dbg_store("xattn" + sfx, xattn, waits=[c_ve.tok()])
        dbg_store("statb1" + sfx, statb, waits=[t_stats1])
        dbg_store("statb" + sfx, statb, waits=[t_stats2])

        t_hfm_done = None
        for j in range(NT):
            tslot = trans_t[:, j % 2, :]
            prev_evac = c_se.tok()
            sch.st("vector", lambda e, j=j, tslot=tslot: (
                e.tensor_scalar(tslot, xattn[:, j, :], mu_col(c0b, j),
                                inv_col(c0b, j), OP.subtract, OP.mult),
                e.tensor_tensor(tslot, tslot, lnw_s, OP.mult),
                e.tensor_tensor(tslot, tslot, lnb_s, OP.add))[-1],
                waits=[t_stats2, prev_evac, t_cast_pre], inc=c_ve)
            t_h = c_ve.tok()
            for g in range(2):
                wse = se_tp_hist[-2] if len(se_tp_hist) >= 2 else (c_se, 0)

                def t_tr(e, g=g, tslot=tslot):
                    r = None
                    for q in range(4):
                        dc = g * 4 + q
                        r = e.transpose(
                            tps(q, g % 2), tslot[:, dc * 128:(dc + 1) * 128],
                            ident_bf)
                    return r
                sch.st("tensor", t_tr, waits=[t_h, wse], inc=c_te)
                t_tp = c_te.tok()
                sch.st("scalar", lambda e, j=j, g=g: e.activation(
                    h_fm[:, g * 4:(g + 1) * 4, j * 128:(j + 1) * 128],
                    tps_grp(g % 2).rearrange("p (a b) -> p a b", a=4),
                    AF.Identity), waits=[t_tp], inc=c_se)
                se_tp_hist.append(c_se.tok())
            if j == 0:
                dbg_store("htm" + sfx, trans_t[:, 0, :], waits=[t_h])
            t_hfm_done = c_se.tok()
            t_hfm_tiles.append(t_hfm_done)
            warm_pe([c_ve.tok()])
        dbg_store("h_fm" + sfx, h_fm, waits=[t_hfm_done])

        b_done = []
        t_gate_sig = None
        n_ic = NIC if warm_only else NPROJ
        for ic in range(n_ic):
            pid = ring.load(lambda slot, ic=ic: (
                lambda e: e.dma_start(
                    out=slot, in_=W_in_r[ic].rearrange("p a b -> p (a b)"))))
            wslot = ring.slot(pid)
            ltok = ring.wait_tok(pid)
            pb = (ic % 2) * 2
            wprev = b_done[-2] if len(b_done) >= 2 else (c_ve, 0)

            for th in range(NTH):
                def t_mm(e, wslot=wslot, pb=pb, th=th):
                    r = None
                    for dc in range(NDC):
                        r = e.matmul(
                            PS[:, (pb + th) * 512:(pb + th + 1) * 512],
                            wslot[:, dc * 128:(dc + 1) * 128],
                            h_fm[:, dc, th * 512:(th + 1) * 512],
                            start=(dc == 0), stop=(dc == NDC - 1))
                    return r
                hw = t_hfm_tiles[min(th * 4 + 3, len(t_hfm_tiles) - 1)]
                sch.st("tensor", t_mm, waits=[ltok, hw, wprev], inc=c_wc)
            t_mmd = c_wc.tok()
            ring.consumed(pid, t_mmd)
            if ic < NIC:
                sch.st("vector", lambda e, ic=ic, pb=pb, T=T: e.tensor_copy(
                    data_u[:, ic, 2:2 + T], PS[:, pb * 512:pb * 512 + T]),
                    waits=[t_mmd], inc=c_ve)
                b_done.append(c_ve.tok())
                dtok = c_ve.tok()
                if warm_only:
                    sch.st("vector", lambda e, ic=ic: e.tensor_tensor(
                        data_u[:, ic, 2:2 + W_WARM],
                        data_u[:, ic, 2:2 + W_WARM],
                        wmask_s, OP.mult), waits=[dtok], inc=c_ve)
                    dtok = c_ve.tok()
                w0 = conv_s[:, ic, 0:1]
                w1 = conv_s[:, ic, 1:2]
                w2 = conv_s[:, ic, 2:3]

                def v_conv(e, ic=ic, w0=w0, w1=w1, w2=w2, T=T):
                    dd = data_u[:, ic, :]
                    e.tensor_copy(dd[:, 0:2], halo_s[:, ic, :])
                    e.tensor_copy(statb[:, 248:256], statb[:, 240:248])
                    e.tensor_scalar(conv_tmp[:, 0:T], dd[:, 0:T], w0, None,
                                    OP.mult)
                    e.scalar_tensor_tensor(conv_tmp[:, 0:T], dd[:, 1:1 + T],
                                           w1, conv_tmp[:, 0:T],
                                           OP.mult, OP.add)
                    e.scalar_tensor_tensor(conv_tmp[:, 0:T], dd[:, 2:2 + T],
                                           w2, conv_tmp[:, 0:T],
                                           OP.mult, OP.add)
                    return e.tensor_copy(halo_s[:, ic, :], dd[:, T:T + 2])
                sch.st("vector", v_conv, waits=[dtok], inc=c_ve)
                t_c = c_ve.tok()
                sch.st("scalar", lambda e, T=T: e.activation(
                    sig_tmp[:, 0:T], conv_tmp[:, 0:T], AF.Sigmoid),
                    waits=[t_c], inc=c_se)
                t_s = c_se.tok()
                sch.st("vector", lambda e, ic=ic, T=T: e.tensor_tensor(
                    data_u[:, ic, 2:2 + T], conv_tmp[:, 0:T], sig_tmp[:, 0:T],
                    OP.mult), waits=[t_s], inc=c_ve)
                t_u_last = c_ve.tok()
            else:
                sch.st("scalar", lambda e, ic=ic, pb=pb, T=T: e.activation(
                    sgate[:, ic - NIC, 0:T], PS[:, pb * 512:pb * 512 + T],
                    AF.Sigmoid), waits=[t_mmd], inc=c_se)
                b_done.append(c_se.tok())
                t_gate_sig = c_se.tok()
        dbg_store("u_fm" + sfx, data_u, waits=[t_u_last])
        if stop_phase == "B":
            st_base += T
            xres_waits = [c_out.tok(), c_ve.tok(), c_se.tok()]
            continue

        def t_bu(e, T=T, NTH2=(T + 511) // 512):
            r = None
            for th in range(NTH2):
                n = min(512, T - th * 512)
                for ic in range(NIC):
                    r = e.matmul(PS[0:S, (4 + th) * 512:(4 + th) * 512 + n],
                                 BmT_s[:, ic, :],
                                 data_u[:, ic, 2 + th * 512:2 + th * 512 + n],
                                 start=(ic == 0), stop=(ic == NIC - 1))
            return r
        sch.st("tensor", t_bu, waits=[t_u_last], inc=c_te)
        t_bu_t = c_te.tok()
        sch.st("vector", lambda e, T=T: e.tensor_copy(
            BuE[0:S, 31:31 + T], PS[0:S, 4 * 512:4 * 512 + T]),
            waits=[t_bu_t], inc=c_ve)
        t_bue_t = c_ve.tok()

        def t_taps(e, T=T):
            r = None
            for th in range((T + 511) // 512):
                n = min(512, T - th * 512)
                for k in range(R_SC):
                    r = e.matmul(PS[0:S, (4 + th) * 512:(4 + th) * 512 + n],
                                 Pall_bf[0:S, k * S:(k + 1) * S],
                                 BuE[0:S, 31 - k + th * 512:31 - k + th * 512 + n],
                                 start=(k == 0), stop=(k == R_SC - 1))
            return r
        dbg_store("BuE" + sfx, BuE[0:S, :], waits=[t_bue_t])
        sch.st("tensor", t_taps, waits=[t_bue_t], inc=c_te)
        t_taps_t = c_te.tok()
        sch.st("vector", lambda e, T=T: (
            e.tensor_copy(Bup[0:S, 0:T], PS[0:S, 4 * 512:4 * 512 + T]),
            e.tensor_copy(statb[:, 248:256], statb[:, 240:248]))[-1],
            waits=[t_taps_t], inc=c_ve)
        tv_prev = c_ve.tok()

        dbg_store("Bup" + sfx, Bup[0:S, :], waits=[tv_prev])
        for jj in range(T // R_SC):
            sch.st("tensor", lambda e, jj=jj: e.matmul(
                PS[0:S, 7 * 512:7 * 512 + R_SC],
                Pall[0:S, R_SC * S:(R_SC + 1) * S],
                states[0:S, jj * R_SC:(jj + 1) * R_SC],
                start=True, stop=True), waits=[tv_prev], inc=c_te)
            tt = c_te.tok()
            sch.st("vector", lambda e, jj=jj: e.tensor_add(
                states[0:S, R_SC + jj * R_SC:R_SC + (jj + 1) * R_SC],
                PS[0:S, 7 * 512:7 * 512 + R_SC],
                Bup[0:S, jj * R_SC:(jj + 1) * R_SC]),
                waits=[tt], inc=c_ve)
            tv_prev = c_ve.tok()

        sch.st("vector", lambda e, T=T: (
            e.tensor_copy(states_bf[0:S, R_SC:R_SC + T],
                          states[0:S, R_SC:R_SC + T]),
            e.tensor_copy(states[0:S, 0:R_SC], states[0:S, T:T + R_SC]))[-1],
            waits=[tv_prev], inc=c_ve)
        t_scan = c_ve.tok()
        dbg_store("states" + sfx, states[0:S, :], waits=[t_scan])
        if stop_phase == "scan":
            st_base += T
            xres_waits = [c_out.tok(), c_ve.tok(), c_se.tok()]
            continue

        if not warm_only:
            t_in2 = []
            for ic in range(NIC):
                pb = (ic % 2) * 2
                wprev = t_in2[-2] if len(t_in2) >= 2 else (c_ve, 0)

                def t_y(e, ic=ic, pb=pb, NTH=NTH):
                    r = None
                    for th in range(NTH):
                        r = e.matmul(
                            PS[:, (pb + th) * 512:(pb + th + 1) * 512],
                            CmT_s[0:S, ic * 128:(ic + 1) * 128],
                            states_bf[0:S, R_SC + th * 512:R_SC + (th + 1) * 512],
                            start=True, stop=True)
                    return r
                sch.st("tensor", t_y, waits=[t_scan, t_gate_sig, wprev],
                       inc=c_te)
                t_yt = c_te.tok()
                sch.st("vector", lambda e, ic=ic, pb=pb, T=T: e.tensor_tensor(
                    sgate[:, ic, 0:T], PS[:, pb * 512:pb * 512 + T],
                    sgate[:, ic, 0:T], OP.mult), waits=[t_yt], inc=c_ve)
                t_in2.append(c_ve.tok())

            t_ao = []
            for dc in range(NDC):
                pids = []
                for hp in range(2):
                    pid = ring.load(lambda slot, dc=dc, hp=hp: (
                        lambda e: e.dma_start(
                            out=slot[:, 0:768].rearrange(
                                "p (a b) -> p a b", a=6),
                            in_=W_out_r[dc, hp * 6:(hp + 1) * 6].rearrange(
                                "a p b -> p a b"), allow_slow_non_contiguous=True)))
                    pids.append(pid)
                ltoks = [ring.wait_tok(p) for p in pids]
                pb = (dc % 2) * 2
                wprev = t_ao[-2] if len(t_ao) >= 2 else (c_ve, 0)

                def t_wo(e, pids=pids, pb=pb, NTH=NTH):
                    r = None
                    for th in range(NTH):
                        for k2 in range(NIC):
                            sl = ring.slot(pids[k2 // 6])
                            r = e.matmul(
                                PS[:, (pb + th) * 512:(pb + th + 1) * 512],
                                sl[:, (k2 % 6) * 128:(k2 % 6 + 1) * 128],
                                sgate[:, k2, th * 512:(th + 1) * 512],
                                start=(k2 == 0), stop=(k2 == NIC - 1))
                    return r
                sch.st("tensor", t_wo, waits=ltoks + [t_in2[-1], wprev],
                       inc=c_wc)
                t_wot = c_wc.tok()
                for pid in pids:
                    ring.consumed(pid, t_wot)
                sch.st("scalar", lambda e, dc=dc, pb=pb, T=T: e.activation(
                    blockout[:, dc, 0:T], PS[:, pb * 512:pb * 512 + T],
                    AF.Identity), waits=[t_wot], inc=c_se)
                t_ao.append(c_se.tok())
            dbg_store("attn_out" + sfx, blockout, waits=[t_ao[-1]])
            if stop_phase == "C":
                st_base += T
                xres_waits = [c_out.tok(), c_ve.tok(), c_se.tok()]
                continue

            c0c = 128
            ve_tp_hist = []
            for j in range(NT):
                for g in range(2):
                    wve = ve_tp_hist[-2] if len(ve_tp_hist) >= 2 else (c_ve, 0)

                    def t_tr(e, j=j, g=g):
                        r = None
                        for q in range(4):
                            dc = g * 4 + q
                            r = e.transpose(
                                tps(q, g % 2),
                                blockout[:, dc, j * 128:(j + 1) * 128],
                                ident_bf)
                        return r
                    sch.st("tensor", t_tr, waits=[t_ao[g * 4 + 3], wve],
                           inc=c_te)
                    t_tp = c_te.tok()
                    sch.st("vector", lambda e, j=j, g=g: e.tensor_tensor(
                        xattn[:, j, g * 512:(g + 1) * 512], tps_grp(g % 2),
                        xattn[:, j, g * 512:(g + 1) * 512], OP.add),
                        waits=[t_tp], inc=c_ve)
                    ve_tp_hist.append(c_ve.tok())
                t_in = c_ve.tok()
                sch.st("vector", lambda e, j=j: e.tensor_scalar(
                    dump16, xattn[:, j, :], 1.0, 0.0, OP.mult, OP.add,
                    accum_out=statb[:, c0c + j:c0c + j + 1]),
                    waits=[t_in], inc=c_ve)
                tv_ = c_ve.tok()
                sch.st("scalar", lambda e, j=j: e.activation(
                    dump16, xattn[:, j, :], AF.Square,
                    accum_out=statb[:, c0c + 8 + j:c0c + 9 + j]),
                    waits=[tv_], inc=c_se)
            t_stats3 = ln_stats_finish(c0c, NT, [c_ve.tok(), c_se.tok()])
            warm_pe([c_ve.tok()])

            t_x1 = []
            for j in range(NT):
                sch.st("vector", lambda e, j=j: (
                    e.tensor_scalar(sig_tmp, xattn[:, j, :], mu_col(c0c, j),
                                    inv_col(c0c, j), OP.subtract, OP.mult),
                    e.tensor_tensor(xres[:, j, :], xres[:, j, :], sig_tmp,
                                    OP.add))[-1],
                    waits=[t_stats3], inc=c_ve)
                t_x1.append(c_ve.tok())
            dbg_store("x1" + sfx, xres, waits=[t_x1[-1]])

            c0d = 192
            for j in range(NT):
                sch.st("vector", lambda e, j=j: e.tensor_scalar(
                    dump16, xres[:, j, :], 1.0, 0.0, OP.mult, OP.add,
                    accum_out=statb[:, c0d + j:c0d + j + 1]),
                    waits=[t_x1[j]], inc=c_ve)
                tv_ = c_ve.tok()
                sch.st("scalar", lambda e, j=j: e.activation(
                    dump16, xres[:, j, :], AF.Square,
                    accum_out=statb[:, c0d + 8 + j:c0d + 9 + j]),
                    waits=[tv_], inc=c_se)
            t_stats4 = ln_stats_finish(c0d, NT, [c_ve.tok(), c_se.tok()])
            warm_pe([c_ve.tok()])

            t_h1_tiles = []
            se_tp_hist = []
            t_h1_done = None
            for j in range(NT):
                tslot = trans_t[:, j % 2, :]
                prev_evac = c_se.tok()
                sch.st("vector", lambda e, j=j, tslot=tslot: e.tensor_scalar(
                    tslot, xres[:, j, :], mu_col(c0d, j), inv_col(c0d, j),
                    OP.subtract, OP.mult), waits=[t_stats4, prev_evac],
                    inc=c_ve)
                t_h = c_ve.tok()
                for g in range(2):
                    wse = se_tp_hist[-2] if len(se_tp_hist) >= 2 else (c_se, 0)

                    def t_tr(e, g=g, tslot=tslot):
                        r = None
                        for q in range(4):
                            dc = g * 4 + q
                            r = e.transpose(
                                tps(q, g % 2), tslot[:, dc * 128:(dc + 1) * 128],
                                ident_bf)
                        return r
                    sch.st("tensor", t_tr, waits=[t_h, wse], inc=c_te)
                    t_tp = c_te.tok()
                    sch.st("scalar", lambda e, j=j, g=g: e.activation(
                        h1_fm[:, g * 4:(g + 1) * 4, j * 128:(j + 1) * 128],
                        tps_grp(g % 2).rearrange("p (a b) -> p a b", a=4),
                        AF.Identity), waits=[t_tp], inc=c_se)
                    se_tp_hist.append(c_se.tok())
                t_h1_done = c_se.tok()
                t_h1_tiles.append(t_h1_done)
                warm_pe([c_ve.tok()])
            dbg_store("h1_fm" + sfx, h1_fm, waits=[t_h1_done])

            wq_pids = [ring.load(lambda slot, dp=dp: (
                lambda e: e.dma_start(out=slot, in_=Wq_r[dp])))
                for dp in range(NDC // 2)]
            t_q = []
            for qc in range(NQC):
                pb = (qc % 2) * 2
                wprev = t_q[-2] if len(t_q) >= 2 else (c_ve, 0)

                def t_wq(e, qc=qc, pb=pb, NTH=NTH):
                    r = None
                    for th in range(NTH):
                        for dc in range(NDC):
                            sl = ring.slot(wq_pids[dc // 2])
                            r = e.matmul(
                                PS[:, (pb + th) * 512:(pb + th + 1) * 512],
                                sl[:, (dc % 2) * 512 + qc * 128:
                                   (dc % 2) * 512 + (qc + 1) * 128],
                                h1_fm[:, dc, th * 512:(th + 1) * 512],
                                start=(dc == 0), stop=(dc == NDC - 1))
                    return r
                sch.st("tensor", t_wq,
                       waits=[ring.wait_tok(p) for p in wq_pids] +
                             [t_h1_done, wprev], inc=c_wc)
                t_qt = c_wc.tok()
                sch.st("vector", lambda e, qc=qc, pb=pb, T=T: e.tensor_copy(
                    q_fm[:, qc, 0:T], PS[:, pb * 512:pb * 512 + T]),
                    waits=[t_qt], inc=c_ve)
                t_q.append(c_ve.tok())
            for pid in wq_pids:
                ring.consumed(pid, t_qt)
            dbg_store("q_fm" + sfx, q_fm, waits=[t_q[-1]])

            if with_ffn:
                t_q_done = t_q[-1]
                t_sim_evac = (c_ve, 0)
                for half in range(2):
                    for jj in range(4):
                        if stop_phase == "F1t0" and (half, jj) != (0, 0):
                            continue
                        j = half * 4 + jj

                        def t_sim(e, j=j):
                            r = None
                            for h in range(NH):
                                r = e.matmul(
                                    PS[:, 4 * 512 + h * 128:4 * 512 + (h + 1) * 128],
                                    q_fm[:, h, j * 128:(j + 1) * 128],
                                    keysbd_s[:, h, :],
                                    start=(h == 0), stop=(h == NH - 1))
                            return r
                        sch.st("tensor", t_sim, waits=[t_q_done, t_sim_evac],
                               inc=c_te)
                        t_simt = c_te.tok()

                        def v_stage(e, half=half, jj=jj):
                            e.tensor_copy(simtm_t, PS[:, 4 * 512:5 * 512])
                            for h in range(NH):
                                e.max(out=stage[:, h * 16:h * 16 + 8],
                                      in_=simtm_t[:, h * 128:h * 128 + 64])
                                e.max(out=stage[:, h * 16 + 8:h * 16 + 16],
                                      in_=simtm_t[:, h * 128 + 64:h * 128 + 128])
                            base = half * 32 + jj * 8
                            e.tensor_copy(statb[:, 248:256], statb[:, 240:248])
                            e.tensor_copy(statb[:, 248:256], statb[:, 240:248])
                            e.tensor_copy(sxbf, stage)
                            thv = thrTM[:, base:base + 8].rearrange(
                                "p (a b) -> p a b", a=4)
                            stv = stage.rearrange("p (a b) -> p a b", a=4)
                            e.tensor_copy(thv[:, :, 0:1], stv[:, :, 7:8])
                            e.tensor_copy(thv[:, :, 1:2], stv[:, :, 15:16])
                            return e.tensor_copy(rall, sxbf)
                        sch.st("vector", v_stage, waits=[t_simt], inc=c_ve)
                        t_sim_evac = c_ve.tok()

                        def v_cand(e, half=half, jj=jj):
                            for h in range(NH):
                                for a in range(8):
                                    e.tensor_scalar(
                                        candg4[:, h, a * 8:(a + 1) * 8],
                                        rall[:, h * 16 + 8:h * 16 + 16],
                                        rall[:, h * 16 + a:h * 16 + a + 1],
                                        None, OP.add)
                                e.max(out=v8tmp[:, h * 8:h * 8 + 8],
                                      in_=candg4[:, h, :])
                            e.tensor_copy(statb[:, 248:256], statb[:, 240:248])
                            e.tensor_copy(statb[:, 248:256], statb[:, 240:248])
                            for h in range(NH):
                                e.tensor_copy(nvz[:, half, jj, h, 0:1],
                                              v8tmp[:, h * 8 + 7:h * 8 + 8])
                                e.tensor_scalar(
                                    nvz[:, half, jj, h, 1:2],
                                    v8tmp[:, h * 8:h * 8 + 1],
                                    -1.0, None, OP.mult)
                            return e.tensor_copy(statb[:, 248:256],
                                                 statb[:, 240:248])
                        sch.st("vector", v_cand, waits=[t_sim_evac], inc=c_ve)
                        tvc = c_ve.tok()

                        def s_exp(e, half=half, jj=jj):
                            r = None
                            for h in range(NH):
                                r = e.activation(exgrid4[:, h, :],
                                                 candg4[:, h, :], AF.Exp,
                                                 bias=nvz[:, half, jj, h, 1:2])
                            return r
                        sch.st("scalar", s_exp, waits=[tvc], inc=c_se)
                        tse = c_se.tok()

                        def v_z(e, half=half, jj=jj):
                            for h in range(NH):
                                e.scalar_tensor_tensor(
                                    exgrid4[:, h, :], candg4[:, h, :],
                                    nvz[:, half, jj, h, 0:1],
                                    exgrid4[:, h, :], OP.is_ge, OP.mult)
                            for h in range(NH):
                                e.reduce_sum(nvz[:, half, jj, h, 2:3],
                                             exgrid4[:, h, :],
                                             axis=mybir.AxisListType.X)
                            return e.tensor_copy(statb[:, 248:256],
                                                 statb[:, 240:248])
                        sch.st("vector", v_z, waits=[tse], inc=c_ve)
                        if stop_phase == "F1t0":
                            dbg_store("f1_simtm" + sfx, simtm_t,
                                      waits=[c_ve.tok()])
                            dbg_store("f1_stage" + sfx, stage,
                                      waits=[c_ve.tok()])
                            dbg_store("f1_rall" + sfx, rall,
                                      waits=[c_ve.tok()])
                            dbg_store("f1_candg" + sfx,
                                      candg4.rearrange("p a b -> p (a b)"),
                                      waits=[c_ve.tok()])
                            dbg_store("f1_exg" + sfx,
                                      exgrid4.rearrange("p a b -> p (a b)"),
                                      waits=[c_ve.tok()])
                            dbg_store("f1_nvz" + sfx,
                                      nvz.rearrange("p a b c d -> p (a b c d)"),
                                      waits=[c_ve.tok()])
                            dbg_store("f1_v8t" + sfx, v8tmp,
                                      waits=[c_ve.tok()])

                tvz_all = c_ve.tok()
                sch.st("scalar", lambda e: e.activation(
                    lnzb, nvz[:, :, :, :, 2].rearrange("p a b c -> p (a b c)"),
                    AF.Ln), waits=[tvz_all], inc=c_se)
                tlnz = c_se.tok()
                sch.st("vector", lambda e: e.tensor_sub(
                    nvz[:, :, :, :, 3].rearrange("p a b c -> p (a b c)"),
                    nvz[:, :, :, :, 1].rearrange("p a b c -> p (a b c)"),
                    lnzb), waits=[tlnz], inc=c_ve)
                t_f1_done = c_ve.tok()
                dbg_store("nvz" + sfx, nvz.rearrange("p a b c d -> p (a b c d)"),
                          waits=[t_f1_done])

                t_f0_done = []
                for half in range(2):
                    def t_thr(e, half=half):
                        r = None
                        for jj in range(4):
                            r = e.transpose(
                                PS[0:8, 5 * 512 + jj * 128:5 * 512 + (jj + 1) * 128],
                                thrTM[:, half * 32 + jj * 8:half * 32 + (jj + 1) * 8],
                                ident_f32)
                        return r
                    sch.st("tensor", t_thr, waits=[t_f1_done], inc=c_te)
                    t_thrt = c_te.tok()
                    sch.st("vector", lambda e: e.tensor_copy(
                        thrRows[0:8, :], PS[0:8, 5 * 512:6 * 512]),
                        waits=[t_thrt], inc=c_ve)
                    t_thre = c_ve.tok()

                    for h in range(NH):
                        b0 = (h % 2) * 2
                        b1 = b0 + 1

                        def t_km(e, half=half, h=h, b0=b0, b1=b1):
                            e.matmul(PS[:, b0 * 512:(b0 + 1) * 512],
                                     keysbd_s[:, h, :],
                                     q_fm[:, h, half * 512:(half + 1) * 512],
                                     start=True, stop=True)
                            return e.matmul(
                                PS[:, b1 * 512:(b1 + 1) * 512],
                                selh_s[0:8, h * 128:(h + 1) * 128],
                                thrRows[0:8, :], start=True, stop=True)
                        wprev = t_f0_done[-2] if len(t_f0_done) >= 2 else (c_ve, 0)
                        sch.st("tensor", t_km, waits=[t_thre, wprev], inc=c_te)
                        t_kmt = c_te.tok()

                        def v_km(e, half=half, h=h, b0=b0, b1=b1):
                            e.tensor_copy(thrKM, PS[:, b1 * 512:(b1 + 1) * 512])
                            e.tensor_tensor(mask8, PS[:, b0 * 512:(b0 + 1) * 512],
                                            thrKM, OP.is_ge)
                            e.tensor_scalar(mask8, mask8, 1.0e30, -1.0e30,
                                            OP.mult, OP.add)
                            return e.tensor_tensor(
                                simkm_bf[:, half, h, :], mask8,
                                PS[:, b0 * 512:(b0 + 1) * 512], OP.add)
                        sch.st("vector", v_km, waits=[t_kmt], inc=c_ve)
                        t_f0_done.append(c_ve.tok())
                t_f0_tok = t_f0_done[-1]

                dbg_store("simkm" + sfx,
                          simkm_bf.rearrange("p a b c -> p (a b c)"),
                          waits=[t_f0_tok])

                t_J_evac_prev = (c_se, 0)
                for half in range(2):
                    if stop_phase in ("F1", "F0", "F1t0"):
                        break
                    unit = 0
                    t_stt_done = {}
                    t_mm_done = {}
                    last_add = (c_ve, 0)
                    for jj in range(4):
                        for h in range(NH):
                            for eh2 in range(2):
                                base = (unit % 2) * 2048
                                expd = tmp_stt if unit % 2 == 0 else tmp2_stt
                                wprev = t_stt_done.get(unit - 2, (c_ve, 0))

                                def t_cf(e, half=half, h=h, jj=jj, eh2=eh2,
                                         base=base):
                                    r = None
                                    for es in range(4):
                                        r = e.matmul(
                                            PS[:, base + es * 512:base + (es + 1) * 512],
                                            simkm_bf[:, half, h,
                                                     jj * 128:(jj + 1) * 128],
                                            Emat[:, eh2 * 2048 + es * 512:
                                                 eh2 * 2048 + (es + 1) * 512],
                                            start=True, stop=True)
                                    return r
                                w_cf = [t_f0_tok, wprev]
                                if unit == 0:
                                    w_cf += [c_ve.tok(), c_se.tok(),
                                             c_te.tok(), t_J_evac_prev]
                                sch.st("tensor", t_cf, waits=w_cf, inc=c_te)
                                t_mm_done[unit] = c_te.tok()

                                sch.st("scalar", lambda e, half=half, jj=jj,
                                       h=h, base=base, expd=expd:
                                       e.activation(
                                           expd, PS[:, base:base + 2048],
                                           AF.Exp,
                                           bias=nvz[:, half, jj, h, 3:4]),
                                       waits=[t_mm_done[unit]], inc=c_se)
                                t_exp = c_se.tok()

                                def v_stt(e, half=half, jj=jj, h=h, base=base,
                                          eh2=eh2, expd=expd):
                                    dst = (acc_j[jj][:, eh2 * 2048:(eh2 + 1) * 2048]
                                           if h == 0 else stt_out)
                                    e.scalar_tensor_tensor(
                                        dst, PS[:, base:base + 2048],
                                        nvz[:, half, jj, h, 0:1], expd,
                                        OP.is_ge, OP.mult)
                                    if h > 0:
                                        return e.tensor_add(
                                            acc_j[jj][:, eh2 * 2048:(eh2 + 1) * 2048],
                                            acc_j[jj][:, eh2 * 2048:(eh2 + 1) * 2048],
                                            stt_out)
                                    return e.tensor_copy(statb[:, 248:256],
                                                         statb[:, 240:248])
                                sch.st("vector", v_stt, waits=[t_exp], inc=c_ve)
                                t_stt_done[unit] = c_ve.tok()
                                unit += 1
                    t_f3_done = c_ve.tok()
                    if half == 0:
                        dbg_store("SWacc" + sfx, acc_j[0], waits=[t_f3_done])
                    if stop_phase == "F3":
                        continue

                    t_tr_done = (c_te, 0)
                    t_evV_prev = (c_se, 0)
                    t_vt_prev = (c_ve, 0)
                    for eq in range(4):
                        t_wallt = None
                        for part in range(2):
                            dcs = range(part * 4, part * 4 + 4)
                            pids = [ring.load(
                                lambda slot, eq=eq, dc=dc: (
                                    lambda e: e.dma_start(out=slot,
                                                          in_=DOWNT_r[eq, dc])))
                                for dc in dcs]

                            t_h1_half = t_h1_tiles[half * 4 + 3]

                            def t_wall(e, half=half, pids=pids, part=part):
                                r = None
                                for i, dc in enumerate(
                                        range(part * 4, part * 4 + 4)):
                                    sl = ring.slot(pids[i])
                                    for jj in range(4):
                                        lhs = h1_fm[:, dc,
                                                    half * 512 + jj * 128:
                                                    half * 512 + (jj + 1) * 128]
                                        for eh in range(2):
                                            r = e.matmul(
                                                PS[:, jj * 1024 + eh * 512:
                                                   jj * 1024 + (eh + 1) * 512],
                                                lhs,
                                                sl[:, eh * 512:(eh + 1) * 512],
                                                start=(dc == 0),
                                                stop=(dc == NDC - 1))
                                return r
                            waits = [ring.wait_tok(p) for p in pids]
                            if part == 0:
                                waits += [t_f3_done, t_evV_prev, t_vt_prev,
                                          t_h1_half]
                            sch.st("tensor", t_wall, waits=waits, inc=c_wc)
                            t_wallt = c_wc.tok()
                            for p in pids:
                                ring.consumed(p, t_wallt)

                        t_ev = []
                        for jj in range(4):
                            sch.st("scalar", lambda e, jj=jj: e.activation(
                                silu_buf[:, jj, :],
                                PS[:, jj * 1024:(jj + 1) * 1024], AF.Silu),
                                waits=[t_wallt], inc=c_se)
                            t_ev.append(c_se.tok())
                        t_evV_prev = t_ev[-1]

                        t_vm = []
                        for jj in range(4):
                            sch.st("vector", lambda e, jj=jj, eq=eq:
                                   e.tensor_tensor(
                                       acc_j[jj][:, eq * 1024:(eq + 1) * 1024],
                                       acc_j[jj][:, eq * 1024:(eq + 1) * 1024],
                                       silu_buf[:, jj, :], OP.mult),
                                   waits=[t_ev[jj]], inc=c_ve)
                            t_vm.append(c_ve.tok())

                        for jj in range(4):
                            def t_vt(e, jj=jj, eq=eq):
                                r = None
                                for ec in range(8):
                                    r = e.transpose(
                                        PSb[:, 2 * jj * 1024 + ec * 128:
                                            2 * jj * 1024 + (ec + 1) * 128],
                                        acc_j[jj][:, (eq * 8 + ec) * 128:
                                                  (eq * 8 + ec + 1) * 128],
                                        ident_bf)
                                return r
                            sch.st("tensor", t_vt,
                                   waits=[t_vm[jj], t_ev[jj]], inc=c_te)
                            t_trt = c_te.tok()
                            sch.st("vector", lambda e, jj=jj, eq=eq:
                                   e.tensor_copy(
                                       VT[:, eq * 8:eq * 8 + 8,
                                          jj * 128:(jj + 1) * 128],
                                       PSb[:, 2 * jj * 1024:2 * jj * 1024 + 1024]
                                       .rearrange("p (a b) -> p a b", a=8)),
                                   waits=[t_trt], inc=c_ve)
                        t_vt_prev = c_ve.tok()
                    t_f2_done = c_ve.tok()
                    if half == 0:
                        dbg_store("VT" + sfx,
                                  VT.rearrange("p a b -> p (a b)"),
                                  waits=[t_f2_done])
                    if stop_phase == "F2":
                        continue

                    t_m_toks = []
                    for dcd in range(NKC):
                        pidg = ring.load(lambda slot, dcd=dcd: (
                            lambda e: e.dma_start(
                                out=slot, in_=Wg_r[dcd].rearrange(
                                    "p a b -> p (a b)"))))
                        pidu = ring.load(lambda slot, dcd=dcd: (
                            lambda e: e.dma_start(
                                out=slot, in_=Wu_r[dcd].rearrange(
                                    "p a b -> p (a b)"))))
                        bkA = (dcd % 2) * 2
                        bkB = bkA + 1
                        wpv = (t_m_toks[-2][0] if len(t_m_toks) >= 2
                               else (c_ve, 0))
                        wpv2 = (t_m_toks[-2][1] if len(t_m_toks) >= 2
                                else (c_se, 0))

                        def t_gu(e, half=half, pidg=pidg, pidu=pidu,
                                 bkA=bkA, bkB=bkB):
                            r = None
                            for pid, bk in ((pidg, bkA), (pidu, bkB)):
                                sl = ring.slot(pid)
                                for dc in range(NDC):
                                    r = e.matmul(
                                        PS[:, bk * 512:(bk + 1) * 512],
                                        sl[:, dc * 128:(dc + 1) * 128],
                                        h1_fm[:, dc,
                                              half * 512:(half + 1) * 512],
                                        start=(dc == 0), stop=(dc == NDC - 1))
                            return r
                        sch.st("tensor", t_gu,
                               waits=[ring.wait_tok(pidg), ring.wait_tok(pidu),
                                      t_f2_done, wpv, wpv2,
                                      t_h1_tiles[half * 4 + 3]], inc=c_wc)
                        t_gut = c_wc.tok()
                        ring.consumed(pidg, t_gut)
                        ring.consumed(pidu, t_gut)

                        sh = (dcd % 2) * 512
                        sch.st("scalar", lambda e, dcd=dcd, bkA=bkA, sh=sh:
                               e.activation(sig_tmp[:, sh:sh + 512],
                                            PS[:, bkA * 512:(bkA + 1) * 512],
                                            AF.Silu,
                                            bias=bgbu_s[:, dcd:dcd + 1]),
                               waits=[t_gut], inc=c_se)
                        t_sg = c_se.tok()

                        def v_m(e, dcd=dcd, bkB=bkB, sh=sh):
                            e.tensor_scalar(
                                dump16[:, sh:sh + 512],
                                PS[:, bkB * 512:(bkB + 1) * 512],
                                bgbu_s[:, NKC + dcd:NKC + dcd + 1], None,
                                OP.add)
                            return e.tensor_tensor(
                                m_fm[:, dcd, :], sig_tmp[:, sh:sh + 512],
                                dump16[:, sh:sh + 512], OP.mult)
                        sch.st("vector", v_m, waits=[t_sg], inc=c_ve)
                        t_m_toks.append((c_ve.tok(), t_sg))
                    t_h_done = c_ve.tok()
                    if half == 0:
                        dbg_store("m" + sfx,
                                  m_fm.rearrange("p a b -> p (a b)"),
                                  waits=[t_h_done])
                    if stop_phase == "H":
                        continue

                    t_i_mm = None
                    for k in range(64):
                        if k < NKC:
                            srcd = Wd_r[k]
                        else:
                            srcd = UE_r[k - NKC]
                        pid = ring.load(lambda slot, srcd=srcd: (
                            lambda e: e.dma_start(
                                out=slot,
                                in_=srcd.rearrange("p a b -> p (a b)"))))

                        def t_i(e, k=k, pid=pid, half=half):
                            sl = ring.slot(pid)
                            rhs = (m_fm[:, k, :] if k < NKC
                                   else VT[:, k - NKC, :])
                            r = None
                            for ds in range(NDC):
                                r = e.matmul(
                                    PS[:, ds * 512:(ds + 1) * 512],
                                    sl[:, ds * 128:(ds + 1) * 128], rhs,
                                    start=(k == 0), stop=(k == 63))
                            return r
                        waits = [ring.wait_tok(pid)]
                        if k == 0:
                            waits += [t_h_done, t_f2_done, t_J_evac_prev]
                        sch.st("tensor", t_i, waits=waits, inc=c_wc)
                        t_i_mm = c_wc.tok()
                        ring.consumed(pid, t_i_mm)

                    t_ffn_ev = []
                    for ds in range(NDC):
                        sch.st("scalar", lambda e, ds=ds: e.activation(
                            ffnout[:, ds, :], PS[:, ds * 512:(ds + 1) * 512],
                            AF.Identity, bias=bd_s[:, ds:ds + 1]),
                            waits=[t_i_mm], inc=c_se)
                        t_ffn_ev.append(c_se.tok())
                    t_J_evac_prev = t_ffn_ev[-1]
                    if half == 0:
                        dbg_store("ffnout" + sfx,
                                  ffnout.rearrange("p a b -> p (a b)"),
                                  waits=[t_J_evac_prev])
                    if stop_phase == "I":
                        continue

                    c0J = 0
                    ve_tpJ_hist = []
                    for pair in range(2):
                        tvJ = None
                        tsJ = None
                        for pj in range(2):
                            jj = pair * 2 + pj
                            for g in range(2):
                                wve = (ve_tpJ_hist[-2]
                                       if len(ve_tpJ_hist) >= 2 else (c_ve, 0))

                                def t_trJ(e, jj=jj, g=g):
                                    r = None
                                    for q in range(4):
                                        dc = g * 4 + q
                                        r = e.transpose(
                                            tps(q, g % 2),
                                            ffnout[:, dc, jj * 128:(jj + 1) * 128],
                                            ident_bf)
                                    return r
                                sch.st("tensor", t_trJ,
                                       waits=[t_ffn_ev[-1], wve,
                                              c_se.tok()], inc=c_te)
                                t_tpJ = c_te.tok()

                                def v_evJ(e, pj=pj, g=g):
                                    return e.tensor_scalar(
                                        trans_t[:, pj, g * 512:(g + 1) * 512],
                                        tps_grp(g % 2), 1.0, 0.0, OP.mult,
                                        OP.add,
                                        accum_out=statb[:, c0J + 48 + g * 8 + pj:
                                                        c0J + 49 + g * 8 + pj])
                                sch.st("vector", v_evJ, waits=[t_tpJ], inc=c_ve)
                                ve_tpJ_hist.append(c_ve.tok())
                            tvJ_t = c_ve.tok()
                            sch.st("vector", lambda e, pj=pj: e.tensor_add(
                                statb[:, c0J + pj:c0J + pj + 1],
                                statb[:, c0J + 48 + pj:c0J + 49 + pj],
                                statb[:, c0J + 56 + pj:c0J + 57 + pj]),
                                waits=[tvJ_t], inc=c_ve)
                            tvJ = c_ve.tok()
                            sch.st("scalar", lambda e, pj=pj: e.activation(
                                dump16, trans_t[:, pj, :], AF.Square,
                                accum_out=statb[:, c0J + 8 + pj:c0J + 9 + pj]),
                                waits=[tvJ], inc=c_se)
                            tsJ = c_se.tok()
                        t_stJ = ln_stats_finish(c0J, 2, [tvJ, tsJ])

                        for pj in range(2):
                            jj = pair * 2 + pj
                            j = half * 4 + jj
                            wst = (st_toks[-2] if len(st_toks) >= 2
                                   else (c_out, 0))

                            def v_x2(e, pj=pj, j=j):
                                e.tensor_scalar(
                                    trans_t[:, pj, :], trans_t[:, pj, :],
                                    mu_col(c0J, pj), inv_col(c0J, pj),
                                    OP.subtract, OP.mult)
                                return e.tensor_tensor(
                                    out_stage[:, pj, :], xres[:, j, :],
                                    trans_t[:, pj, :], OP.add)
                            sch.st("vector", v_x2, waits=[t_stJ, wst], inc=c_ve)
                            t_x2 = c_ve.tok()
                            row0 = (st_base - W_WARM) + j * 128
                            tok = sch.st("sync", lambda e, pj=pj, row0=row0:
                                         e.dma_start(
                                             out=y_out[row0:row0 + 128, :],
                                             in_=out_stage[:, pj, :]),
                                         waits=[t_x2], inc=c_out, inc_amt=16)
                            st_toks.append(tok)
            else:
                for j in range(NT):
                    sch.st("sync", lambda e, j=j: e.dma_start(
                        out=y_out[(st_base - W_WARM) + j * 128:
                                  (st_base - W_WARM) + (j + 1) * 128, :],
                        in_=xres[:, j, :]), waits=[t_x1[-1]],
                        inc=c_out, inc_amt=16)

        st_base += T
        xres_waits = [c_out.tok(), c_ve.tok(), c_se.tok()]

    sch.st("sync", None, waits=[c_out.tok(), c_dbg.tok()])

    with nc.Block() as block:
        @block.tensor
        def _(e):
            sch.replay("tensor", e)

        @block.vector
        def _(e):
            sch.replay("vector", e)

        @block.scalar
        def _(e):
            sch.replay("scalar", e)

        @block.gpsimd
        def _(e):
            sch.replay("gpsimd", e)

        @block.sync
        def _(e):
            sch.replay("sync", e)

    return nc, sorted(dbg_written)


def emit_ffn_st(ctx):
    raise NotImplementedError


def kernel(**inputs):
    nc, _ = build_nc()
    in_maps = [_stage_core_inputs(inputs, c) for c in range(NCORES)]
    res = run_bass_kernel_spmd(nc, in_maps, core_ids=list(range(NCORES)))
    out = np.empty((BATCH, SEQ, D), np.float32)
    for c in range(NCORES):
        b, half = c // 2, c % 2
        out[b, half * T_CORE:(half + 1) * T_CORE] = res.results[c]["y_out"]
    return out
